# version 05
# baseline (speedup 1.0000x reference)
import math
import numpy as np
import concourse.bass as bass
import concourse.mybir as mybir
from concourse.bass_utils import run_bass_kernel_spmd

F32 = mybir.dt.float32
BF16 = mybir.dt.bfloat16
I8 = mybir.dt.int8
I32 = mybir.dt.int32
AF = mybir.ActivationFunctionType
ALU = mybir.AluOpType

T = 2048
D = 2048
DC = 16
H = 16
DH = 128
NB = 4
SPAN = 512
IC = 64
SCL = 1.0 / math.sqrt(DH)
PI = math.pi
NEG = -1.0e30
SBUF_LIMIT = 212863
S = SPAN

ENGS = ("pe", "act", "dve", "pool", "sp")

ALIAS = {
    "big0": ["fuse"], "fuse": ["big0"],
    "slot0": [],
    "slot1E": ["wgu"], "slot1P": ["wgu"], "slot1PT": ["wgu"],
    "wgu": ["slot1E", "slot1P", "slot1PT"],
    "qh": ["wdn"], "kh": ["wdn"], "vt": ["wdn"], "vsb": ["wdn"],
    "qt1": ["wdn"], "qt2": ["wdn"], "osb": ["wdn"],
    "wdn": ["qh", "kh", "vt", "vsb", "qt1", "qt2", "osb"],
    "x_blk": ["rowi"], "rowi": ["x_blk"],
    "wr00": ["wro0"], "wro0": ["wr00"],
    "wr01": ["wro1"], "wro1": ["wr01"],
}


class Prog:
    def __init__(self):
        self.nc = bass.Bass("TRN2", target_bir_lowering=False)
        self.ops = {e: [] for e in ENGS}
        self.sems = {}
        self.cnt = {}
        self.waited = {e: {} for e in ENGS}
        self.rsrc = {}

    def _sem(self, name):
        if name not in self.sems:
            self.sems[name] = self.nc.alloc_semaphore(name=name)
            self.cnt[name] = 0
        return self.sems[name]

    def emit(self, eng, fn, waits=(), inc=None, by=1):
        real = []
        for wtok in waits:
            if wtok is None:
                continue
            n, v = wtok
            self._sem(n)
            if n == eng and eng == "pe":
                continue
            if self.waited[eng].get(n, -1) >= v:
                continue
            self.waited[eng][n] = v
            real.append((n, v))
        tok = None
        if inc is not None:
            self._sem(inc)
            self.cnt[inc] += by
            tok = (inc, self.cnt[inc])
        sems = self.sems

        def run(e, fn=fn, real=tuple(real), inc=inc, by=by):
            for n, v in real:
                e.wait_ge(sems[n], v)
            ins = fn(e)
            if inc is not None:
                ins.then_inc(sems[inc], by)
        self.ops[eng].append(run)
        return tok

    def mm(self, out, lhsT, rhs, start, stop, waits=(), inc=None):
        return self.emit(
            "pe", lambda t, o=out, a=lhsT, b=rhs, s1=start, s2=stop:
            t.matmul(o, a, b, start=s1, stop=s2), waits, inc)

    def tr(self, out, in_, ident, waits=(), inc=None):
        return self.emit(
            "pe", lambda t, o=out, i=in_, d=ident: t.transpose(o, i, d),
            waits, inc)

    def dma(self, q, sem, out, in_, waits=()):
        return self.emit(
            q, lambda e, o=out, i=in_: e.dma_start(out=o, in_=i),
            waits, sem, by=16)

    def _grp(self, n):
        return [n] + ALIAS.get(n, [])

    def wr_waits(self, n):
        out = []
        for n2 in self._grp(n):
            r = self.rsrc.get(n2)
            if r:
                out += r[0] + r[1]
                self.rsrc[n2] = ([], [])
        return out

    def wr_done(self, n, tok, append=False):
        r = self.rsrc.setdefault(n, ([], []))
        if append:
            r[0].append(tok)
        else:
            self.rsrc[n] = ([tok], [])

    def rd_waits(self, n):
        return list(self.rsrc.get(n, ([], []))[0])

    def rd_done(self, n, tok):
        self.rsrc.setdefault(n, ([], []))[1].append(tok)

    def tap(self, name, ap, waits):
        shape = [int(c) for _, c in ap.ap]
        dt_ = ap.tensor.dtype
        d = self.nc.dram_tensor(f"dbg_{name}", shape, dt_,
                                kind="ExternalOutput").ap()
        return self.dma("sp", "xd", d, ap, waits=waits)

    def finalize(self, final_waits):
        sems = self.sems
        for wtok in final_waits:
            if wtok is None:
                continue
            n, v = wtok
            self._sem(n)
            if self.waited["sp"].get(n, -1) >= v:
                continue
            self.waited["sp"][n] = v

            def run(e, n=n, v=v):
                e.wait_ge(sems[n], v)
            self.ops["sp"].append(run)
        nc = self.nc
        with nc.Block() as block:
            @block.tensor
            def _(t):
                for f in self.ops["pe"]:
                    f(t)

            @block.scalar
            def _(sc):
                for f in self.ops["act"]:
                    f(sc)

            @block.vector
            def _(v):
                for f in self.ops["dve"]:
                    f(v)

            @block.gpsimd
            def _(g):
                for f in self.ops["pool"]:
                    f(g)

            @block.sync
            def _(s):
                for f in self.ops["sp"]:
                    f(s)
        return nc


class Bufs:
    pass


def alloc_sbuf(nc):
    B = Bufs()
    base = (nc.sbuf_base + 63) // 64 * 64
    cur = [base]

    def at(name, shape, dtype, offset=None):
        bpe = {F32: 4, BF16: 2, I8: 1, I32: 4}[dtype]
        nbytes = int(np.prod(shape[1:])) * bpe
        if offset is None:
            offset = cur[0]
        cur[0] = max(cur[0], offset + ((nbytes + 63) // 64) * 64)
        h = nc.alloc_sbuf_tensor_at(name, shape, dtype, offset=offset)
        setattr(B, name, h.ap())
        return offset

    at("ones_col", [128, 1], BF16)
    at("ones_row", [1, 128], BF16)
    at("ident", [128, 128], BF16)
    at("identf", [128, 128], F32)
    at("pmat", [128, 128], BF16)
    at("m_tril_neg", [128, 128], F32)
    at("m_triu_neg", [128, 128], F32)
    at("m_tril01", [128, 128], F32)
    at("sign_col", [128, 1], F32)
    at("cful", [128, T], BF16)
    at("sful", [128, T], BF16)
    at("row_f", [1, 6 * S], F32)
    at("row_bf", [1, 2 * S], BF16)
    at("col_f", [128, 32], F32)
    at("ccol", [128, 8], F32)
    at("sqm", [128, S], BF16)
    at("bc2", [128, S], BF16)
    at("sgc", [128, IC], F32)
    at("suc", [128, IC], F32)
    at("sdc", [128, DC], F32)
    at("nrmw", [128, 2], F32)

    xblk = at("x_blk", [128, DC * S], F32)
    at("rowi", [1, T], I32, xblk)
    at("tabf", [128, S], F32, xblk + 16384)
    at("ifr_row", [1, 128], F32, xblk + 20480)
    at("tabi", [128, S], I32, xblk + 24576)
    at("tabg", [128, S], F32, xblk + 26624)
    big0 = at("x_lnT", [128, DC * T], BF16)
    at("fuse", [128, IC * S], BF16, big0)
    slot0 = at("osl", [128, DC * S], BF16)
    at("x_lnB", [128, DC * S], BF16, slot0)
    at("stage", [128, DC * S], BF16, slot0)
    slot1 = at("E", [128, T], F32)
    at("P", [128, T], BF16)
    at("PT", [128, T], BF16)
    at("wgu", [128, 4 * DC * 128], BF16, slot1)
    slot2 = at("qh", [128, T], BF16)
    at("kh", [128, T], BF16)
    at("vt", [128, T], BF16)
    at("vsb", [128, T], BF16)
    at("qt1", [128, T], BF16)
    at("qt2", [128, T], BF16)
    at("osb", [128, T], BF16)
    at("wdn", [128, 2 * IC * 128], BF16, slot2)
    at("wring", [128, 5 * DC * 128], BF16)
    assert cur[0] <= nc.sbuf_top, f"SBUF over budget: {cur[0]} > {nc.sbuf_top}"
    B.end = cur[0]
    return B


def alloc_psum(nc):
    PS = Bufs()
    for i in range(5):
        setattr(PS, f"P{i}", nc.alloc_psum_tensor(f"P{i}", [128, S], F32).ap())
    PS.pstat = nc.alloc_psum_tensor("pstat", [1, S], F32).ap()
    PS.ptrA = nc.alloc_psum_tensor("ptrA", [128, S], BF16).ap()
    PS.ptrB = nc.alloc_psum_tensor("ptrB", [128, S], BF16).ap()
    return PS


def emit_prelude(K, B, PS, need_rope):
    K.rope_tok = None
    tk = None

    def pool(fn):
        nonlocal tk
        tk = K.emit("pool", fn, inc="pool")
        return tk

    pool(lambda g: g.memset(B.ones_col[:, :], 1.0))
    pool(lambda g: g.memset(B.ones_row[:, :], 1.0))
    pool(lambda g: g.memset(B.ident[:, :], 1.0))
    pool(lambda g: g.affine_select(
        B.ident[:, :], B.ident[:, :], [[1, 128]], ALU.is_equal, 0.0,
        base=0, channel_multiplier=-1))
    pool(lambda g: g.memset(B.identf[:, :], 1.0))
    pool(lambda g: g.affine_select(
        B.identf[:, :], B.identf[:, :], [[1, 128]], ALU.is_equal, 0.0,
        base=0, channel_multiplier=-1))
    pool(lambda g: g.memset(B.pmat[:, :], 1.0))
    pool(lambda g: g.affine_select(
        B.pmat[:, 0:64], B.pmat[:, 0:64], [[1, 64]], ALU.is_equal, 0.0,
        base=64, channel_multiplier=-1))
    pool(lambda g: g.affine_select(
        B.pmat[:, 64:128], B.pmat[:, 64:128], [[1, 64]], ALU.is_equal, 0.0,
        base=0, channel_multiplier=-1))
    pool(lambda g: g.memset(B.m_tril_neg[:, :], 0.0))
    pool(lambda g: g.affine_select(
        B.m_tril_neg[:, :], B.m_tril_neg[:, :], [[-1, 128]], ALU.is_ge, NEG,
        base=0, channel_multiplier=1))
    pool(lambda g: g.memset(B.m_triu_neg[:, :], 0.0))
    pool(lambda g: g.affine_select(
        B.m_triu_neg[:, :], B.m_triu_neg[:, :], [[1, 128]], ALU.is_ge, NEG,
        base=-1, channel_multiplier=-1))
    pool(lambda g: g.memset(B.m_tril01[:, :], 1.0))
    pool(lambda g: g.affine_select(
        B.m_tril01[:, :], B.m_tril01[:, :], [[-1, 128]], ALU.is_ge, 0.0,
        base=0, channel_multiplier=1))
    pool(lambda g: g.memset(B.ccol[:, 0:1], SCL))
    pool(lambda g: g.memset(B.ccol[:, 1:2], -PI))
    pool(lambda g: g.memset(B.ccol[:, 2:3], -2.0 * math.log(10000.0) / DH))
    pool(lambda g: g.memset(B.ccol[:, 3:4], 1e-5))
    pool(lambda g: g.memset(B.ccol[:, 4:5], 1.0 / DH))
    pool(lambda g: g.memset(B.ccol[:, 5:6], 1e-6))
    pool(lambda g: g.memset(B.ccol[:, 6:7], 2.0 * PI))
    pool(lambda g: g.memset(B.sign_col[:, :], 1.0))
    tsign = pool(lambda g: g.memset(B.sign_col[0:64, :], -1.0))
    K.consts_tok = tsign

    if need_rope:
        tp = pool(lambda g: g.iota(B.rowi[:, :], [[1, T]], base=0,
                                   channel_multiplier=0))
        tpos = K.emit("dve", lambda v: v.tensor_copy(
            B.row_f[:, 0:T], B.rowi[:, :]), waits=[tp], inc="dve")
        K.rd_done("rowi", tpos)
        tpc = pool(lambda g: g.iota(B.tabi[:, 0:1], [[0, 1]], base=0,
                                    channel_multiplier=1))
        tc1 = K.emit("dve", lambda v: v.tensor_copy(
            B.ccol[:, 7:8], B.tabi[:, 0:1]), waits=[tpc, K.consts_tok],
            inc="dve")
        tc2 = K.emit("dve", lambda v: v.tensor_single_scalar(
            B.ccol[64:128, 7:8], B.ccol[64:128, 7:8], -64.0, ALU.add),
            waits=[tc1], inc="dve")
        tc3 = K.emit("act", lambda sc: sc.activation(
            B.ccol[:, 7:8], B.ccol[:, 7:8], AF.Exp,
            bias=0.0, scale=B.ccol[:, 2:3]), waits=[tc2], inc="act")
        inv2pi = 1.0 / (2.0 * PI)
        prev_td = None
        for s in range(NB):
            sl = slice(s * S, (s + 1) * S)
            tpi = K.emit("pool", lambda g, s=s: g.iota(
                B.tabi[:, :], [[1, S]], base=s * S, channel_multiplier=0),
                waits=[tc1, prev_td], inc="pool")
            tg1 = K.emit("dve", lambda v: v.tensor_copy(
                B.tabg[:, :], B.tabi[:, :]), waits=[tpi], inc="dve")
            tang = K.emit("dve", lambda v: v.tensor_single_scalar(
                B.tabg[:, :], B.tabg[:, :], B.ccol[:, 7:8], ALU.mult),
                waits=[tg1, tc3], inc="dve")

            def redsin(dstap, shift, wprev):
                t1 = K.emit("dve", lambda v, shift=shift: v.tensor_scalar(
                    B.tabf[:, :], B.tabg[:, :], shift, inv2pi,
                    ALU.add, ALU.mult), waits=[tang, wprev], inc="dve")
                t2 = K.emit("dve", lambda v: v.tensor_copy(
                    B.tabi[:, :], B.tabf[:, :]), waits=[t1], inc="dve")
                t3 = K.emit("dve", lambda v: v.tensor_tensor(
                    B.tabf[:, :], B.tabf[:, :], B.tabi[:, :], ALU.subtract),
                    waits=[t2], inc="dve")
                t4 = K.emit("act", lambda sc, dstap=dstap: sc.activation(
                    dstap, B.tabf[:, :], AF.Sin, bias=0.0,
                    scale=B.ccol[:, 6:7]), waits=[t3], inc="act")
                return t4
            tb = redsin(B.sful[:, sl], 0.0, None)
            td = redsin(B.cful[:, sl], 0.5 * PI, tb)
            prev_td = td
            tsg_ = K.emit("dve", lambda v, sl=sl: v.tensor_single_scalar(
                B.sful[:, sl], B.sful[:, sl], B.sign_col[:, :], ALU.mult),
                waits=[tb, tsign], inc="dve")
            K.rope_tok = td
            K.rd_done("x_blk", td)
            K.rd_done("x_blk", tsg_)


def emit_ln_block(K, B, PS, tok_x, dst_slices, dst_res):
    w0 = K.wr_waits("slot0")
    t_cp = K.emit("act", lambda sc: sc.activation(
        B.stage[:, :], B.x_blk[:, :], AF.Copy, bias=0.0, scale=1.0),
        waits=[tok_x] + w0, inc="act")
    K.rd_done("x_blk", t_cp)
    mt = None
    wps = K.wr_waits("pstat")
    for c in range(DC):
        mt = K.mm(PS.pstat[:, :], B.ones_col[:, :],
                  B.stage[:, c * S:(c + 1) * S],
                  start=(c == 0), stop=(c == DC - 1),
                  waits=([t_cp, K.consts_tok] + wps) if c == 0 else [t_cp],
                  inc="pe" if c == DC - 1 else None)
    tB = K.emit("dve", lambda v: v.tensor_copy(
        B.row_f[:, S:2 * S], PS.pstat[:, :]), waits=[mt], inc="dve")
    t_sq1 = K.emit("act", lambda sc: sc.activation(
        B.stage[:, :], B.x_blk[:, :], AF.Square, bias=0.0, scale=1.0),
        waits=[mt], inc="act")
    K.rd_done("x_blk", t_sq1)
    for c in range(DC):
        mt = K.mm(PS.pstat[:, :], B.ones_col[:, :],
                  B.stage[:, c * S:(c + 1) * S],
                  start=(c == 0), stop=(c == DC - 1),
                  waits=[t_sq1, tB], inc="pe" if c == DC - 1 else None)
    K.wr_done("slot0", mt)
    tA = K.emit("dve", lambda v: v.tensor_copy(B.row_f[:, 0:S], PS.pstat[:, :]),
                waits=[mt], inc="dve")
    K.wr_done("pstat", tA)
    t2 = K.emit("dve", lambda v: v.tensor_single_scalar(
        B.row_f[:, 2 * S:3 * S], B.row_f[:, S:2 * S], 1.0 / D, ALU.mult),
        waits=[tB], inc="dve")
    t3 = K.emit("dve", lambda v: v.tensor_tensor(
        B.row_f[:, 3 * S:4 * S], B.row_f[:, 2 * S:3 * S],
        B.row_f[:, 2 * S:3 * S], ALU.mult), waits=[t2], inc="dve")
    t4 = K.emit("dve", lambda v: v.scalar_tensor_tensor(
        B.row_f[:, 3 * S:4 * S], B.row_f[:, 0:S], 1.0 / D,
        B.row_f[:, 3 * S:4 * S], ALU.mult, ALU.subtract), waits=[t3, tA], inc="dve")
    t5 = K.emit("act", lambda sc: sc.activation(
        B.row_f[:, 0:S], B.row_f[:, 3 * S:4 * S], AF.Sqrt,
        bias=B.ccol[0:1, 3:4], scale=1.0), waits=[t4], inc="act")
    t6 = K.emit("dve", lambda v: v.reciprocal(B.row_f[:, 0:S], B.row_f[:, 0:S]),
                waits=[t5], inc="dve")
    t7 = K.emit("dve", lambda v: v.tensor_copy(B.row_bf[:, 0:S],
                                               B.row_f[:, 0:S]),
                waits=[t6], inc="dve")
    t8 = K.emit("dve", lambda v: v.tensor_tensor(
        B.row_f[:, S:2 * S], B.row_f[:, 2 * S:3 * S], B.row_f[:, 0:S],
        ALU.mult), waits=[t6], inc="dve")
    t9 = K.emit("dve", lambda v: v.tensor_single_scalar(
        B.row_bf[:, S:2 * S], B.row_f[:, S:2 * S], -1.0, ALU.mult),
        waits=[t8], inc="dve")
    tb1 = K.mm(PS.P3[:, :], B.ones_row[:, :], B.row_bf[:, 0:S], True, True,
               waits=[t7] + K.wr_waits("P3"), inc="pe")
    tb2 = K.mm(PS.P4[:, :], B.ones_row[:, :], B.row_bf[:, S:2 * S], True, True,
               waits=[t9] + K.wr_waits("P4"), inc="pe")
    out = []
    wdst = K.wr_waits(dst_res) if dst_res else []
    for c in range(DC):
        dsl = dst_slices[c]
        ta = K.emit("dve", lambda v, c=c, dsl=dsl: v.tensor_tensor(
            dsl, B.x_blk[:, c * S:(c + 1) * S], PS.P3[:, :], ALU.mult),
            waits=[tb1] + (wdst if c == 0 else []), inc="dve")
        tbb = K.emit("dve", lambda v, dsl=dsl: v.tensor_tensor(
            dsl, dsl, PS.P4[:, :], ALU.add), waits=[tb2, ta], inc="dve")
        K.rd_done("x_blk", tbb)
        out.append(tbb)
    K.wr_done("P3", out[-1])
    K.wr_done("P4", out[-1])
    return out


def emit_attention(K, B, PS, kind, Wap, xsrc, xdst):
    nc = K.nc
    windowed = kind in ("a0a", "a0b")
    use_rope = kind in ("a0a", "a0b")
    use_bound = kind in ("a0a", "a0b", "a1b")
    ost = Wap["ost"]

    tnrm = None
    if kind in ("a0b", "a1a"):
        tnrm = K.dma("sp", "xd", B.nrmw[:, :], Wap["nrm"],
                     waits=K.wr_waits("nrmw"))

    ln_toks = [[None] * DC for _ in range(NB)]
    wbig = K.wr_waits("big0")
    for b in range(NB):
        tok_x = K.dma("sp", "xd", B.x_blk[:, :], xsrc[b],
                      waits=K.wr_waits("x_blk") + (wbig if b == 0 else []))
        if getattr(K, "dbg", False) and b == 3 and not getattr(K, "_tapped", 0):
            K._tapped = 1
            K.tap("xblk3", B.x_blk[:, :], [tok_x])
        dst = [B.x_lnT[:, c * T + b * S: c * T + b * S + S] for c in range(DC)]
        toks = emit_ln_block(K, B, PS, tok_x, dst, None)
        ln_toks[b] = toks
        for tk_ in toks:
            K.wr_done("big0", tk_, append=True)

    if getattr(K, "ln_only", False):
        tst = K.dma("sp", "xd", xdst[3], B.x_blk[:, :],
                    waits=ln_toks[3])
        K.rd_done("x_blk", tst)
        return tst

    otok_prev = [None] * H
    for h in range(H):
        stat_toks = []

        def wslot(idx, par):
            o = (idx * 2 + par) * DC * 128 if idx < 2 else 4 * DC * 128
            return B.wring[:, o:o + DC * 128]
        par = h % 2
        res_q, res_k, res_v = f"wr0{par}", f"wr1{par}", "wr20"
        tkw = K.dma("pool", "wd", wslot(1, par), Wap["wk"][h],
                    waits=K.wr_waits(res_k))
        tqw = K.dma("pool", "wd", wslot(0, par), Wap["wq"][h],
                    waits=K.wr_waits(res_q))
        tvw = K.dma("pool", "wd", wslot(2, par), Wap["wv"][h],
                    waits=K.wr_waits(res_v))

        def proj(mat, wtok, wap, res_w, dst, dres):
            wdst = K.wr_waits(dres)
            for s in range(NB):
                Pa = PS.P0 if s % 2 == 0 else PS.P1
                bank = "P0" if s % 2 == 0 else "P1"
                wb = K.wr_waits(bank)
                mt = None
                for c in range(DC):
                    mt = K.mm(Pa[:, :], wap[:, c * 128:(c + 1) * 128],
                              B.x_lnT[:, c * T + s * S: c * T + s * S + S],
                              start=(c == 0), stop=(c == DC - 1),
                              waits=([wtok, ln_toks[s][c]] + (wb + wdst if c == 0 else [])),
                              inc="pe" if c == DC - 1 else None)
                K.rd_done(res_w, mt)
                K.rd_done("big0", mt)
                epilogue(mat, s, Pa, bank, mt, dst, dres)

        def sqstat(src_ap, wtok, seg):
            wsq = K.wr_waits("sqm")
            ts_ = K.emit("act", lambda sc: sc.activation(
                B.sqm[:, :], src_ap, AF.Square, bias=0.0, scale=1.0),
                waits=[wtok] + wsq, inc="act")
            mt = K.mm(PS.pstat[:, :], B.ones_col[:, :], B.sqm[:, :], True, True,
                      waits=[ts_] + K.wr_waits("pstat"), inc="pe")
            K.wr_done("sqm", mt)
            te = K.emit("dve", lambda v, seg=seg: v.tensor_copy(
                B.row_f[:, seg * S:(seg + 1) * S], PS.pstat[:, :]),
                waits=[mt], inc="dve")
            K.wr_done("pstat", te)
            stat_toks.append(te)
            return te

        def rope(src_ap, s, dst, dres, extra=()):
            sl = slice(s * S, (s + 1) * S)
            wsw = K.wr_waits("P2")
            tsw = K.mm(PS.P2[:, :], B.pmat[:, :], src_ap, True, True,
                       waits=list(extra) + wsw + [K.consts_tok], inc="pe")
            t_a = K.emit("dve", lambda v: v.tensor_tensor(
                B.qt2[:, sl], PS.P2[:, :], B.sful[:, sl], ALU.mult),
                waits=[tsw] + K.wr_waits("qt2"), inc="dve")
            K.wr_done("P2", t_a)
            t_b = K.emit("dve", lambda v: v.tensor_tensor(
                dst[:, sl], src_ap, B.cful[:, sl], ALU.mult),
                waits=list(extra) + [K.rope_tok] +
                (K.wr_waits(dres) if s == 0 else []),
                inc="dve")
            t_c = K.emit("dve", lambda v: v.tensor_tensor(
                dst[:, sl], dst[:, sl], B.qt2[:, sl], ALU.add),
                waits=[t_a, t_b], inc="dve")
            K.wr_done("qt2", t_c)
            K.wr_done(dres, t_c, append=(s > 0))
            return t_c

        def epilogue(mat, s, Pa, bank, mt, dst, dres):
            sl = slice(s * S, (s + 1) * S)
            seg = s
            if kind == "a1b":
                tcp = K.emit("act", lambda sc: sc.activation(
                    dst[:, sl], Pa[:, :], AF.Copy, bias=0.0, scale=1.0),
                    waits=[mt] + (K.wr_waits(dres) if s == 0 else []),
                    inc="act")
                K.wr_done(dres, tcp, append=(s > 0))
                if mat != "v":
                    te = sqstat(Pa[:, :], mt, seg)
                    K.rd_done(bank, te)
                K.rd_done(bank, tcp)
            elif kind == "a0a":
                if mat != "v":
                    te = sqstat(Pa[:, :], mt, seg)
                    K.rd_done(bank, te)
                    tcp = K.emit("act", lambda sc: sc.activation(
                        B.qt1[:, sl], Pa[:, :], AF.Copy, bias=0.0, scale=1.0),
                        waits=[mt] + (K.wr_waits("qt1") if s == 0 else []),
                        inc="act")
                    K.rd_done(bank, tcp)
                    K.wr_done("qt1", tcp, append=(s > 0))
                    rope(B.qt1[:, sl], s, dst, dres, extra=(tcp,))
                else:
                    tcp = K.emit("act", lambda sc: sc.activation(
                        dst[:, sl], Pa[:, :], AF.Copy, bias=0.0, scale=1.0),
                        waits=[mt] + (K.wr_waits(dres) if s == 0 else []),
                        inc="act")
                    K.wr_done(dres, tcp, append=(s > 0))
                    K.rd_done(bank, tcp)
            elif kind == "a0b":
                if mat != "v":
                    wsq = K.wr_waits("sqm")
                    ts_ = K.emit("act", lambda sc: sc.activation(
                        B.sqm[:, :], Pa[:, :], AF.Square, bias=0.0, scale=1.0),
                        waits=[mt] + wsq, inc="act")
                    mt2 = K.mm(PS.pstat[:, :], B.ones_col[:, :], B.sqm[:, :],
                               True, True, waits=[ts_] + K.wr_waits("pstat"),
                               inc="pe")
                    K.wr_done("sqm", mt2)
                    te = K.emit("dve", lambda v: v.tensor_copy(
                        B.row_f[:, 4 * S:5 * S], PS.pstat[:, :]),
                        waits=[mt2], inc="dve")
                    K.wr_done("pstat", te)
                    t5_ = K.emit("act", lambda sc: sc.activation(
                        B.row_f[:, 4 * S:5 * S], B.row_f[:, 4 * S:5 * S],
                        AF.Sqrt, bias=B.ccol[0:1, 5:6], scale=B.ccol[0:1, 4:5]),
                        waits=[te], inc="act")
                    t6_ = K.emit("dve", lambda v: v.reciprocal(
                        B.row_f[:, 4 * S:5 * S], B.row_f[:, 4 * S:5 * S]),
                        waits=[t5_], inc="dve")
                    t7_ = K.emit("dve", lambda v: v.tensor_copy(
                        B.row_bf[:, 0:S], B.row_f[:, 4 * S:5 * S]),
                        waits=[t6_], inc="dve")
                    tb_ = K.mm(PS.P3[:, :], B.ones_row[:, :], B.row_bf[:, 0:S],
                               True, True, waits=[t7_] + K.wr_waits("P3"),
                               inc="pe")
                    tbc = K.emit("act", lambda sc: sc.activation(
                        B.bc2[:, :], PS.P3[:, :], AF.Copy, bias=0.0, scale=1.0),
                        waits=[tb_] + K.wr_waits("bc2"), inc="act")
                    K.wr_done("P3", tbc)
                    mi = 0 if mat == "q" else 1
                    tn = K.emit("dve", lambda v, mi=mi: v.scalar_tensor_tensor(
                        B.qt1[:, sl], Pa[:, :], B.nrmw[:, mi:mi + 1],
                        B.bc2[:, :], ALU.mult, ALU.mult),
                        waits=[tnrm, tbc, mt] + (K.wr_waits("qt1") if s == 0 else []),
                        inc="dve")
                    K.wr_done("qt1", tn, append=(s > 0))
                    K.wr_done("bc2", tn)
                    K.rd_done(bank, tn)
                    te2 = sqstat(B.qt1[:, sl], tn, seg)
                    rope(B.qt1[:, sl], s, dst, dres, extra=(tn,))
                else:
                    tcp = K.emit("act", lambda sc: sc.activation(
                        dst[:, sl], Pa[:, :], AF.Copy, bias=0.0, scale=1.0),
                        waits=[mt] + (K.wr_waits(dres) if s == 0 else []),
                        inc="act")
                    K.wr_done(dres, tcp, append=(s > 0))
                    K.rd_done(bank, tcp)
            elif kind == "a1a":
                if mat != "v":
                    tu = K.emit("dve", lambda v: v.tensor_scalar(
                        B.qt1[:, sl], Pa[:, :], B.nrmw[:, 0:1], B.nrmw[:, 1:2],
                        ALU.mult, ALU.add),
                        waits=[tnrm, mt] + (K.wr_waits("qt1") if s == 0 else []),
                        inc="dve")
                    K.wr_done("qt1", tu, append=(s > 0))
                    K.rd_done(bank, tu)
                    wsq = K.wr_waits("sqm")
                    ts_ = K.emit("act", lambda sc: sc.activation(
                        B.sqm[:, :], B.qt1[:, sl], AF.Square, bias=0.0,
                        scale=1.0), waits=[tu] + wsq, inc="act")
                    mtA = K.mm(PS.pstat[:, :], B.ones_col[:, :], B.sqm[:, :],
                               True, True, waits=[ts_] + K.wr_waits("pstat"),
                               inc="pe")
                    K.wr_done("sqm", mtA)
                    teA = K.emit("dve", lambda v: v.tensor_copy(
                        B.row_f[:, 4 * S:5 * S], PS.pstat[:, :]),
                        waits=[mtA], inc="dve")
                    mtB = K.mm(PS.pstat[:, :], B.ones_col[:, :], B.qt1[:, sl],
                               True, True, waits=[tu, teA], inc="pe")
                    teB = K.emit("dve", lambda v: v.tensor_copy(
                        B.row_f[:, 5 * S:6 * S], PS.pstat[:, :]),
                        waits=[mtB], inc="dve")
                    K.wr_done("pstat", teB)
                    r1 = K.emit("dve", lambda v: v.tensor_single_scalar(
                        B.row_f[:, 5 * S:6 * S], B.row_f[:, 5 * S:6 * S],
                        1.0 / DH, ALU.mult), waits=[teB], inc="dve")
                    r2 = K.emit("dve", lambda v: v.tensor_tensor(
                        B.row_f[:, 0:S], B.row_f[:, 5 * S:6 * S],
                        B.row_f[:, 5 * S:6 * S], ALU.mult),
                        waits=[r1], inc="dve")
                    r3 = K.emit("dve", lambda v: v.scalar_tensor_tensor(
                        B.row_f[:, 0:S], B.row_f[:, 4 * S:5 * S], 1.0 / DH,
                        B.row_f[:, 0:S], ALU.mult, ALU.subtract),
                        waits=[r2, teA], inc="dve")
                    r4 = K.emit("act", lambda sc: sc.activation(
                        B.row_f[:, 0:S], B.row_f[:, 0:S], AF.Sqrt,
                        bias=B.ccol[0:1, 3:4], scale=1.0), waits=[r3], inc="act")
                    r5 = K.emit("dve", lambda v: v.reciprocal(
                        B.row_f[:, 0:S], B.row_f[:, 0:S]), waits=[r4],
                        inc="dve")
                    qscale = SCL if mat == "q" else 1.0
                    r6 = K.emit("dve", lambda v, qs=qscale: v.tensor_single_scalar(
                        B.row_bf[:, 0:S], B.row_f[:, 0:S], qs, ALU.mult),
                        waits=[r5], inc="dve")
                    r7 = K.emit("dve", lambda v: v.tensor_tensor(
                        B.row_f[:, S:2 * S], B.row_f[:, 5 * S:6 * S],
                        B.row_f[:, 0:S], ALU.mult), waits=[r5], inc="dve")
                    qs2 = -qscale
                    r8 = K.emit("dve", lambda v, qs=qs2: v.tensor_single_scalar(
                        B.row_bf[:, S:2 * S], B.row_f[:, S:2 * S], qs,
                        ALU.mult), waits=[r7], inc="dve")
                    tb1 = K.mm(PS.P3[:, :], B.ones_row[:, :], B.row_bf[:, 0:S],
                               True, True, waits=[r6] + K.wr_waits("P3"),
                               inc="pe")
                    tb2 = K.mm(PS.P4[:, :], B.ones_row[:, :],
                               B.row_bf[:, S:2 * S], True, True,
                               waits=[r8] + K.wr_waits("P4"), inc="pe")
                    tbc = K.emit("act", lambda sc: sc.activation(
                        B.bc2[:, :], PS.P3[:, :], AF.Copy, bias=0.0, scale=1.0),
                        waits=[tb1] + K.wr_waits("bc2"), inc="act")
                    K.wr_done("P3", tbc)
                    w1 = K.emit("dve", lambda v: v.tensor_tensor(
                        dst[:, sl], B.qt1[:, sl], B.bc2[:, :], ALU.mult),
                        waits=[tbc, tu] + (K.wr_waits(dres) if s == 0 else []),
                        inc="dve")
                    K.wr_done("bc2", w1)
                    w2 = K.emit("dve", lambda v: v.tensor_tensor(
                        dst[:, sl], dst[:, sl], PS.P4[:, :], ALU.add),
                        waits=[tb2, w1], inc="dve")
                    K.wr_done("P4", w2)
                    K.wr_done(dres, w2, append=(s > 0))
                else:
                    tcp = K.emit("act", lambda sc: sc.activation(
                        dst[:, sl], Pa[:, :], AF.Copy, bias=0.0, scale=1.0),
                        waits=[mt] + (K.wr_waits(dres) if s == 0 else []),
                        inc="act")
                    K.wr_done(dres, tcp, append=(s > 0))
                    K.rd_done(bank, tcp)

        proj("k", tkw, wslot(1, par), res_k, B.kh, "kh")
        if use_bound:
            tm1 = K.emit("dve", lambda v: v.reduce_max(
                B.col_f[0:1, 24:25], B.row_f[:, 0:4 * S], mybir.AxisListType.X),
                inc="dve")
            tm2 = K.emit("dve", lambda v: v.tensor_single_scalar(
                B.col_f[0:1, 25:26], B.col_f[0:1, 24:25], 1.0 / DH, ALU.mult),
                waits=[tm1], inc="dve")
            tm3 = K.emit("dve", lambda v: v.tensor_copy(
                B.row_bf[0:1, S:S + 1], B.col_f[0:1, 25:26]),
                waits=[tm2], inc="dve")
            tb_ = K.mm(PS.P3[:, 0:1], B.ones_row[:, :], B.row_bf[0:1, S:S + 1],
                       True, True, waits=[tm3] + K.wr_waits("P3"), inc="pe")
            tk2 = K.emit("act", lambda sc: sc.activation(
                B.col_f[:, 26:27], PS.P3[:, 0:1], AF.Copy, bias=0.0,
                scale=1.0), waits=[tb_], inc="act")
            K.wr_done("P3", tk2)
        proj("q", tqw, wslot(0, par), res_q, B.qh, "qh")
        if use_bound:
            tprev = None
            for i in range(NT_TILES):
                wtr = K.wr_waits("P3")
                ttr = K.mm(PS.P3[:, 0:1], B.row_f[0:1, i * 128:(i + 1) * 128],
                           B.identf[0:1, 0:1], True, True,
                           waits=wtr + [stat_toks[-1], K.consts_tok], inc="pe")
                tcv = K.emit("act", lambda sc, i=i: sc.activation(
                    B.col_f[:, i:i + 1], PS.P3[:, 0:1], AF.Copy,
                    bias=0.0, scale=1.0), waits=[ttr], inc="act")
                K.wr_done("P3", tcv)
                tprev = tcv
            tnb1 = K.emit("act", lambda sc: sc.activation(
                B.col_f[:, 0:NT_TILES], B.col_f[:, 0:NT_TILES], AF.Sqrt,
                bias=0.0, scale=B.col_f[:, 26:27]),
                waits=[tprev, tk2], inc="act")
            tnb = K.emit("dve", lambda v: v.tensor_single_scalar(
                B.col_f[:, 0:NT_TILES], B.col_f[:, 0:NT_TILES], -1.0,
                ALU.mult), waits=[tnb1], inc="dve")
        else:
            tnb = None
        proj("v", tvw, wslot(2, par), res_v, B.vt, "vt")
        vtk = None
        for g4 in range(4):
            ptr = PS.ptrA if g4 % 2 == 0 else PS.ptrB
            pres = "ptrA" if g4 % 2 == 0 else "ptrB"
            wp = K.wr_waits(pres)
            tt_ = None
            for k4 in range(4):
                tt = g4 * 4 + k4
                tt_ = K.tr(ptr[:, k4 * 128:(k4 + 1) * 128],
                           B.vt[:, tt * 128:(tt + 1) * 128], B.ident[:, :],
                           waits=(wp + [K.consts_tok] +
                                  K.rd_waits("vt") if k4 == 0 else []),
                           inc="pe" if k4 == 3 else None)
            vtk = K.emit("dve", lambda v, g4=g4, ptr=ptr: v.tensor_copy(
                B.vsb[:, g4 * S:(g4 + 1) * S], ptr[:, :]),
                waits=[tt_] + (K.wr_waits("vsb") if g4 == 0 else []),
                inc="dve")
            K.wr_done(pres, vtk)
            K.wr_done("vsb", vtk, append=(g4 > 0))

        if getattr(K, "proj_only", False):
            continue
        for i in range(NT_TILES):
            jlo = max(0, i - WTILES) if windowed else 0
            nj = i - jlo + 1
            ncols = nj * 128
            nsp = (ncols + S - 1) // S
            banks = ["P0", "P1", "P2", "P3"][:nsp]
            stoks = []
            for sp_i in range(nsp):
                lc = sp_i * S
                w = min(S, ncols - lc)
                Pa = getattr(PS, banks[sp_i])
                wb = K.wr_waits(banks[sp_i])
                mt = K.mm(Pa[:, 0:w], B.qh[:, i * 128:(i + 1) * 128],
                          B.kh[:, jlo * 128 + lc: jlo * 128 + lc + w],
                          True, True,
                          waits=wb + (K.rd_waits("qh") + K.rd_waits("kh")
                                      if sp_i == 0 else []), inc="pe")
                stoks.append(mt)
            K.rd_done("qh", stoks[-1])
            K.rd_done("kh", stoks[-1])
            dloc = (i - jlo) * 128
            sd, off = dloc // S, dloc % S
            mtoks = list(stoks)
            if kind == "a1a":
                tmk = K.emit("dve", lambda v, sd=sd, off=off: v.tensor_tensor(
                    getattr(PS, banks[sd])[:, off:off + 128],
                    getattr(PS, banks[sd])[:, off:off + 128],
                    B.m_tril01[:, :], ALU.mult),
                    waits=[stoks[sd], K.consts_tok], inc="dve")
                mtoks[sd] = tmk
            else:
                tmk = K.emit("dve", lambda v, sd=sd, off=off: v.tensor_tensor(
                    getattr(PS, banks[sd])[:, off:off + 128],
                    getattr(PS, banks[sd])[:, off:off + 128],
                    B.m_tril_neg[:, :], ALU.add),
                    waits=[stoks[sd], K.consts_tok], inc="dve")
                mtoks[sd] = tmk
                if windowed and i >= WTILES:
                    tmk2 = K.emit("dve", lambda v: v.tensor_tensor(
                        getattr(PS, banks[0])[:, 0:128],
                        getattr(PS, banks[0])[:, 0:128],
                        B.m_triu_neg[:, :], ALU.add),
                        waits=[stoks[0], mtoks[0]], inc="dve")
                    mtoks[0] = tmk2
            etoks = []
            wE = K.wr_waits("slot1E") if kind != "a1a" else []
            wP = K.wr_waits("slot1P")
            for sp_i in range(nsp):
                lc = sp_i * S
                w = min(S, ncols - lc)
                Pa = getattr(PS, banks[sp_i])
                if kind == "a1a":
                    te = K.emit("act", lambda sc, Pa=Pa, lc=lc, w=w, sp_i=sp_i:
                                sc.activation(
                        B.P[:, lc:lc + w], Pa[:, 0:w], AF.Square,
                        bias=0.0, scale=1.0,
                        accum_out=B.col_f[:, 16 + sp_i:17 + sp_i]),
                        waits=[mtoks[sp_i]] + wP, inc="act")
                else:
                    te = K.emit("act", lambda sc, Pa=Pa, lc=lc, w=w, sp_i=sp_i:
                                sc.activation(
                        B.E[:, lc:lc + w], Pa[:, 0:w], AF.Exp,
                        bias=B.col_f[:, i:i + 1], scale=B.ccol[:, 0:1],
                        accum_out=B.col_f[:, 16 + sp_i:17 + sp_i]),
                        waits=[mtoks[sp_i], tnb, K.consts_tok] + wE,
                        inc="act")
                etoks.append(te)
                K.wr_done(banks[sp_i], te)
            td1 = K.emit("dve", lambda v, nsp=nsp: v.reduce_sum(
                B.col_f[:, 20:21], B.col_f[:, 16:16 + nsp],
                mybir.AxisListType.X), waits=etoks, inc="dve")
            epsv = 1e-5 if kind == "a1a" else 1e-30
            td2 = K.emit("dve", lambda v, e=epsv: v.tensor_single_scalar(
                B.col_f[:, 20:21], B.col_f[:, 20:21], e, ALU.add),
                waits=[td1], inc="dve")
            td3 = K.emit("dve", lambda v: v.reciprocal(
                B.col_f[:, 21:22], B.col_f[:, 20:21]), waits=[td2], inc="dve")
            if kind == "a1a":
                tp = K.emit("dve", lambda v, nc_=ncols: v.tensor_single_scalar(
                    B.P[:, 0:nc_], B.P[:, 0:nc_], B.col_f[:, 21:22], ALU.mult),
                    waits=[td3] + etoks + wP, inc="dve")
            else:
                tp = K.emit("dve", lambda v, nc_=ncols: v.tensor_single_scalar(
                    B.P[:, 0:nc_], B.E[:, 0:nc_], B.col_f[:, 21:22], ALU.mult),
                    waits=[td3] + etoks + wP, inc="dve")
                for te in etoks:
                    K.wr_done("slot1E", tp, append=True)
            K.wr_done("slot1P", tp)
            ptk = []
            for g4 in range((nj + 3) // 4):
                ptr = PS.ptrA if g4 % 2 == 0 else PS.ptrB
                pres = "ptrA" if g4 % 2 == 0 else "ptrB"
                wp = K.wr_waits(pres)
                n4 = min(4, nj - g4 * 4)
                tt_ = None
                for k4 in range(n4):
                    jj = g4 * 4 + k4
                    tt_ = K.tr(ptr[:, k4 * 128:(k4 + 1) * 128],
                               B.P[:, jj * 128:(jj + 1) * 128], B.ident[:, :],
                               waits=[tp] + (wp if k4 == 0 else []),
                               inc="pe" if k4 == n4 - 1 else None)
                tev = K.emit("dve", lambda v, g4=g4, n4=n4, ptr=ptr:
                             v.tensor_copy(
                    B.PT[:, g4 * S:g4 * S + n4 * 128], ptr[:, 0:n4 * 128]),
                    waits=[tt_] + (K.wr_waits("slot1PT") if g4 == 0 else []),
                    inc="dve")
                K.wr_done(pres, tev)
                K.wr_done("slot1PT", tev, append=(g4 > 0))
                ptk.append(tev)
            w4 = K.wr_waits("P4")
            mo = None
            for jj in range(nj):
                j = jlo + jj
                mo = K.mm(PS.P4[:, 0:128], B.vsb[:, j * 128:(j + 1) * 128],
                          B.PT[:, jj * 128:(jj + 1) * 128],
                          start=(jj == 0), stop=(jj == nj - 1),
                          waits=[ptk[jj // 4], vtk] + (w4 if jj == 0 else []),
                          inc="pe" if jj == nj - 1 else None)
            K.rd_done("slot1PT", mo)
            K.rd_done("vsb", mo)
            to = K.emit("act", lambda sc, i=i: sc.activation(
                B.osb[:, i * 128:(i + 1) * 128], PS.P4[:, 0:128], AF.Copy,
                bias=0.0, scale=1.0),
                waits=[mo] + (K.wr_waits("osb") if i == 0 else []), inc="act")
            K.wr_done("P4", to)
            K.wr_done("osb", to, append=(i > 0))
        if getattr(K, "dbg", False) and h == 0:
            K.tap("qh", B.qh[:, :], K.rd_waits("qh"))
            K.tap("kh", B.kh[:, :], K.rd_waits("kh"))
            K.tap("vsb", B.vsb[:, :], K.rd_waits("vsb"))
            K.tap("rowf", B.row_f[:, :], [self_last_dve(K)])
            K.tap("colf", B.col_f[:, :], [self_last_dve(K), self_last_act(K)])
            K.tap("E", B.E[:, :], [self_last_act(K)])
            K.tap("P", B.P[:, :], [self_last_dve(K)])
            K.tap("osb", B.osb[:, :], K.rd_waits("osb"))
            K.tap("xln", B.x_lnT[:, :], K.rd_waits("big0"))
            K.tap("cful", B.cful[:, :], [self_last_act(K)])
            K.tap("sful", B.sful[:, :], [self_last_dve(K)])
        tso = K.dma("sp", "xd", ost[h], B.osb[:, :],
                    waits=K.rd_waits("osb"))
        K.rd_done("osb", tso)
        otok_prev[h] = tso

    if getattr(K, "proj_only", False):
        tst = K.dma("sp", "xd", xdst[3], B.x_blk[:, :],
                    waits=[self_last_dve(K), self_last_act(K)])
        K.rd_done("x_blk", tst)
        return tst

    for ts in range(NB):
        tosl = K.dma("sp", "xd", B.osl[:, :].rearrange("p (h u) -> p h u", h=H),
                     ost[:, :, ts * S:(ts + 1) * S].rearrange("h p u -> p h u"),
                     waits=otok_prev + K.wr_waits("slot0"))
        tox = K.dma("sp", "xd", B.x_blk[:, :], xsrc[ts],
                    waits=K.wr_waits("x_blk"))
        rtoks = []
        for dc in range(DC):
            wslot_o = B.wring[:, (dc % 2) * DC * 128:(dc % 2 + 1) * DC * 128]
            res_o = f"wro{dc % 2}"
            tww = K.dma("pool", "wd", wslot_o, Wap["wo"][dc],
                        waits=K.wr_waits(res_o))
            Pa = PS.P0 if dc % 2 == 0 else PS.P1
            bank = "P0" if dc % 2 == 0 else "P1"
            wb = K.wr_waits(bank)
            mt = None
            for c in range(DC):
                mt = K.mm(Pa[:, :], wslot_o[:, c * 128:(c + 1) * 128],
                          B.osl[:, c * S:(c + 1) * S],
                          start=(c == 0), stop=(c == DC - 1),
                          waits=[tww, tosl] + (wb if c == 0 else []),
                          inc="pe" if c == DC - 1 else None)
            K.rd_done(res_o, mt)
            K.rd_done("slot0", mt)
            tr_ = K.emit("dve", lambda v, dc=dc, Pa=Pa: v.tensor_tensor(
                B.x_blk[:, dc * S:(dc + 1) * S],
                B.x_blk[:, dc * S:(dc + 1) * S], Pa[:, :], ALU.add),
                waits=[mt, tox], inc="dve")
            K.wr_done(bank, tr_)
            rtoks.append(tr_)
        tst = K.dma("sp", "xd", xdst[ts], B.x_blk[:, :], waits=rtoks)
        K.rd_done("x_blk", tst)
        for r in rtoks:
            K.rd_done("x_blk", r)
    return tst


NT_TILES = 16
WTILES = 8


def self_last_dve(K):
    return ("dve", K.cnt.get("dve", 0))


def self_last_act(K):
    return ("act", K.cnt.get("act", 0))


def emit_ffn(K, B, PS, kind, Wap, xsrc, xdst):
    is0 = (kind == "ffn0")
    tsg = K.dma("sp", "xd", B.sgc[:, :], Wap["sg"], waits=K.wr_waits("sgc"))
    tsu = None
    if is0:
        tsu = K.dma("sp", "xd", B.suc[:, :], Wap["su"], waits=K.wr_waits("suc"))
    tsd = K.dma("sp", "xd", B.sdc[:, :], Wap["sd"], waits=K.wr_waits("sdc"))

    tst = None
    for b in range(NB):
        tok_x = K.dma("sp", "xd", B.x_blk[:, :], xsrc[b],
                      waits=K.wr_waits("x_blk"))
        dst = [B.x_lnB[:, c * S:(c + 1) * S] for c in range(DC)]
        ln_toks = emit_ln_block(K, B, PS, tok_x, dst, "slot0")
        for tk_ in ln_toks:
            K.wr_done("slot0", tk_, append=True)

        wfuse = K.wr_waits("fuse")
        ftoks = []
        for ic in range(IC):
            par = ic % 2
            gsl = B.wgu[:, (0 + par) * DC * 128:(1 + par) * DC * 128]
            usl = B.wgu[:, (2 + par) * DC * 128:(3 + par) * DC * 128]
            gres, ures = f"wg{par}", f"wu{par}"
            tgw = K.dma("pool", "wd", gsl, Wap["g_blob"][ic],
                        waits=K.wr_waits(gres))
            tuw = None
            if is0:
                tuw = K.dma("pool", "wd", usl, Wap["u_blob"][ic],
                            waits=K.wr_waits(ures))
            Pg = PS.P0 if par == 0 else PS.P2
            gbank = "P0" if par == 0 else "P2"
            Pu = PS.P1 if par == 0 else PS.P3
            ubank = "P1" if par == 0 else "P3"
            wgb = K.wr_waits(gbank)
            mg = None
            for c in range(DC):
                mg = K.mm(Pg[:, :], gsl[:, c * 128:(c + 1) * 128], dst[c],
                          start=(c == 0), stop=(c == DC - 1),
                          waits=[tgw, ln_toks[c]] + (wgb if c == 0 else []),
                          inc="pe" if c == DC - 1 else None)
            K.rd_done(gres, mg)
            K.rd_done("slot0", mg)
            if is0:
                wub = K.wr_waits(ubank)
                mu = None
                for c in range(DC):
                    mu = K.mm(Pu[:, :], usl[:, c * 128:(c + 1) * 128], dst[c],
                              start=(c == 0), stop=(c == DC - 1),
                              waits=[tuw, ln_toks[c]] + (wub if c == 0 else []),
                              inc="pe" if c == DC - 1 else None)
                K.rd_done(ures, mu)
                K.rd_done("slot0", mu)
                stg = B.sqm if par == 0 else B.bc2
                sres = "sqm" if par == 0 else "bc2"
                tsl = K.emit("act", lambda sc, Pg=Pg, stg=stg, ic=ic:
                             sc.activation(
                    stg[:, :], Pg[:, :], AF.Silu, bias=0.0,
                    scale=B.sgc[:, ic:ic + 1]),
                    waits=[mg, tsg] + K.wr_waits(sres), inc="act")
                K.rd_done(gbank, tsl)
                tf = K.emit("dve", lambda v, Pu=Pu, stg=stg, ic=ic:
                            v.scalar_tensor_tensor(
                    B.fuse[:, ic * S:(ic + 1) * S], Pu[:, :],
                    B.suc[:, ic:ic + 1], stg[:, :], ALU.mult, ALU.mult),
                    waits=[mu, tsu, tsl] + (wfuse if ic == 0 else []),
                    inc="dve")
                K.rd_done(ubank, tf)
                K.wr_done(sres, tf)
            else:
                tf = K.emit("act", lambda sc, Pg=Pg, ic=ic: sc.activation(
                    B.fuse[:, ic * S:(ic + 1) * S], Pg[:, :], AF.Gelu,
                    bias=B.sgc[:, ic:ic + 1], scale=1.0),
                    waits=[mg, tsg] + (wfuse if ic == 0 else []), inc="act")
                K.rd_done(gbank, tf)
            ftoks.append(tf)
            K.wr_done("fuse", tf, append=(ic > 0))

        rtoks = []
        for oc in range(DC):
            par = oc % 2
            dsl = B.wdn[:, par * IC * 128:(par + 1) * IC * 128]
            dres = f"wd{par}"
            tdw = K.dma("pool", "wd", dsl, Wap["d_blob"][oc],
                        waits=K.wr_waits(dres))
            Pd = PS.P0 if par == 0 else PS.P1
            dbank = "P0" if par == 0 else "P1"
            wdb = K.wr_waits(dbank)
            if getattr(K, "dbg", False) and oc == DC - 1 and b == 3 and                     not getattr(K, "_dtapped", 0):
                K._dtapped = 1
                K.tap("fuse_end", B.fuse[:, 48 * S:64 * S], [tdw] + wdb)
                K.tap("wd15", dsl, [tdw])
            md = None
            for c in range(IC):
                md = K.mm(Pd[:, :], dsl[:, c * 128:(c + 1) * 128],
                          B.fuse[:, c * S:(c + 1) * S],
                          start=(c == 0), stop=(c == IC - 1),
                          waits=[tdw, ftoks[c]] + (wdb if c == 0 else []),
                          inc="pe" if c == IC - 1 else None)
            K.rd_done(dres, md)
            K.rd_done("fuse", md)
            if is0:
                tr_ = K.emit("dve", lambda v, oc=oc, Pd=Pd:
                             v.scalar_tensor_tensor(
                    B.x_blk[:, oc * S:(oc + 1) * S], Pd[:, :],
                    B.sdc[:, oc:oc + 1], B.x_blk[:, oc * S:(oc + 1) * S],
                    ALU.mult, ALU.add), waits=[md, tsd, tok_x], inc="dve")
            else:
                tr_ = K.emit("dve", lambda v, oc=oc, Pd=Pd:
                             v.scalar_tensor_tensor(
                    B.x_blk[:, oc * S:(oc + 1) * S], Pd[:, :],
                    B.sdc[:, oc:oc + 1], B.x_blk[:, oc * S:(oc + 1) * S],
                    ALU.add, ALU.add), waits=[md, tsd, tok_x], inc="dve")
            K.wr_done(dbank, tr_)
            rtoks.append(tr_)
        if getattr(K, "dbg", False) and b == 3 and not getattr(K, "_ftapped", 0):
            K._ftapped = 1
            K.tap("fxblk3", B.x_blk[:, :], rtoks)
        tst = K.dma("sp", "xd", xdst[b], B.x_blk[:, :], waits=rtoks)
        K.rd_done("x_blk", tst)
        for r in rtoks:
            K.rd_done("x_blk", r)
    return tst


SUBLAYERS = [
    ("a0a", "L0aa"), ("ffn0", "L0f"), ("a0b", "L0ab"), ("ffn0", "L0f"),
    ("a1a", "L1aa"), ("ffn1", "L1f"), ("a1b", "L1ab"), ("ffn1", "L1f"),
]


def build_program(sub_idx_list):
    K = Prog()
    nc = K.nc
    B = alloc_sbuf(nc)
    PS = alloc_psum(nc)
    need_rope = any(SUBLAYERS[i][0] in ("a0a", "a0b") for i in sub_idx_list)
    need_attn = any(SUBLAYERS[i][0].startswith("a") for i in sub_idx_list)

    xin = nc.dram_tensor("xin", [NB, 128, DC * S], F32, kind="ExternalInput").ap()
    xout = nc.dram_tensor("xout", [NB, 128, DC * S], F32,
                          kind="ExternalOutput").ap()
    nsub = len(sub_idx_list)
    xst = xst2 = None
    if nsub > 1:
        xst = nc.dram_tensor("xst", [NB, 128, DC * S], F32, kind="Internal").ap()
    if nsub > 2:
        xst2 = nc.dram_tensor("xst2", [NB, 128, DC * S], F32,
                              kind="Internal").ap()
    ost = None
    if need_attn:
        ost = nc.dram_tensor("ost", [H, 128, T], BF16, kind="Internal").ap()

    Waps = {}
    for i in sub_idx_list:
        kind, pfx = SUBLAYERS[i]
        if pfx in Waps:
            continue
        W = {}
        if kind.startswith("a"):
            for nm in ("wq", "wk", "wv", "wo"):
                W[nm] = nc.dram_tensor(f"{pfx}_{nm}", [H, 128, DC * 128], F32,
                                       kind="ExternalInput").ap()
            if kind in ("a0b", "a1a"):
                W["nrm"] = nc.dram_tensor(f"{pfx}_nrm", [128, 2], F32,
                                          kind="ExternalInput").ap()
            W["ost"] = ost
        elif kind == "ffn0":
            W["g_blob"] = nc.dram_tensor(f"{pfx}_g", [IC, 128, DC * 128], I8,
                                         kind="ExternalInput").ap()
            W["u_blob"] = nc.dram_tensor(f"{pfx}_u", [IC, 128, DC * 128], I8,
                                         kind="ExternalInput").ap()
            W["d_blob"] = nc.dram_tensor(f"{pfx}_d", [DC, 128, IC * 128], I8,
                                         kind="ExternalInput").ap()
            W["sg"] = nc.dram_tensor(f"{pfx}_sg", [128, IC], F32,
                                     kind="ExternalInput").ap()
            W["su"] = nc.dram_tensor(f"{pfx}_su", [128, IC], F32,
                                     kind="ExternalInput").ap()
            W["sd"] = nc.dram_tensor(f"{pfx}_sd", [128, DC], F32,
                                     kind="ExternalInput").ap()
        else:
            W["g_blob"] = nc.dram_tensor(f"{pfx}_g", [IC, 128, DC * 128], F32,
                                         kind="ExternalInput").ap()
            W["d_blob"] = nc.dram_tensor(f"{pfx}_d", [DC, 128, IC * 128], F32,
                                         kind="ExternalInput").ap()
            W["sg"] = nc.dram_tensor(f"{pfx}_sg", [128, IC], F32,
                                     kind="ExternalInput").ap()
            W["sd"] = nc.dram_tensor(f"{pfx}_sd", [128, DC], F32,
                                     kind="ExternalInput").ap()
        Waps[pfx] = W

    emit_prelude(K, B, PS, need_rope)
    last_tok = None
    for j, i in enumerate(sub_idx_list):
        kind, pfx = SUBLAYERS[i]
        src = xin if j == 0 else (xst if j % 2 == 1 else xst2)
        dst = xout if j == nsub - 1 else (xst if j % 2 == 0 else xst2)
        if kind.startswith("a"):
            last_tok = emit_attention(K, B, PS, kind, Waps[pfx], src, dst)
        else:
            last_tok = emit_ffn(K, B, PS, kind, Waps[pfx], src, dst)
    nc = K.finalize([last_tok])
    return nc


def proj_blob(W):
    din, dout = W.shape
    ndc, noc = din // 128, dout // 128
    return np.ascontiguousarray(
        W.reshape(ndc, 128, noc, 128).transpose(2, 1, 0, 3)
        .reshape(noc, 128, ndc * 128))


def x_to_blocks(xc):
    xT = xc.T
    return np.ascontiguousarray(
        xT.reshape(DC, 128, NB, S).transpose(2, 1, 0, 3)
        .reshape(NB, 128, DC * S))


def x_from_blocks(xb):
    xT = xb.reshape(NB, 128, DC, S).transpose(2, 1, 0, 3).reshape(D, T)
    return np.ascontiguousarray(xT.T)


def colmaj(v):
    return np.ascontiguousarray(v.reshape(-1, 128).T)


def prep_weights(inputs):
    P = {}
    P["L0aa"] = {
        "wq": proj_blob(inputs["L0_attn_a_wq"]),
        "wk": proj_blob(inputs["L0_attn_a_wk"]),
        "wv": proj_blob(inputs["L0_attn_a_wv"]),
        "wo": proj_blob(inputs["L0_attn_a_wo"]),
    }
    P["L0ab"] = {
        "wq": proj_blob(np.ascontiguousarray(inputs["L0_attn_b_Wq"].T)),
        "wk": proj_blob(np.ascontiguousarray(inputs["L0_attn_b_Wk"].T)),
        "wv": proj_blob(np.ascontiguousarray(inputs["L0_attn_b_Wv"].T)),
        "wo": proj_blob(np.ascontiguousarray(inputs["L0_attn_b_Wo"].T)),
        "nrm": np.ascontiguousarray(np.stack(
            [inputs["L0_attn_b_q_norm_w"], inputs["L0_attn_b_k_norm_w"]],
            axis=1)),
    }
    P["L0f"] = {
        "g_blob": proj_blob(inputs["L0_ffn_w_gate"]),
        "u_blob": proj_blob(inputs["L0_ffn_w_up"]),
        "d_blob": proj_blob(inputs["L0_ffn_w_down"]),
        "sg": colmaj(inputs["L0_ffn_s_gate"]),
        "su": colmaj(inputs["L0_ffn_s_up"]),
        "sd": colmaj(inputs["L0_ffn_s_down"]),
    }
    P["L1aa"] = {
        "wq": proj_blob(np.ascontiguousarray(inputs["L1_attn_a_Wq"].T)),
        "wk": proj_blob(np.ascontiguousarray(inputs["L1_attn_a_Wk"].T)),
        "wv": proj_blob(np.ascontiguousarray(inputs["L1_attn_a_Wv"].T)),
        "wo": proj_blob(np.ascontiguousarray(inputs["L1_attn_a_Wo"].T)),
        "nrm": np.ascontiguousarray(np.stack(
            [inputs["L1_attn_a_gamma"], inputs["L1_attn_a_beta"]], axis=1)),
    }
    P["L1f"] = {
        "g_blob": proj_blob(np.ascontiguousarray(inputs["L1_ffn_W0"].T)),
        "d_blob": proj_blob(np.ascontiguousarray(inputs["L1_ffn_W1"].T)),
        "sg": colmaj(inputs["L1_ffn_bias0"]),
        "sd": colmaj(inputs["L1_ffn_bias1"]),
    }
    P["L1ab"] = {
        "wq": proj_blob(np.ascontiguousarray(
            inputs["L1_attn_b_W_Q"].transpose(2, 0, 1).reshape(D, D))),
        "wk": proj_blob(np.ascontiguousarray(
            inputs["L1_attn_b_W_K"].transpose(2, 0, 1).reshape(D, D))),
        "wv": proj_blob(np.ascontiguousarray(
            inputs["L1_attn_b_W_V"].transpose(2, 0, 1).reshape(D, D))),
        "wo": proj_blob(np.ascontiguousarray(
            inputs["L1_attn_b_W_O"].transpose(0, 2, 1).reshape(D, D))),
    }
    return P


def in_map_for(sub_idx_list, P, xblk):
    m = {"xin": xblk}
    seen = set()
    for i in sub_idx_list:
        kind, pfx = SUBLAYERS[i]
        if pfx in seen:
            continue
        seen.add(pfx)
        W = P[pfx]
        if kind.startswith("a"):
            m[f"{pfx}_wq"] = W["wq"]
            m[f"{pfx}_wk"] = W["wk"]
            m[f"{pfx}_wv"] = W["wv"]
            m[f"{pfx}_wo"] = W["wo"]
            if "nrm" in W:
                m[f"{pfx}_nrm"] = W["nrm"]
        else:
            m[f"{pfx}_g"] = W["g_blob"]
            m[f"{pfx}_d"] = W["d_blob"]
            m[f"{pfx}_sg"] = W["sg"]
            m[f"{pfx}_sd"] = W["sd"]
            if "u_blob" in W:
                m[f"{pfx}_u"] = W["u_blob"]
                m[f"{pfx}_su"] = W["su"]
    return m


def run_groups(inputs, groups):
    P = prep_weights(inputs)
    x = np.asarray(inputs["L0_attn_a_hidden_states"], np.float32)
    ncores = x.shape[0]
    xblk = [x_to_blocks(x[c]) for c in range(ncores)]
    for grp in groups:
        nc = build_program(grp)
        in_maps = [in_map_for(grp, P, xblk[c]) for c in range(ncores)]
        res = run_bass_kernel_spmd(nc, in_maps, core_ids=list(range(ncores)))
        xblk = [np.asarray(res.results[c]["xout"], np.float32)
                for c in range(ncores)]
    return np.stack([x_from_blocks(xblk[c]) for c in range(ncores)])


def kernel(**inputs):
    out = run_groups(inputs, [list(range(8))])
    return out.astype(np.float32)


# version 06
# speedup vs baseline: 1.0798x; 1.0798x over previous
import math
import numpy as np
import concourse.bass as bass
import concourse.mybir as mybir
from concourse.bass_utils import run_bass_kernel_spmd

F32 = mybir.dt.float32
BF16 = mybir.dt.bfloat16
I8 = mybir.dt.int8
I32 = mybir.dt.int32
AF = mybir.ActivationFunctionType
ALU = mybir.AluOpType

T = 2048
D = 2048
DC = 16
H = 16
DH = 128
NB = 4
SPAN = 512
IC = 64
SCL = 1.0 / math.sqrt(DH)
PI = math.pi
NEG = -1.0e30
SBUF_LIMIT = 212863
S = SPAN

ENGS = ("pe", "act", "dve", "pool", "sp")

ALIAS = {
    "big0": ["fuse"], "fuse": ["big0"],
    "slot0": [],
    "slot1E": ["wgu"], "slot1P": ["wgu"], "slot1PT": ["wgu"],
    "wgu": ["slot1E", "slot1P", "slot1PT"],
    "qh": ["wdn"], "kh": ["wdn"], "vt": ["wdn"], "vsb": ["wdn"],
    "qt1": ["wdn"], "qt2": ["wdn"], "osb": ["wdn"],
    "wdn": ["qh", "kh", "vt", "vsb", "qt1", "qt2", "osb"],
    "x_blk": ["rowi"], "rowi": ["x_blk"],
    "wr00": ["wro0"], "wro0": ["wr00"],
    "wr01": ["wro1"], "wro1": ["wr01"],
}


class Prog:
    def __init__(self):
        self.nc = bass.Bass("TRN2", target_bir_lowering=False)
        self.ops = {e: [] for e in ENGS}
        self.sems = {}
        self.cnt = {}
        self.waited = {e: {} for e in ENGS}
        self.rsrc = {}

    def _sem(self, name):
        if name not in self.sems:
            self.sems[name] = self.nc.alloc_semaphore(name=name)
            self.cnt[name] = 0
        return self.sems[name]

    def emit(self, eng, fn, waits=(), inc=None, by=1):
        real = []
        for wtok in waits:
            if wtok is None:
                continue
            n, v = wtok
            self._sem(n)
            if n == eng and eng == "pe":
                continue
            if self.waited[eng].get(n, -1) >= v:
                continue
            self.waited[eng][n] = v
            real.append((n, v))
        tok = None
        if inc is not None:
            self._sem(inc)
            self.cnt[inc] += by
            tok = (inc, self.cnt[inc])
        sems = self.sems

        def run(e, fn=fn, real=tuple(real), inc=inc, by=by):
            for n, v in real:
                e.wait_ge(sems[n], v)
            ins = fn(e)
            if inc is not None:
                ins.then_inc(sems[inc], by)
        self.ops[eng].append(run)
        return tok

    def mm(self, out, lhsT, rhs, start, stop, waits=(), inc=None):
        return self.emit(
            "pe", lambda t, o=out, a=lhsT, b=rhs, s1=start, s2=stop:
            t.matmul(o, a, b, start=s1, stop=s2), waits, inc)

    def tr(self, out, in_, ident, waits=(), inc=None):
        return self.emit(
            "pe", lambda t, o=out, i=in_, d=ident: t.transpose(o, i, d),
            waits, inc)

    def dma(self, q, sem, out, in_, waits=()):
        return self.emit(
            q, lambda e, o=out, i=in_: e.dma_start(out=o, in_=i),
            waits, sem, by=16)

    def _grp(self, n):
        return [n] + ALIAS.get(n, [])

    def wr_waits(self, n):
        out = []
        for n2 in self._grp(n):
            r = self.rsrc.get(n2)
            if r:
                out += r[0] + r[1]
                self.rsrc[n2] = ([], [])
        return out

    def wr_done(self, n, tok, append=False):
        r = self.rsrc.setdefault(n, ([], []))
        if append:
            r[0].append(tok)
        else:
            self.rsrc[n] = ([tok], [])

    def rd_waits(self, n):
        return list(self.rsrc.get(n, ([], []))[0])

    def rd_done(self, n, tok):
        self.rsrc.setdefault(n, ([], []))[1].append(tok)

    def tap(self, name, ap, waits):
        shape = [int(c) for _, c in ap.ap]
        dt_ = ap.tensor.dtype
        d = self.nc.dram_tensor(f"dbg_{name}", shape, dt_,
                                kind="ExternalOutput").ap()
        return self.dma("sp", "xd", d, ap, waits=waits)

    def finalize(self, final_waits):
        sems = self.sems
        for wtok in final_waits:
            if wtok is None:
                continue
            n, v = wtok
            self._sem(n)
            if self.waited["sp"].get(n, -1) >= v:
                continue
            self.waited["sp"][n] = v

            def run(e, n=n, v=v):
                e.wait_ge(sems[n], v)
            self.ops["sp"].append(run)
        nc = self.nc
        with nc.Block() as block:
            @block.tensor
            def _(t):
                for f in self.ops["pe"]:
                    f(t)

            @block.scalar
            def _(sc):
                for f in self.ops["act"]:
                    f(sc)

            @block.vector
            def _(v):
                for f in self.ops["dve"]:
                    f(v)

            @block.gpsimd
            def _(g):
                for f in self.ops["pool"]:
                    f(g)

            @block.sync
            def _(s):
                for f in self.ops["sp"]:
                    f(s)
        return nc


class Bufs:
    pass


def alloc_sbuf(nc):
    B = Bufs()
    base = (nc.sbuf_base + 63) // 64 * 64
    cur = [base]

    def at(name, shape, dtype, offset=None):
        bpe = {F32: 4, BF16: 2, I8: 1, I32: 4}[dtype]
        nbytes = int(np.prod(shape[1:])) * bpe
        if offset is None:
            offset = cur[0]
        cur[0] = max(cur[0], offset + ((nbytes + 63) // 64) * 64)
        h = nc.alloc_sbuf_tensor_at(name, shape, dtype, offset=offset)
        setattr(B, name, h.ap())
        return offset

    at("ones_col", [128, 1], BF16)
    at("ones_row", [1, 128], BF16)
    at("ident", [128, 128], BF16)
    at("identf", [128, 128], F32)
    at("pmat", [128, 128], BF16)
    at("m_tril_neg", [128, 128], F32)
    at("m_triu_neg", [128, 128], F32)
    at("m_tril01", [128, 128], F32)
    at("sign_col", [128, 1], F32)
    at("cful", [128, T], BF16)
    at("sful", [128, T], BF16)
    at("row_f", [1, 6 * S], F32)
    at("row_bf", [1, 2 * S], BF16)
    at("col_f", [128, 32], F32)
    at("ccol", [128, 8], F32)
    at("sqm", [128, S], BF16)
    at("bc2", [128, S], BF16)
    at("sgc", [128, IC], F32)
    at("suc", [128, IC], F32)
    at("sdc", [128, DC], F32)
    at("nrmw", [128, 2], F32)

    xblk = at("x_blk", [128, DC * S], F32)
    at("rowi", [1, T], I32, xblk)
    at("tabf", [128, S], F32, xblk + 16384)
    at("ifr_row", [1, 128], F32, xblk + 20480)
    at("tabi", [128, S], I32, xblk + 24576)
    at("tabg", [128, S], F32, xblk + 26624)
    big0 = at("x_lnT", [128, DC * T], BF16)
    at("fuse", [128, IC * S], BF16, big0)
    slot0 = at("osl", [128, DC * S], BF16)
    at("x_lnB", [128, DC * S], BF16, slot0)
    at("stage", [128, DC * S], BF16, slot0)
    slot1 = at("E", [128, T], F32)
    at("P", [128, T], BF16)
    at("PT", [128, T], BF16)
    at("wgu", [128, 4 * DC * 128], BF16, slot1)
    slot2 = at("qh", [128, T], BF16)
    at("kh", [128, T], BF16)
    at("vt", [128, T], BF16)
    at("vsb", [128, T], BF16)
    at("qt1", [128, T], BF16)
    at("qt2", [128, T], BF16)
    at("osb", [128, T], BF16)
    at("wdn", [128, 2 * IC * 128], BF16, slot2)
    at("wring", [128, 5 * DC * 128], BF16)
    assert cur[0] <= nc.sbuf_top, f"SBUF over budget: {cur[0]} > {nc.sbuf_top}"
    B.end = cur[0]
    return B


def alloc_psum(nc):
    PS = Bufs()
    for i in range(5):
        setattr(PS, f"P{i}", nc.alloc_psum_tensor(f"P{i}", [128, S], F32).ap())
    PS.pstat = nc.alloc_psum_tensor("pstat", [1, S], F32).ap()
    PS.ptrA = nc.alloc_psum_tensor("ptrA", [128, S], BF16).ap()
    PS.ptrB = nc.alloc_psum_tensor("ptrB", [128, S], BF16).ap()
    return PS


def emit_prelude(K, B, PS, need_rope):
    K.rope_tok = None
    tk = None

    def pool(fn):
        nonlocal tk
        tk = K.emit("pool", fn, inc="pool")
        return tk

    pool(lambda g: g.memset(B.ones_col[:, :], 1.0))
    pool(lambda g: g.memset(B.ones_row[:, :], 1.0))
    pool(lambda g: g.memset(B.ident[:, :], 1.0))
    pool(lambda g: g.affine_select(
        B.ident[:, :], B.ident[:, :], [[1, 128]], ALU.is_equal, 0.0,
        base=0, channel_multiplier=-1))
    pool(lambda g: g.memset(B.identf[:, :], 1.0))
    pool(lambda g: g.affine_select(
        B.identf[:, :], B.identf[:, :], [[1, 128]], ALU.is_equal, 0.0,
        base=0, channel_multiplier=-1))
    pool(lambda g: g.memset(B.pmat[:, :], 1.0))
    pool(lambda g: g.affine_select(
        B.pmat[:, 0:64], B.pmat[:, 0:64], [[1, 64]], ALU.is_equal, 0.0,
        base=64, channel_multiplier=-1))
    pool(lambda g: g.affine_select(
        B.pmat[:, 64:128], B.pmat[:, 64:128], [[1, 64]], ALU.is_equal, 0.0,
        base=0, channel_multiplier=-1))
    pool(lambda g: g.memset(B.m_tril_neg[:, :], 0.0))
    pool(lambda g: g.affine_select(
        B.m_tril_neg[:, :], B.m_tril_neg[:, :], [[-1, 128]], ALU.is_ge, NEG,
        base=0, channel_multiplier=1))
    pool(lambda g: g.memset(B.m_triu_neg[:, :], 0.0))
    pool(lambda g: g.affine_select(
        B.m_triu_neg[:, :], B.m_triu_neg[:, :], [[1, 128]], ALU.is_ge, NEG,
        base=-1, channel_multiplier=-1))
    pool(lambda g: g.memset(B.m_tril01[:, :], 1.0))
    pool(lambda g: g.affine_select(
        B.m_tril01[:, :], B.m_tril01[:, :], [[-1, 128]], ALU.is_ge, 0.0,
        base=0, channel_multiplier=1))
    pool(lambda g: g.memset(B.ccol[:, 0:1], SCL))
    pool(lambda g: g.memset(B.ccol[:, 1:2], -PI))
    pool(lambda g: g.memset(B.ccol[:, 2:3], -2.0 * math.log(10000.0) / DH))
    pool(lambda g: g.memset(B.ccol[:, 3:4], 1e-5))
    pool(lambda g: g.memset(B.ccol[:, 4:5], 1.0 / DH))
    pool(lambda g: g.memset(B.ccol[:, 5:6], 1e-6))
    pool(lambda g: g.memset(B.ccol[:, 6:7], 2.0 * PI))
    pool(lambda g: g.memset(B.sign_col[:, :], 1.0))
    tsign = pool(lambda g: g.memset(B.sign_col[0:64, :], -1.0))
    K.consts_tok = tsign

    if need_rope:
        tp = pool(lambda g: g.iota(B.rowi[:, :], [[1, T]], base=0,
                                   channel_multiplier=0))
        tpos = K.emit("dve", lambda v: v.tensor_copy(
            B.row_f[:, 0:T], B.rowi[:, :]), waits=[tp], inc="dve")
        K.rd_done("rowi", tpos)
        tpc = pool(lambda g: g.iota(B.tabi[:, 0:1], [[0, 1]], base=0,
                                    channel_multiplier=1))
        tc1 = K.emit("dve", lambda v: v.tensor_copy(
            B.ccol[:, 7:8], B.tabi[:, 0:1]), waits=[tpc, K.consts_tok],
            inc="dve")
        tc2 = K.emit("dve", lambda v: v.tensor_single_scalar(
            B.ccol[64:128, 7:8], B.ccol[64:128, 7:8], -64.0, ALU.add),
            waits=[tc1], inc="dve")
        tc3 = K.emit("act", lambda sc: sc.activation(
            B.ccol[:, 7:8], B.ccol[:, 7:8], AF.Exp,
            bias=0.0, scale=B.ccol[:, 2:3]), waits=[tc2], inc="act")
        inv2pi = 1.0 / (2.0 * PI)
        prev_td = None
        for s in range(NB):
            sl = slice(s * S, (s + 1) * S)
            tpi = K.emit("pool", lambda g, s=s: g.iota(
                B.tabi[:, :], [[1, S]], base=s * S, channel_multiplier=0),
                waits=[tc1, prev_td], inc="pool")
            tg1 = K.emit("dve", lambda v: v.tensor_copy(
                B.tabg[:, :], B.tabi[:, :]), waits=[tpi], inc="dve")
            tang = K.emit("dve", lambda v: v.tensor_single_scalar(
                B.tabg[:, :], B.tabg[:, :], B.ccol[:, 7:8], ALU.mult),
                waits=[tg1, tc3], inc="dve")

            def redsin(dstap, shift, wprev):
                t1 = K.emit("dve", lambda v, shift=shift: v.tensor_scalar(
                    B.tabf[:, :], B.tabg[:, :], shift, inv2pi,
                    ALU.add, ALU.mult), waits=[tang, wprev], inc="dve")
                t2 = K.emit("dve", lambda v: v.tensor_copy(
                    B.tabi[:, :], B.tabf[:, :]), waits=[t1], inc="dve")
                t3 = K.emit("dve", lambda v: v.tensor_tensor(
                    B.tabf[:, :], B.tabf[:, :], B.tabi[:, :], ALU.subtract),
                    waits=[t2], inc="dve")
                t4 = K.emit("act", lambda sc, dstap=dstap: sc.activation(
                    dstap, B.tabf[:, :], AF.Sin, bias=0.0,
                    scale=B.ccol[:, 6:7]), waits=[t3], inc="act")
                return t4
            tb = redsin(B.sful[:, sl], 0.0, None)
            td = redsin(B.cful[:, sl], 0.5 * PI, tb)
            prev_td = td
            tsg_ = K.emit("dve", lambda v, sl=sl: v.tensor_single_scalar(
                B.sful[:, sl], B.sful[:, sl], B.sign_col[:, :], ALU.mult),
                waits=[tb, tsign], inc="dve")
            K.rope_tok = td
            K.rd_done("x_blk", td)
            K.rd_done("x_blk", tsg_)


def emit_ln_block(K, B, PS, tok_x, dst_slices, dst_res):
    w0 = K.wr_waits("slot0")
    t_cp = K.emit("act", lambda sc: sc.activation(
        B.stage[:, :], B.x_blk[:, :], AF.Copy, bias=0.0, scale=1.0),
        waits=[tok_x] + w0, inc="act")
    K.rd_done("x_blk", t_cp)
    mt = None
    wps = K.wr_waits("pstat")
    for c in range(DC):
        mt = K.mm(PS.pstat[:, :], B.ones_col[:, :],
                  B.stage[:, c * S:(c + 1) * S],
                  start=(c == 0), stop=(c == DC - 1),
                  waits=([t_cp, K.consts_tok] + wps) if c == 0 else [t_cp],
                  inc="pe" if c == DC - 1 else None)
    tB = K.emit("dve", lambda v: v.tensor_copy(
        B.row_f[:, S:2 * S], PS.pstat[:, :]), waits=[mt], inc="dve")
    t_sq1 = K.emit("act", lambda sc: sc.activation(
        B.stage[:, :], B.x_blk[:, :], AF.Square, bias=0.0, scale=1.0),
        waits=[mt], inc="act")
    K.rd_done("x_blk", t_sq1)
    for c in range(DC):
        mt = K.mm(PS.pstat[:, :], B.ones_col[:, :],
                  B.stage[:, c * S:(c + 1) * S],
                  start=(c == 0), stop=(c == DC - 1),
                  waits=[t_sq1, tB], inc="pe" if c == DC - 1 else None)
    K.wr_done("slot0", mt)
    tA = K.emit("dve", lambda v: v.tensor_copy(B.row_f[:, 0:S], PS.pstat[:, :]),
                waits=[mt], inc="dve")
    K.wr_done("pstat", tA)
    t2 = K.emit("dve", lambda v: v.tensor_single_scalar(
        B.row_f[:, 2 * S:3 * S], B.row_f[:, S:2 * S], 1.0 / D, ALU.mult),
        waits=[tB], inc="dve")
    t3 = K.emit("dve", lambda v: v.tensor_tensor(
        B.row_f[:, 3 * S:4 * S], B.row_f[:, 2 * S:3 * S],
        B.row_f[:, 2 * S:3 * S], ALU.mult), waits=[t2], inc="dve")
    t4 = K.emit("dve", lambda v: v.scalar_tensor_tensor(
        B.row_f[:, 3 * S:4 * S], B.row_f[:, 0:S], 1.0 / D,
        B.row_f[:, 3 * S:4 * S], ALU.mult, ALU.subtract), waits=[t3, tA], inc="dve")
    t5 = K.emit("act", lambda sc: sc.activation(
        B.row_f[:, 0:S], B.row_f[:, 3 * S:4 * S], AF.Sqrt,
        bias=B.ccol[0:1, 3:4], scale=1.0), waits=[t4], inc="act")
    t6 = K.emit("dve", lambda v: v.reciprocal(B.row_f[:, 0:S], B.row_f[:, 0:S]),
                waits=[t5], inc="dve")
    t7 = K.emit("dve", lambda v: v.tensor_copy(B.row_bf[:, 0:S],
                                               B.row_f[:, 0:S]),
                waits=[t6], inc="dve")
    t8 = K.emit("dve", lambda v: v.tensor_tensor(
        B.row_f[:, S:2 * S], B.row_f[:, 2 * S:3 * S], B.row_f[:, 0:S],
        ALU.mult), waits=[t6], inc="dve")
    t9 = K.emit("dve", lambda v: v.tensor_single_scalar(
        B.row_bf[:, S:2 * S], B.row_f[:, S:2 * S], -1.0, ALU.mult),
        waits=[t8], inc="dve")
    tb1 = K.mm(PS.P3[:, :], B.ones_row[:, :], B.row_bf[:, 0:S], True, True,
               waits=[t7] + K.wr_waits("P3"), inc="pe")
    tb2 = K.mm(PS.P4[:, :], B.ones_row[:, :], B.row_bf[:, S:2 * S], True, True,
               waits=[t9] + K.wr_waits("P4"), inc="pe")
    out = []
    wdst = K.wr_waits(dst_res) if dst_res else []
    for c in range(DC):
        dsl = dst_slices[c]
        ta = K.emit("dve", lambda v, c=c, dsl=dsl: v.tensor_tensor(
            dsl, B.x_blk[:, c * S:(c + 1) * S], PS.P3[:, :], ALU.mult),
            waits=[tb1] + (wdst if c == 0 else []), inc="dve")
        tbb = K.emit("dve", lambda v, dsl=dsl: v.tensor_tensor(
            dsl, dsl, PS.P4[:, :], ALU.add), waits=[tb2, ta], inc="dve")
        K.rd_done("x_blk", tbb)
        out.append(tbb)
    K.wr_done("P3", out[-1])
    K.wr_done("P4", out[-1])
    return out


def emit_attention(K, B, PS, kind, Wap, xsrc, xdst):
    nc = K.nc
    windowed = kind in ("a0a", "a0b")
    use_rope = kind in ("a0a", "a0b")
    use_bound = kind in ("a0a", "a0b", "a1b")
    ost = Wap["ost"]

    tnrm = None
    if kind in ("a0b", "a1a"):
        tnrm = K.dma("sp", "xd", B.nrmw[:, :], Wap["nrm"],
                     waits=K.wr_waits("nrmw"))

    ln_toks = [[None] * DC for _ in range(NB)]
    wbig = K.wr_waits("big0")
    for b in range(NB):
        tok_x = K.dma("sp", "xd", B.x_blk[:, :], xsrc[b],
                      waits=K.wr_waits("x_blk") + (wbig if b == 0 else []))
        if getattr(K, "dbg", False) and b == 3 and not getattr(K, "_tapped", 0):
            K._tapped = 1
            K.tap("xblk3", B.x_blk[:, :], [tok_x])
        dst = [B.x_lnT[:, c * T + b * S: c * T + b * S + S] for c in range(DC)]
        toks = emit_ln_block(K, B, PS, tok_x, dst, None)
        ln_toks[b] = toks
        for tk_ in toks:
            K.wr_done("big0", tk_, append=True)

    if getattr(K, "ln_only", False):
        tst = K.dma("sp", "xd", xdst[3], B.x_blk[:, :],
                    waits=ln_toks[3])
        K.rd_done("x_blk", tst)
        return tst

    otok_prev = [None] * H
    for h in range(H):
        stat_toks = []

        def wslot(idx, par):
            o = (idx * 2 + par) * DC * 128 if idx < 2 else 4 * DC * 128
            return B.wring[:, o:o + DC * 128]
        par = h % 2
        res_q, res_k, res_v = f"wr0{par}", f"wr1{par}", "wr20"
        tkw = K.dma("pool", "wd", wslot(1, par), Wap["wk"][h],
                    waits=K.wr_waits(res_k))
        tqw = K.dma("pool", "wd", wslot(0, par), Wap["wq"][h],
                    waits=K.wr_waits(res_q))
        tvw = K.dma("pool", "wd", wslot(2, par), Wap["wv"][h],
                    waits=K.wr_waits(res_v))

        def proj(mat, wtok, wap, res_w, dst, dres):
            wdst = K.wr_waits(dres)
            for s in range(NB):
                Pa = PS.P0 if s % 2 == 0 else PS.P1
                bank = "P0" if s % 2 == 0 else "P1"
                wb = K.wr_waits(bank)
                mt = None
                for c in range(DC):
                    mt = K.mm(Pa[:, :], wap[:, c * 128:(c + 1) * 128],
                              B.x_lnT[:, c * T + s * S: c * T + s * S + S],
                              start=(c == 0), stop=(c == DC - 1),
                              waits=([wtok, ln_toks[s][c]] + (wb + wdst if c == 0 else [])),
                              inc="pe" if c == DC - 1 else None)
                K.rd_done(res_w, mt)
                K.rd_done("big0", mt)
                epilogue(mat, s, Pa, bank, mt, dst, dres)

        def sqstat(src_ap, wtok, seg):
            wsq = K.wr_waits("sqm")
            ts_ = K.emit("act", lambda sc: sc.activation(
                B.sqm[:, :], src_ap, AF.Square, bias=0.0, scale=1.0),
                waits=[wtok] + wsq, inc="act")
            mt = K.mm(PS.pstat[:, :], B.ones_col[:, :], B.sqm[:, :], True, True,
                      waits=[ts_] + K.wr_waits("pstat"), inc="pe")
            K.wr_done("sqm", mt)
            te = K.emit("dve", lambda v, seg=seg: v.tensor_copy(
                B.row_f[:, seg * S:(seg + 1) * S], PS.pstat[:, :]),
                waits=[mt], inc="dve")
            K.wr_done("pstat", te)
            stat_toks.append(te)
            return te

        def rope(src_ap, s, dst, dres, extra=()):
            sl = slice(s * S, (s + 1) * S)
            wsw = K.wr_waits("P2")
            tsw = K.mm(PS.P2[:, :], B.pmat[:, :], src_ap, True, True,
                       waits=list(extra) + wsw + [K.consts_tok], inc="pe")
            t_a = K.emit("dve", lambda v: v.tensor_tensor(
                B.qt2[:, sl], PS.P2[:, :], B.sful[:, sl], ALU.mult),
                waits=[tsw] + K.wr_waits("qt2"), inc="dve")
            K.wr_done("P2", t_a)
            t_b = K.emit("dve", lambda v: v.tensor_tensor(
                dst[:, sl], src_ap, B.cful[:, sl], ALU.mult),
                waits=list(extra) + [K.rope_tok] +
                (K.wr_waits(dres) if s == 0 else []),
                inc="dve")
            t_c = K.emit("dve", lambda v: v.tensor_tensor(
                dst[:, sl], dst[:, sl], B.qt2[:, sl], ALU.add),
                waits=[t_a, t_b], inc="dve")
            K.wr_done("qt2", t_c)
            K.wr_done(dres, t_c, append=(s > 0))
            return t_c

        def epilogue(mat, s, Pa, bank, mt, dst, dres):
            sl = slice(s * S, (s + 1) * S)
            seg = s
            if kind == "a1b":
                tcp = K.emit("act", lambda sc: sc.activation(
                    dst[:, sl], Pa[:, :], AF.Copy, bias=0.0, scale=1.0),
                    waits=[mt] + (K.wr_waits(dres) if s == 0 else []),
                    inc="act")
                K.wr_done(dres, tcp, append=(s > 0))
                if mat != "v":
                    te = sqstat(Pa[:, :], mt, seg)
                    K.rd_done(bank, te)
                K.rd_done(bank, tcp)
            elif kind == "a0a":
                if mat != "v":
                    te = sqstat(Pa[:, :], mt, seg)
                    K.rd_done(bank, te)
                    tcp = K.emit("act", lambda sc: sc.activation(
                        B.qt1[:, sl], Pa[:, :], AF.Copy, bias=0.0, scale=1.0),
                        waits=[mt] + (K.wr_waits("qt1") if s == 0 else []),
                        inc="act")
                    K.rd_done(bank, tcp)
                    K.wr_done("qt1", tcp, append=(s > 0))
                    rope(B.qt1[:, sl], s, dst, dres, extra=(tcp,))
                else:
                    tcp = K.emit("act", lambda sc: sc.activation(
                        dst[:, sl], Pa[:, :], AF.Copy, bias=0.0, scale=1.0),
                        waits=[mt] + (K.wr_waits(dres) if s == 0 else []),
                        inc="act")
                    K.wr_done(dres, tcp, append=(s > 0))
                    K.rd_done(bank, tcp)
            elif kind == "a0b":
                if mat != "v":
                    wsq = K.wr_waits("sqm")
                    ts_ = K.emit("act", lambda sc: sc.activation(
                        B.sqm[:, :], Pa[:, :], AF.Square, bias=0.0, scale=1.0),
                        waits=[mt] + wsq, inc="act")
                    mt2 = K.mm(PS.pstat[:, :], B.ones_col[:, :], B.sqm[:, :],
                               True, True, waits=[ts_] + K.wr_waits("pstat"),
                               inc="pe")
                    K.wr_done("sqm", mt2)
                    te = K.emit("dve", lambda v: v.tensor_copy(
                        B.row_f[:, 4 * S:5 * S], PS.pstat[:, :]),
                        waits=[mt2], inc="dve")
                    K.wr_done("pstat", te)
                    t5_ = K.emit("act", lambda sc: sc.activation(
                        B.row_f[:, 4 * S:5 * S], B.row_f[:, 4 * S:5 * S],
                        AF.Sqrt, bias=B.ccol[0:1, 5:6], scale=B.ccol[0:1, 4:5]),
                        waits=[te], inc="act")
                    t6_ = K.emit("dve", lambda v: v.reciprocal(
                        B.row_f[:, 4 * S:5 * S], B.row_f[:, 4 * S:5 * S]),
                        waits=[t5_], inc="dve")
                    t7_ = K.emit("dve", lambda v: v.tensor_copy(
                        B.row_bf[:, 0:S], B.row_f[:, 4 * S:5 * S]),
                        waits=[t6_], inc="dve")
                    tb_ = K.mm(PS.P3[:, :], B.ones_row[:, :], B.row_bf[:, 0:S],
                               True, True, waits=[t7_] + K.wr_waits("P3"),
                               inc="pe")
                    tbc = K.emit("act", lambda sc: sc.activation(
                        B.bc2[:, :], PS.P3[:, :], AF.Copy, bias=0.0, scale=1.0),
                        waits=[tb_] + K.wr_waits("bc2"), inc="act")
                    K.wr_done("P3", tbc)
                    mi = 0 if mat == "q" else 1
                    tn = K.emit("dve", lambda v, mi=mi: v.scalar_tensor_tensor(
                        B.qt1[:, sl], Pa[:, :], B.nrmw[:, mi:mi + 1],
                        B.bc2[:, :], ALU.mult, ALU.mult),
                        waits=[tnrm, tbc, mt] + (K.wr_waits("qt1") if s == 0 else []),
                        inc="dve")
                    K.wr_done("qt1", tn, append=(s > 0))
                    K.wr_done("bc2", tn)
                    K.rd_done(bank, tn)
                    te2 = sqstat(B.qt1[:, sl], tn, seg)
                    rope(B.qt1[:, sl], s, dst, dres, extra=(tn,))
                else:
                    tcp = K.emit("act", lambda sc: sc.activation(
                        dst[:, sl], Pa[:, :], AF.Copy, bias=0.0, scale=1.0),
                        waits=[mt] + (K.wr_waits(dres) if s == 0 else []),
                        inc="act")
                    K.wr_done(dres, tcp, append=(s > 0))
                    K.rd_done(bank, tcp)
            elif kind == "a1a":
                if mat != "v":
                    tu = K.emit("dve", lambda v: v.tensor_scalar(
                        B.qt1[:, sl], Pa[:, :], B.nrmw[:, 0:1], B.nrmw[:, 1:2],
                        ALU.mult, ALU.add),
                        waits=[tnrm, mt] + (K.wr_waits("qt1") if s == 0 else []),
                        inc="dve")
                    K.wr_done("qt1", tu, append=(s > 0))
                    K.rd_done(bank, tu)
                    wsq = K.wr_waits("sqm")
                    ts_ = K.emit("act", lambda sc: sc.activation(
                        B.sqm[:, :], B.qt1[:, sl], AF.Square, bias=0.0,
                        scale=1.0), waits=[tu] + wsq, inc="act")
                    mtA = K.mm(PS.pstat[:, :], B.ones_col[:, :], B.sqm[:, :],
                               True, True, waits=[ts_] + K.wr_waits("pstat"),
                               inc="pe")
                    K.wr_done("sqm", mtA)
                    teA = K.emit("dve", lambda v: v.tensor_copy(
                        B.row_f[:, 4 * S:5 * S], PS.pstat[:, :]),
                        waits=[mtA], inc="dve")
                    mtB = K.mm(PS.pstat[:, :], B.ones_col[:, :], B.qt1[:, sl],
                               True, True, waits=[tu, teA], inc="pe")
                    teB = K.emit("dve", lambda v: v.tensor_copy(
                        B.row_f[:, 5 * S:6 * S], PS.pstat[:, :]),
                        waits=[mtB], inc="dve")
                    K.wr_done("pstat", teB)
                    r1 = K.emit("dve", lambda v: v.tensor_single_scalar(
                        B.row_f[:, 5 * S:6 * S], B.row_f[:, 5 * S:6 * S],
                        1.0 / DH, ALU.mult), waits=[teB], inc="dve")
                    r2 = K.emit("dve", lambda v: v.tensor_tensor(
                        B.row_f[:, 0:S], B.row_f[:, 5 * S:6 * S],
                        B.row_f[:, 5 * S:6 * S], ALU.mult),
                        waits=[r1], inc="dve")
                    r3 = K.emit("dve", lambda v: v.scalar_tensor_tensor(
                        B.row_f[:, 0:S], B.row_f[:, 4 * S:5 * S], 1.0 / DH,
                        B.row_f[:, 0:S], ALU.mult, ALU.subtract),
                        waits=[r2, teA], inc="dve")
                    r4 = K.emit("act", lambda sc: sc.activation(
                        B.row_f[:, 0:S], B.row_f[:, 0:S], AF.Sqrt,
                        bias=B.ccol[0:1, 3:4], scale=1.0), waits=[r3], inc="act")
                    r5 = K.emit("dve", lambda v: v.reciprocal(
                        B.row_f[:, 0:S], B.row_f[:, 0:S]), waits=[r4],
                        inc="dve")
                    qscale = SCL if mat == "q" else 1.0
                    r6 = K.emit("dve", lambda v, qs=qscale: v.tensor_single_scalar(
                        B.row_bf[:, 0:S], B.row_f[:, 0:S], qs, ALU.mult),
                        waits=[r5], inc="dve")
                    r7 = K.emit("dve", lambda v: v.tensor_tensor(
                        B.row_f[:, S:2 * S], B.row_f[:, 5 * S:6 * S],
                        B.row_f[:, 0:S], ALU.mult), waits=[r5], inc="dve")
                    qs2 = -qscale
                    r8 = K.emit("dve", lambda v, qs=qs2: v.tensor_single_scalar(
                        B.row_bf[:, S:2 * S], B.row_f[:, S:2 * S], qs,
                        ALU.mult), waits=[r7], inc="dve")
                    tb1 = K.mm(PS.P3[:, :], B.ones_row[:, :], B.row_bf[:, 0:S],
                               True, True, waits=[r6] + K.wr_waits("P3"),
                               inc="pe")
                    tb2 = K.mm(PS.P4[:, :], B.ones_row[:, :],
                               B.row_bf[:, S:2 * S], True, True,
                               waits=[r8] + K.wr_waits("P4"), inc="pe")
                    tbc = K.emit("act", lambda sc: sc.activation(
                        B.bc2[:, :], PS.P3[:, :], AF.Copy, bias=0.0, scale=1.0),
                        waits=[tb1] + K.wr_waits("bc2"), inc="act")
                    K.wr_done("P3", tbc)
                    w1 = K.emit("dve", lambda v: v.tensor_tensor(
                        dst[:, sl], B.qt1[:, sl], B.bc2[:, :], ALU.mult),
                        waits=[tbc, tu] + (K.wr_waits(dres) if s == 0 else []),
                        inc="dve")
                    K.wr_done("bc2", w1)
                    w2 = K.emit("dve", lambda v: v.tensor_tensor(
                        dst[:, sl], dst[:, sl], PS.P4[:, :], ALU.add),
                        waits=[tb2, w1], inc="dve")
                    K.wr_done("P4", w2)
                    K.wr_done(dres, w2, append=(s > 0))
                else:
                    tcp = K.emit("act", lambda sc: sc.activation(
                        dst[:, sl], Pa[:, :], AF.Copy, bias=0.0, scale=1.0),
                        waits=[mt] + (K.wr_waits(dres) if s == 0 else []),
                        inc="act")
                    K.wr_done(dres, tcp, append=(s > 0))
                    K.rd_done(bank, tcp)

        proj("k", tkw, wslot(1, par), res_k, B.kh, "kh")
        if use_bound:
            tm1 = K.emit("dve", lambda v: v.reduce_max(
                B.col_f[0:1, 28:29], B.row_f[:, 0:4 * S], mybir.AxisListType.X),
                inc="dve")
            tm2 = K.emit("dve", lambda v: v.tensor_single_scalar(
                B.col_f[0:1, 29:30], B.col_f[0:1, 28:29], 1.0 / DH, ALU.mult),
                waits=[tm1], inc="dve")
            tm3 = K.emit("dve", lambda v: v.tensor_copy(
                B.row_bf[0:1, S:S + 1], B.col_f[0:1, 29:30]),
                waits=[tm2], inc="dve")
            tb_ = K.mm(PS.P3[:, 0:1], B.ones_row[:, :], B.row_bf[0:1, S:S + 1],
                       True, True, waits=[tm3] + K.wr_waits("P3"), inc="pe")
            tk2 = K.emit("act", lambda sc: sc.activation(
                B.col_f[:, 30:31], PS.P3[:, 0:1], AF.Copy, bias=0.0,
                scale=1.0), waits=[tb_], inc="act")
            K.wr_done("P3", tk2)
        proj("q", tqw, wslot(0, par), res_q, B.qh, "qh")
        if use_bound:
            wtr = K.wr_waits("P3")
            ttr = None
            for i in range(NT_TILES):
                ttr = K.mm(PS.P3[:, i:i + 1],
                           B.row_f[0:1, i * 128:(i + 1) * 128],
                           B.identf[0:1, 0:1], True, True,
                           waits=(wtr + [stat_toks[-1], K.consts_tok])
                           if i == 0 else [],
                           inc="pe" if i == NT_TILES - 1 else None)
            tcv = K.emit("act", lambda sc: sc.activation(
                B.col_f[:, 0:NT_TILES], PS.P3[:, 0:NT_TILES], AF.Copy,
                bias=0.0, scale=1.0), waits=[ttr], inc="act")
            K.wr_done("P3", tcv)
            tnb1 = K.emit("act", lambda sc: sc.activation(
                B.col_f[:, 0:NT_TILES], B.col_f[:, 0:NT_TILES], AF.Sqrt,
                bias=0.0, scale=B.col_f[:, 30:31]),
                waits=[tcv, tk2], inc="act")
            tnb = K.emit("dve", lambda v: v.tensor_single_scalar(
                B.col_f[:, 0:NT_TILES], B.col_f[:, 0:NT_TILES], -1.0,
                ALU.mult), waits=[tnb1], inc="dve")
        else:
            tnb = None
        proj("v", tvw, wslot(2, par), res_v, B.vt, "vt")
        vtk = None
        for g4 in range(4):
            ptr = PS.ptrA if g4 % 2 == 0 else PS.ptrB
            pres = "ptrA" if g4 % 2 == 0 else "ptrB"
            wp = K.wr_waits(pres)
            tt_ = None
            for k4 in range(4):
                tt = g4 * 4 + k4
                tt_ = K.tr(ptr[:, k4 * 128:(k4 + 1) * 128],
                           B.vt[:, tt * 128:(tt + 1) * 128], B.ident[:, :],
                           waits=(wp + [K.consts_tok] +
                                  K.rd_waits("vt") if k4 == 0 else []),
                           inc="pe" if k4 == 3 else None)
            vtk = K.emit("dve", lambda v, g4=g4, ptr=ptr: v.tensor_copy(
                B.vsb[:, g4 * S:(g4 + 1) * S], ptr[:, :]),
                waits=[tt_] + (K.wr_waits("vsb") if g4 == 0 else []),
                inc="dve")
            K.wr_done(pres, vtk)
            K.wr_done("vsb", vtk, append=(g4 > 0))

        if getattr(K, "proj_only", False):
            continue
        def emit_ptpv(st):
            i, jlo, nj, tp = st
            ptk = []
            for g4 in range((nj + 3) // 4):
                ptr = PS.ptrA if g4 % 2 == 0 else PS.ptrB
                pres = "ptrA" if g4 % 2 == 0 else "ptrB"
                wp = K.wr_waits(pres)
                n4 = min(4, nj - g4 * 4)
                tt_ = None
                for k4 in range(n4):
                    jj = g4 * 4 + k4
                    tt_ = K.tr(ptr[:, k4 * 128:(k4 + 1) * 128],
                               B.P[:, jj * 128:(jj + 1) * 128], B.ident[:, :],
                               waits=[tp] + (wp if k4 == 0 else []),
                               inc="pe" if k4 == n4 - 1 else None)
                tev = K.emit("dve", lambda v, g4=g4, n4=n4, ptr=ptr:
                             v.tensor_copy(
                    B.PT[:, g4 * S:g4 * S + n4 * 128], ptr[:, 0:n4 * 128]),
                    waits=[tt_] + (K.wr_waits("slot1PT") if g4 == 0 else []),
                    inc="dve")
                K.wr_done(pres, tev)
                K.wr_done("slot1PT", tev, append=(g4 > 0))
                ptk.append(tev)
            w4 = K.wr_waits("P4")
            mo = None
            for jj in range(nj):
                j = jlo + jj
                mo = K.mm(PS.P4[:, 0:128], B.vsb[:, j * 128:(j + 1) * 128],
                          B.PT[:, jj * 128:(jj + 1) * 128],
                          start=(jj == 0), stop=(jj == nj - 1),
                          waits=[ptk[jj // 4], vtk] + (w4 if jj == 0 else []),
                          inc="pe" if jj == nj - 1 else None)
            K.rd_done("slot1PT", mo)
            K.rd_done("vsb", mo)
            to = K.emit("act", lambda sc, i=i: sc.activation(
                B.osb[:, i * 128:(i + 1) * 128], PS.P4[:, 0:128], AF.Copy,
                bias=0.0, scale=1.0),
                waits=[mo] + (K.wr_waits("osb") if i == 0 else []), inc="act")
            K.wr_done("P4", to)
            K.wr_done("osb", to, append=(i > 0))

        prev_tile = None
        for i in range(NT_TILES):
            jlo = max(0, i - WTILES) if windowed else 0
            nj = i - jlo + 1
            ncols = nj * 128
            nsp = (ncols + S - 1) // S
            par2 = i % 2
            banks = ["P0", "P1"] if par2 == 0 else ["P2", "P3"]
            dp0 = 16 if par2 == 0 else 22
            dnc, drc = (20, 21) if par2 == 0 else (26, 27)
            dloc = (i - jlo) * 128
            sd, off = dloc // S, dloc % S
            etoks = []
            wE = K.wr_waits("slot1E") if kind != "a1a" else []
            wP = K.wr_waits("slot1P")
            first_grp_toks = None
            for sp_i in range(nsp):
                lc = sp_i * S
                w = min(S, ncols - lc)
                bname = banks[sp_i % 2]
                Pa = getattr(PS, bname)
                wb = K.wr_waits(bname)
                extra = []
                if sp_i >= 2:
                    extra = [etoks[sp_i - 2]]
                mt = K.mm(Pa[:, 0:w], B.qh[:, i * 128:(i + 1) * 128],
                          B.kh[:, jlo * 128 + lc: jlo * 128 + lc + w],
                          True, True,
                          waits=wb + extra + (K.rd_waits("qh") +
                                              K.rd_waits("kh")
                                              if sp_i == 0 else []), inc="pe")
                mtk = mt
                if sp_i == sd:
                    if kind == "a1a":
                        mtk = K.emit("dve", lambda v, Pa=Pa, off=off:
                                     v.tensor_tensor(
                            Pa[:, off:off + 128], Pa[:, off:off + 128],
                            B.m_tril01[:, :], ALU.mult),
                            waits=[mt, K.consts_tok], inc="dve")
                    else:
                        mtk = K.emit("dve", lambda v, Pa=Pa, off=off:
                                     v.tensor_tensor(
                            Pa[:, off:off + 128], Pa[:, off:off + 128],
                            B.m_tril_neg[:, :], ALU.add),
                            waits=[mt, K.consts_tok], inc="dve")
                if sp_i == 0 and windowed and i >= WTILES:
                    mtk = K.emit("dve", lambda v, Pa=Pa: v.tensor_tensor(
                        Pa[:, 0:128], Pa[:, 0:128],
                        B.m_triu_neg[:, :], ALU.add),
                        waits=[mt, mtk, K.consts_tok], inc="dve")
                if kind == "a1a":
                    te = K.emit("act", lambda sc, Pa=Pa, lc=lc, w=w, sp_i=sp_i,
                                dp0=dp0: sc.activation(
                        B.P[:, lc:lc + w], Pa[:, 0:w], AF.Square,
                        bias=0.0, scale=1.0,
                        accum_out=B.col_f[:, dp0 + sp_i:dp0 + sp_i + 1]),
                        waits=[mtk] + wP, inc="act")
                else:
                    te = K.emit("act", lambda sc, Pa=Pa, lc=lc, w=w, sp_i=sp_i,
                                dp0=dp0, i=i: sc.activation(
                        B.E[:, lc:lc + w], Pa[:, 0:w], AF.Exp,
                        bias=B.col_f[:, i:i + 1], scale=B.ccol[:, 0:1],
                        accum_out=B.col_f[:, dp0 + sp_i:dp0 + sp_i + 1]),
                        waits=[mtk, tnb, K.consts_tok] + wE, inc="act")
                etoks.append(te)
                K.wr_done(bname, te)
                if sp_i == 1 and prev_tile is not None:
                    emit_ptpv(prev_tile)
                    prev_tile = None
            K.rd_done("qh", mt)
            K.rd_done("kh", mt)
            if prev_tile is not None:
                emit_ptpv(prev_tile)
                prev_tile = None
            td1 = K.emit("dve", lambda v, nsp=nsp, dp0=dp0, dnc=dnc:
                         v.reduce_sum(
                B.col_f[:, dnc:dnc + 1], B.col_f[:, dp0:dp0 + nsp],
                mybir.AxisListType.X), waits=etoks, inc="dve")
            epsv = 1e-5 if kind == "a1a" else 1e-30
            td2 = K.emit("dve", lambda v, e=epsv, dnc=dnc:
                         v.tensor_single_scalar(
                B.col_f[:, dnc:dnc + 1], B.col_f[:, dnc:dnc + 1], e, ALU.add),
                waits=[td1], inc="dve")
            td3 = K.emit("dve", lambda v, dnc=dnc, drc=drc: v.reciprocal(
                B.col_f[:, drc:drc + 1], B.col_f[:, dnc:dnc + 1]),
                waits=[td2], inc="dve")
            if kind == "a1a":
                tp = K.emit("dve", lambda v, nc_=ncols, drc=drc:
                            v.tensor_single_scalar(
                    B.P[:, 0:nc_], B.P[:, 0:nc_], B.col_f[:, drc:drc + 1],
                    ALU.mult), waits=[td3] + etoks + wP, inc="dve")
            else:
                tp = K.emit("dve", lambda v, nc_=ncols, drc=drc:
                            v.tensor_single_scalar(
                    B.P[:, 0:nc_], B.E[:, 0:nc_], B.col_f[:, drc:drc + 1],
                    ALU.mult), waits=[td3] + etoks + wP, inc="dve")
                for te in etoks:
                    K.wr_done("slot1E", tp, append=True)
            K.wr_done("slot1P", tp)
            prev_tile = (i, jlo, nj, tp)
        emit_ptpv(prev_tile)
        prev_tile = None

        if getattr(K, "dbg", False) and h == 0:
            K.tap("qh", B.qh[:, :], K.rd_waits("qh"))
            K.tap("kh", B.kh[:, :], K.rd_waits("kh"))
            K.tap("vsb", B.vsb[:, :], K.rd_waits("vsb"))
            K.tap("rowf", B.row_f[:, :], [self_last_dve(K)])
            K.tap("colf", B.col_f[:, :], [self_last_dve(K), self_last_act(K)])
            K.tap("E", B.E[:, :], [self_last_act(K)])
            K.tap("P", B.P[:, :], [self_last_dve(K)])
            K.tap("osb", B.osb[:, :], K.rd_waits("osb"))
            K.tap("xln", B.x_lnT[:, :], K.rd_waits("big0"))
            K.tap("cful", B.cful[:, :], [self_last_act(K)])
            K.tap("sful", B.sful[:, :], [self_last_dve(K)])
        tso = K.dma("sp", "xd", ost[h], B.osb[:, :],
                    waits=K.rd_waits("osb"))
        K.rd_done("osb", tso)
        otok_prev[h] = tso

    if getattr(K, "proj_only", False):
        tst = K.dma("sp", "xd", xdst[3], B.x_blk[:, :],
                    waits=[self_last_dve(K), self_last_act(K)])
        K.rd_done("x_blk", tst)
        return tst

    for ts in range(NB):
        tosl = K.dma("sp", "xd", B.osl[:, :].rearrange("p (h u) -> p h u", h=H),
                     ost[:, :, ts * S:(ts + 1) * S].rearrange("h p u -> p h u"),
                     waits=otok_prev + K.wr_waits("slot0"))
        tox = K.dma("sp", "xd", B.x_blk[:, :], xsrc[ts],
                    waits=K.wr_waits("x_blk"))
        rtoks = []
        for dc in range(DC):
            wslot_o = B.wring[:, (dc % 2) * DC * 128:(dc % 2 + 1) * DC * 128]
            res_o = f"wro{dc % 2}"
            tww = K.dma("pool", "wd", wslot_o, Wap["wo"][dc],
                        waits=K.wr_waits(res_o))
            Pa = PS.P0 if dc % 2 == 0 else PS.P1
            bank = "P0" if dc % 2 == 0 else "P1"
            wb = K.wr_waits(bank)
            mt = None
            for c in range(DC):
                mt = K.mm(Pa[:, :], wslot_o[:, c * 128:(c + 1) * 128],
                          B.osl[:, c * S:(c + 1) * S],
                          start=(c == 0), stop=(c == DC - 1),
                          waits=[tww, tosl] + (wb if c == 0 else []),
                          inc="pe" if c == DC - 1 else None)
            K.rd_done(res_o, mt)
            K.rd_done("slot0", mt)
            tr_ = K.emit("dve", lambda v, dc=dc, Pa=Pa: v.tensor_tensor(
                B.x_blk[:, dc * S:(dc + 1) * S],
                B.x_blk[:, dc * S:(dc + 1) * S], Pa[:, :], ALU.add),
                waits=[mt, tox], inc="dve")
            K.wr_done(bank, tr_)
            rtoks.append(tr_)
        tst = K.dma("sp", "xd", xdst[ts], B.x_blk[:, :], waits=rtoks)
        K.rd_done("x_blk", tst)
        for r in rtoks:
            K.rd_done("x_blk", r)
    return tst


NT_TILES = 16
WTILES = 8


def self_last_dve(K):
    return ("dve", K.cnt.get("dve", 0))


def self_last_act(K):
    return ("act", K.cnt.get("act", 0))


def emit_ffn(K, B, PS, kind, Wap, xsrc, xdst):
    is0 = (kind == "ffn0")
    tsg = K.dma("sp", "xd", B.sgc[:, :], Wap["sg"], waits=K.wr_waits("sgc"))
    tsu = None
    if is0:
        tsu = K.dma("sp", "xd", B.suc[:, :], Wap["su"], waits=K.wr_waits("suc"))
    tsd = K.dma("sp", "xd", B.sdc[:, :], Wap["sd"], waits=K.wr_waits("sdc"))

    tst = None
    for b in range(NB):
        tok_x = K.dma("sp", "xd", B.x_blk[:, :], xsrc[b],
                      waits=K.wr_waits("x_blk"))
        dst = [B.x_lnB[:, c * S:(c + 1) * S] for c in range(DC)]
        ln_toks = emit_ln_block(K, B, PS, tok_x, dst, "slot0")
        for tk_ in ln_toks:
            K.wr_done("slot0", tk_, append=True)

        wfuse = K.wr_waits("fuse")
        ftoks = []
        for ic in range(IC):
            par = ic % 2
            gsl = B.wgu[:, (0 + par) * DC * 128:(1 + par) * DC * 128]
            usl = B.wgu[:, (2 + par) * DC * 128:(3 + par) * DC * 128]
            gres, ures = f"wg{par}", f"wu{par}"
            tgw = K.dma("pool", "wd", gsl, Wap["g_blob"][ic],
                        waits=K.wr_waits(gres))
            tuw = None
            if is0:
                tuw = K.dma("pool", "wd", usl, Wap["u_blob"][ic],
                            waits=K.wr_waits(ures))
            Pg = PS.P0 if par == 0 else PS.P2
            gbank = "P0" if par == 0 else "P2"
            Pu = PS.P1 if par == 0 else PS.P3
            ubank = "P1" if par == 0 else "P3"
            wgb = K.wr_waits(gbank)
            mg = None
            for c in range(DC):
                mg = K.mm(Pg[:, :], gsl[:, c * 128:(c + 1) * 128], dst[c],
                          start=(c == 0), stop=(c == DC - 1),
                          waits=[tgw, ln_toks[c]] + (wgb if c == 0 else []),
                          inc="pe" if c == DC - 1 else None)
            K.rd_done(gres, mg)
            K.rd_done("slot0", mg)
            if is0:
                wub = K.wr_waits(ubank)
                mu = None
                for c in range(DC):
                    mu = K.mm(Pu[:, :], usl[:, c * 128:(c + 1) * 128], dst[c],
                              start=(c == 0), stop=(c == DC - 1),
                              waits=[tuw, ln_toks[c]] + (wub if c == 0 else []),
                              inc="pe" if c == DC - 1 else None)
                K.rd_done(ures, mu)
                K.rd_done("slot0", mu)
                stg = B.sqm if par == 0 else B.bc2
                sres = "sqm" if par == 0 else "bc2"
                tsl = K.emit("act", lambda sc, Pg=Pg, stg=stg, ic=ic:
                             sc.activation(
                    stg[:, :], Pg[:, :], AF.Silu, bias=0.0,
                    scale=B.sgc[:, ic:ic + 1]),
                    waits=[mg, tsg] + K.wr_waits(sres), inc="act")
                K.rd_done(gbank, tsl)
                tf = K.emit("dve", lambda v, Pu=Pu, stg=stg, ic=ic:
                            v.scalar_tensor_tensor(
                    B.fuse[:, ic * S:(ic + 1) * S], Pu[:, :],
                    B.suc[:, ic:ic + 1], stg[:, :], ALU.mult, ALU.mult),
                    waits=[mu, tsu, tsl] + (wfuse if ic == 0 else []),
                    inc="dve")
                K.rd_done(ubank, tf)
                K.wr_done(sres, tf)
            else:
                tf = K.emit("act", lambda sc, Pg=Pg, ic=ic: sc.activation(
                    B.fuse[:, ic * S:(ic + 1) * S], Pg[:, :], AF.Gelu,
                    bias=B.sgc[:, ic:ic + 1], scale=1.0),
                    waits=[mg, tsg] + (wfuse if ic == 0 else []), inc="act")
                K.rd_done(gbank, tf)
            ftoks.append(tf)
            K.wr_done("fuse", tf, append=(ic > 0))

        rtoks = []
        for oc in range(DC):
            par = oc % 2
            dsl = B.wdn[:, par * IC * 128:(par + 1) * IC * 128]
            dres = f"wd{par}"
            tdw = K.dma("pool", "wd", dsl, Wap["d_blob"][oc],
                        waits=K.wr_waits(dres))
            Pd = PS.P0 if par == 0 else PS.P1
            dbank = "P0" if par == 0 else "P1"
            wdb = K.wr_waits(dbank)
            if getattr(K, "dbg", False) and oc == DC - 1 and b == 3 and                     not getattr(K, "_dtapped", 0):
                K._dtapped = 1
                K.tap("fuse_end", B.fuse[:, 48 * S:64 * S], [tdw] + wdb)
                K.tap("wd15", dsl, [tdw])
            md = None
            for c in range(IC):
                md = K.mm(Pd[:, :], dsl[:, c * 128:(c + 1) * 128],
                          B.fuse[:, c * S:(c + 1) * S],
                          start=(c == 0), stop=(c == IC - 1),
                          waits=[tdw, ftoks[c]] + (wdb if c == 0 else []),
                          inc="pe" if c == IC - 1 else None)
            K.rd_done(dres, md)
            K.rd_done("fuse", md)
            if is0:
                tr_ = K.emit("dve", lambda v, oc=oc, Pd=Pd:
                             v.scalar_tensor_tensor(
                    B.x_blk[:, oc * S:(oc + 1) * S], Pd[:, :],
                    B.sdc[:, oc:oc + 1], B.x_blk[:, oc * S:(oc + 1) * S],
                    ALU.mult, ALU.add), waits=[md, tsd, tok_x], inc="dve")
            else:
                tr_ = K.emit("dve", lambda v, oc=oc, Pd=Pd:
                             v.scalar_tensor_tensor(
                    B.x_blk[:, oc * S:(oc + 1) * S], Pd[:, :],
                    B.sdc[:, oc:oc + 1], B.x_blk[:, oc * S:(oc + 1) * S],
                    ALU.add, ALU.add), waits=[md, tsd, tok_x], inc="dve")
            K.wr_done(dbank, tr_)
            rtoks.append(tr_)
        if getattr(K, "dbg", False) and b == 3 and not getattr(K, "_ftapped", 0):
            K._ftapped = 1
            K.tap("fxblk3", B.x_blk[:, :], rtoks)
        tst = K.dma("sp", "xd", xdst[b], B.x_blk[:, :], waits=rtoks)
        K.rd_done("x_blk", tst)
        for r in rtoks:
            K.rd_done("x_blk", r)
    return tst


SUBLAYERS = [
    ("a0a", "L0aa"), ("ffn0", "L0f"), ("a0b", "L0ab"), ("ffn0", "L0f"),
    ("a1a", "L1aa"), ("ffn1", "L1f"), ("a1b", "L1ab"), ("ffn1", "L1f"),
]


def build_program(sub_idx_list):
    K = Prog()
    nc = K.nc
    B = alloc_sbuf(nc)
    PS = alloc_psum(nc)
    need_rope = any(SUBLAYERS[i][0] in ("a0a", "a0b") for i in sub_idx_list)
    need_attn = any(SUBLAYERS[i][0].startswith("a") for i in sub_idx_list)

    xin = nc.dram_tensor("xin", [NB, 128, DC * S], F32, kind="ExternalInput").ap()
    xout = nc.dram_tensor("xout", [NB, 128, DC * S], F32,
                          kind="ExternalOutput").ap()
    nsub = len(sub_idx_list)
    xst = xst2 = None
    if nsub > 1:
        xst = nc.dram_tensor("xst", [NB, 128, DC * S], F32, kind="Internal").ap()
    if nsub > 2:
        xst2 = nc.dram_tensor("xst2", [NB, 128, DC * S], F32,
                              kind="Internal").ap()
    ost = None
    if need_attn:
        ost = nc.dram_tensor("ost", [H, 128, T], BF16, kind="Internal").ap()

    Waps = {}
    for i in sub_idx_list:
        kind, pfx = SUBLAYERS[i]
        if pfx in Waps:
            continue
        W = {}
        if kind.startswith("a"):
            for nm in ("wq", "wk", "wv", "wo"):
                W[nm] = nc.dram_tensor(f"{pfx}_{nm}", [H, 128, DC * 128], F32,
                                       kind="ExternalInput").ap()
            if kind in ("a0b", "a1a"):
                W["nrm"] = nc.dram_tensor(f"{pfx}_nrm", [128, 2], F32,
                                          kind="ExternalInput").ap()
            W["ost"] = ost
        elif kind == "ffn0":
            W["g_blob"] = nc.dram_tensor(f"{pfx}_g", [IC, 128, DC * 128], I8,
                                         kind="ExternalInput").ap()
            W["u_blob"] = nc.dram_tensor(f"{pfx}_u", [IC, 128, DC * 128], I8,
                                         kind="ExternalInput").ap()
            W["d_blob"] = nc.dram_tensor(f"{pfx}_d", [DC, 128, IC * 128], I8,
                                         kind="ExternalInput").ap()
            W["sg"] = nc.dram_tensor(f"{pfx}_sg", [128, IC], F32,
                                     kind="ExternalInput").ap()
            W["su"] = nc.dram_tensor(f"{pfx}_su", [128, IC], F32,
                                     kind="ExternalInput").ap()
            W["sd"] = nc.dram_tensor(f"{pfx}_sd", [128, DC], F32,
                                     kind="ExternalInput").ap()
        else:
            W["g_blob"] = nc.dram_tensor(f"{pfx}_g", [IC, 128, DC * 128], F32,
                                         kind="ExternalInput").ap()
            W["d_blob"] = nc.dram_tensor(f"{pfx}_d", [DC, 128, IC * 128], F32,
                                         kind="ExternalInput").ap()
            W["sg"] = nc.dram_tensor(f"{pfx}_sg", [128, IC], F32,
                                     kind="ExternalInput").ap()
            W["sd"] = nc.dram_tensor(f"{pfx}_sd", [128, DC], F32,
                                     kind="ExternalInput").ap()
        Waps[pfx] = W

    emit_prelude(K, B, PS, need_rope)
    last_tok = None
    for j, i in enumerate(sub_idx_list):
        kind, pfx = SUBLAYERS[i]
        src = xin if j == 0 else (xst if j % 2 == 1 else xst2)
        dst = xout if j == nsub - 1 else (xst if j % 2 == 0 else xst2)
        if kind.startswith("a"):
            last_tok = emit_attention(K, B, PS, kind, Waps[pfx], src, dst)
        else:
            last_tok = emit_ffn(K, B, PS, kind, Waps[pfx], src, dst)
    nc = K.finalize([last_tok])
    return nc


def proj_blob(W):
    din, dout = W.shape
    ndc, noc = din // 128, dout // 128
    return np.ascontiguousarray(
        W.reshape(ndc, 128, noc, 128).transpose(2, 1, 0, 3)
        .reshape(noc, 128, ndc * 128))


def x_to_blocks(xc):
    xT = xc.T
    return np.ascontiguousarray(
        xT.reshape(DC, 128, NB, S).transpose(2, 1, 0, 3)
        .reshape(NB, 128, DC * S))


def x_from_blocks(xb):
    xT = xb.reshape(NB, 128, DC, S).transpose(2, 1, 0, 3).reshape(D, T)
    return np.ascontiguousarray(xT.T)


def colmaj(v):
    return np.ascontiguousarray(v.reshape(-1, 128).T)


def prep_weights(inputs):
    P = {}
    P["L0aa"] = {
        "wq": proj_blob(inputs["L0_attn_a_wq"]),
        "wk": proj_blob(inputs["L0_attn_a_wk"]),
        "wv": proj_blob(inputs["L0_attn_a_wv"]),
        "wo": proj_blob(inputs["L0_attn_a_wo"]),
    }
    P["L0ab"] = {
        "wq": proj_blob(np.ascontiguousarray(inputs["L0_attn_b_Wq"].T)),
        "wk": proj_blob(np.ascontiguousarray(inputs["L0_attn_b_Wk"].T)),
        "wv": proj_blob(np.ascontiguousarray(inputs["L0_attn_b_Wv"].T)),
        "wo": proj_blob(np.ascontiguousarray(inputs["L0_attn_b_Wo"].T)),
        "nrm": np.ascontiguousarray(np.stack(
            [inputs["L0_attn_b_q_norm_w"], inputs["L0_attn_b_k_norm_w"]],
            axis=1)),
    }
    P["L0f"] = {
        "g_blob": proj_blob(inputs["L0_ffn_w_gate"]),
        "u_blob": proj_blob(inputs["L0_ffn_w_up"]),
        "d_blob": proj_blob(inputs["L0_ffn_w_down"]),
        "sg": colmaj(inputs["L0_ffn_s_gate"]),
        "su": colmaj(inputs["L0_ffn_s_up"]),
        "sd": colmaj(inputs["L0_ffn_s_down"]),
    }
    P["L1aa"] = {
        "wq": proj_blob(np.ascontiguousarray(inputs["L1_attn_a_Wq"].T)),
        "wk": proj_blob(np.ascontiguousarray(inputs["L1_attn_a_Wk"].T)),
        "wv": proj_blob(np.ascontiguousarray(inputs["L1_attn_a_Wv"].T)),
        "wo": proj_blob(np.ascontiguousarray(inputs["L1_attn_a_Wo"].T)),
        "nrm": np.ascontiguousarray(np.stack(
            [inputs["L1_attn_a_gamma"], inputs["L1_attn_a_beta"]], axis=1)),
    }
    P["L1f"] = {
        "g_blob": proj_blob(np.ascontiguousarray(inputs["L1_ffn_W0"].T)),
        "d_blob": proj_blob(np.ascontiguousarray(inputs["L1_ffn_W1"].T)),
        "sg": colmaj(inputs["L1_ffn_bias0"]),
        "sd": colmaj(inputs["L1_ffn_bias1"]),
    }
    P["L1ab"] = {
        "wq": proj_blob(np.ascontiguousarray(
            inputs["L1_attn_b_W_Q"].transpose(2, 0, 1).reshape(D, D))),
        "wk": proj_blob(np.ascontiguousarray(
            inputs["L1_attn_b_W_K"].transpose(2, 0, 1).reshape(D, D))),
        "wv": proj_blob(np.ascontiguousarray(
            inputs["L1_attn_b_W_V"].transpose(2, 0, 1).reshape(D, D))),
        "wo": proj_blob(np.ascontiguousarray(
            inputs["L1_attn_b_W_O"].transpose(0, 2, 1).reshape(D, D))),
    }
    return P


def in_map_for(sub_idx_list, P, xblk):
    m = {"xin": xblk}
    seen = set()
    for i in sub_idx_list:
        kind, pfx = SUBLAYERS[i]
        if pfx in seen:
            continue
        seen.add(pfx)
        W = P[pfx]
        if kind.startswith("a"):
            m[f"{pfx}_wq"] = W["wq"]
            m[f"{pfx}_wk"] = W["wk"]
            m[f"{pfx}_wv"] = W["wv"]
            m[f"{pfx}_wo"] = W["wo"]
            if "nrm" in W:
                m[f"{pfx}_nrm"] = W["nrm"]
        else:
            m[f"{pfx}_g"] = W["g_blob"]
            m[f"{pfx}_d"] = W["d_blob"]
            m[f"{pfx}_sg"] = W["sg"]
            m[f"{pfx}_sd"] = W["sd"]
            if "u_blob" in W:
                m[f"{pfx}_u"] = W["u_blob"]
                m[f"{pfx}_su"] = W["su"]
    return m


def run_groups(inputs, groups):
    P = prep_weights(inputs)
    x = np.asarray(inputs["L0_attn_a_hidden_states"], np.float32)
    ncores = x.shape[0]
    xblk = [x_to_blocks(x[c]) for c in range(ncores)]
    for grp in groups:
        nc = build_program(grp)
        in_maps = [in_map_for(grp, P, xblk[c]) for c in range(ncores)]
        res = run_bass_kernel_spmd(nc, in_maps, core_ids=list(range(ncores)))
        xblk = [np.asarray(res.results[c]["xout"], np.float32)
                for c in range(ncores)]
    return np.stack([x_from_blocks(xblk[c]) for c in range(ncores)])


def kernel(**inputs):
    out = run_groups(inputs, [list(range(8))])
    return out.astype(np.float32)


# version 07
# speedup vs baseline: 1.1432x; 1.0587x over previous
import math
import numpy as np
import concourse.bass as bass
import concourse.mybir as mybir
from concourse.bass_utils import run_bass_kernel_spmd

F32 = mybir.dt.float32
BF16 = mybir.dt.bfloat16
I8 = mybir.dt.int8
I32 = mybir.dt.int32
AF = mybir.ActivationFunctionType
ALU = mybir.AluOpType

T = 2048
D = 2048
DC = 16
H = 16
DH = 128
NB = 4
SPAN = 512
IC = 64
SCL = 1.0 / math.sqrt(DH)
PI = math.pi
NEG = -1.0e30
SBUF_LIMIT = 212863
S = SPAN

ENGS = ("pe", "act", "dve", "pool", "sp")

ALIAS = {
    "big0": ["fuse"], "fuse": ["big0"],
    "slot0": [],
    "slot1E": ["wgu"], "slot1P": ["wgu"], "slot1PT": ["wgu"],
    "wgu": ["slot1E", "slot1P", "slot1PT"],
    "qh": ["wdn"], "kh": ["wdn"], "vt": ["wdn"], "vsb": ["wdn"],
    "qt1": ["wdn"], "qt2": ["wdn"], "osb": ["wdn"],
    "wdn": ["qh", "kh", "vt", "vsb", "qt1", "qt2", "osb"],
    "x_blk": ["rowi"], "rowi": ["x_blk"],
    "wr00": ["wro0"], "wro0": ["wr00"],
    "wr01": ["wro1"], "wro1": ["wr01"],
}


class Prog:
    def __init__(self):
        self.nc = bass.Bass("TRN2", target_bir_lowering=False)
        self.ops = {e: [] for e in ENGS}
        self.sems = {}
        self.cnt = {}
        self.waited = {e: {} for e in ENGS}
        self.rsrc = {}

    def _sem(self, name):
        if name not in self.sems:
            self.sems[name] = self.nc.alloc_semaphore(name=name)
            self.cnt[name] = 0
        return self.sems[name]

    def emit(self, eng, fn, waits=(), inc=None, by=1):
        real = []
        for wtok in waits:
            if wtok is None:
                continue
            n, v = wtok
            self._sem(n)
            if n == eng and eng == "pe":
                continue
            if self.waited[eng].get(n, -1) >= v:
                continue
            self.waited[eng][n] = v
            real.append((n, v))
        tok = None
        if inc is not None:
            self._sem(inc)
            self.cnt[inc] += by
            tok = (inc, self.cnt[inc])
        sems = self.sems

        def run(e, fn=fn, real=tuple(real), inc=inc, by=by):
            for n, v in real:
                e.wait_ge(sems[n], v)
            ins = fn(e)
            if inc is not None:
                ins.then_inc(sems[inc], by)
        self.ops[eng].append(run)
        return tok

    def mm(self, out, lhsT, rhs, start, stop, waits=(), inc=None):
        return self.emit(
            "pe", lambda t, o=out, a=lhsT, b=rhs, s1=start, s2=stop:
            t.matmul(o, a, b, start=s1, stop=s2), waits, inc)

    def tr(self, out, in_, ident, waits=(), inc=None):
        return self.emit(
            "pe", lambda t, o=out, i=in_, d=ident: t.transpose(o, i, d),
            waits, inc)

    def dma(self, q, sem, out, in_, waits=()):
        return self.emit(
            q, lambda e, o=out, i=in_: e.dma_start(out=o, in_=i),
            waits, sem, by=16)

    def _grp(self, n):
        return [n] + ALIAS.get(n, [])

    def wr_waits(self, n):
        out = []
        for n2 in self._grp(n):
            r = self.rsrc.get(n2)
            if r:
                out += r[0] + r[1]
                self.rsrc[n2] = ([], [])
        return out

    def wr_done(self, n, tok, append=False):
        r = self.rsrc.setdefault(n, ([], []))
        if append:
            r[0].append(tok)
        else:
            self.rsrc[n] = ([tok], [])

    def rd_waits(self, n):
        return list(self.rsrc.get(n, ([], []))[0])

    def rd_done(self, n, tok):
        self.rsrc.setdefault(n, ([], []))[1].append(tok)

    def tap(self, name, ap, waits):
        shape = [int(c) for _, c in ap.ap]
        dt_ = ap.tensor.dtype
        d = self.nc.dram_tensor(f"dbg_{name}", shape, dt_,
                                kind="ExternalOutput").ap()
        return self.dma("sp", "xd", d, ap, waits=waits)

    def finalize(self, final_waits):
        sems = self.sems
        for wtok in final_waits:
            if wtok is None:
                continue
            n, v = wtok
            self._sem(n)
            if self.waited["sp"].get(n, -1) >= v:
                continue
            self.waited["sp"][n] = v

            def run(e, n=n, v=v):
                e.wait_ge(sems[n], v)
            self.ops["sp"].append(run)
        nc = self.nc
        with nc.Block() as block:
            @block.tensor
            def _(t):
                for f in self.ops["pe"]:
                    f(t)

            @block.scalar
            def _(sc):
                for f in self.ops["act"]:
                    f(sc)

            @block.vector
            def _(v):
                for f in self.ops["dve"]:
                    f(v)

            @block.gpsimd
            def _(g):
                for f in self.ops["pool"]:
                    f(g)

            @block.sync
            def _(s):
                for f in self.ops["sp"]:
                    f(s)
        return nc


class Bufs:
    pass


def alloc_sbuf(nc):
    B = Bufs()
    base = (nc.sbuf_base + 63) // 64 * 64
    cur = [base]

    def at(name, shape, dtype, offset=None):
        bpe = {F32: 4, BF16: 2, I8: 1, I32: 4}[dtype]
        nbytes = int(np.prod(shape[1:])) * bpe
        if offset is None:
            offset = cur[0]
        cur[0] = max(cur[0], offset + ((nbytes + 63) // 64) * 64)
        h = nc.alloc_sbuf_tensor_at(name, shape, dtype, offset=offset)
        setattr(B, name, h.ap())
        return offset

    at("ones_col", [128, 1], BF16)
    at("ones_row", [1, 128], BF16)
    at("ident", [128, 128], BF16)
    at("identf", [128, 128], F32)
    at("pmat", [128, 128], BF16)
    at("m_tril_neg", [128, 128], F32)
    at("m_triu_neg", [128, 128], F32)
    at("m_tril01", [128, 128], F32)
    at("sign_col", [128, 1], F32)
    at("cful", [128, T], BF16)
    at("sful", [128, T], BF16)
    at("row_f", [1, 6 * S], F32)
    at("row_bf", [1, 2 * S], BF16)
    at("col_f", [128, 32], F32)
    at("ccol", [128, 12], F32)
    at("sqm", [128, S], BF16)
    at("bc2", [128, S], BF16)
    at("sgc", [128, IC], F32)
    at("suc", [128, IC], F32)
    at("sdc", [128, DC], F32)
    at("nrmw", [128, 2], F32)

    xblk = at("x_blk", [128, DC * S], F32)
    at("rowi", [1, T], I32, xblk)
    at("tabf", [128, S], F32, xblk + 16384)
    at("ifr_row", [1, 128], F32, xblk + 20480)
    at("tabi", [128, S], I32, xblk + 24576)
    at("tabg", [128, S], F32, xblk + 26624)
    big0 = at("x_lnT", [128, DC * T], BF16)
    at("fuse", [128, IC * S], BF16, big0)
    slot0 = at("osl", [128, DC * S], BF16)
    at("x_lnB", [128, DC * S], BF16, slot0)
    at("stage", [128, DC * S], BF16, slot0)
    slot1 = at("E", [128, T], F32)
    at("P", [128, T], BF16)
    at("PT", [128, T], BF16)
    at("wgu", [128, 4 * DC * 128], BF16, slot1)
    slot2 = at("qh", [128, T], BF16)
    at("kh", [128, T], BF16)
    at("vt", [128, T], BF16)
    at("vsb", [128, T], BF16)
    at("qt1", [128, T], BF16)
    at("qt2", [128, T], BF16)
    at("osb", [128, T], BF16)
    at("wdn", [128, 2 * IC * 128], BF16, slot2)
    at("wring", [128, 5 * DC * 128], BF16)
    assert cur[0] <= nc.sbuf_top, f"SBUF over budget: {cur[0]} > {nc.sbuf_top}"
    B.end = cur[0]
    return B


def alloc_psum(nc):
    PS = Bufs()
    for i in range(5):
        setattr(PS, f"P{i}", nc.alloc_psum_tensor(f"P{i}", [128, S], F32).ap())
    PS.pstat = nc.alloc_psum_tensor("pstat", [1, S], F32).ap()
    PS.ptrA = nc.alloc_psum_tensor("ptrA", [128, S], BF16).ap()
    PS.ptrB = nc.alloc_psum_tensor("ptrB", [128, S], BF16).ap()
    return PS


def emit_prelude(K, B, PS, need_rope):
    K.rope_tok = None
    tk = None

    def pool(fn):
        nonlocal tk
        tk = K.emit("pool", fn, inc="pool")
        return tk

    pool(lambda g: g.memset(B.ones_col[:, :], 1.0))
    pool(lambda g: g.memset(B.ones_row[:, :], 1.0))
    pool(lambda g: g.memset(B.ident[:, :], 1.0))
    pool(lambda g: g.affine_select(
        B.ident[:, :], B.ident[:, :], [[1, 128]], ALU.is_equal, 0.0,
        base=0, channel_multiplier=-1))
    pool(lambda g: g.memset(B.identf[:, :], 1.0))
    pool(lambda g: g.affine_select(
        B.identf[:, :], B.identf[:, :], [[1, 128]], ALU.is_equal, 0.0,
        base=0, channel_multiplier=-1))
    pool(lambda g: g.memset(B.pmat[:, :], 1.0))
    pool(lambda g: g.affine_select(
        B.pmat[:, 0:64], B.pmat[:, 0:64], [[1, 64]], ALU.is_equal, 0.0,
        base=64, channel_multiplier=-1))
    pool(lambda g: g.affine_select(
        B.pmat[:, 64:128], B.pmat[:, 64:128], [[1, 64]], ALU.is_equal, 0.0,
        base=0, channel_multiplier=-1))
    pool(lambda g: g.memset(B.m_tril_neg[:, :], 0.0))
    pool(lambda g: g.affine_select(
        B.m_tril_neg[:, :], B.m_tril_neg[:, :], [[-1, 128]], ALU.is_ge, NEG,
        base=0, channel_multiplier=1))
    pool(lambda g: g.memset(B.m_triu_neg[:, :], 0.0))
    pool(lambda g: g.affine_select(
        B.m_triu_neg[:, :], B.m_triu_neg[:, :], [[1, 128]], ALU.is_ge, NEG,
        base=-1, channel_multiplier=-1))
    pool(lambda g: g.memset(B.m_tril01[:, :], 1.0))
    pool(lambda g: g.affine_select(
        B.m_tril01[:, :], B.m_tril01[:, :], [[-1, 128]], ALU.is_ge, 0.0,
        base=0, channel_multiplier=1))
    pool(lambda g: g.memset(B.ccol[:, 0:1], SCL))
    pool(lambda g: g.memset(B.ccol[:, 1:2], -PI))
    pool(lambda g: g.memset(B.ccol[:, 2:3], -2.0 * math.log(10000.0) / DH))
    pool(lambda g: g.memset(B.ccol[:, 3:4], 1e-5))
    pool(lambda g: g.memset(B.ccol[:, 4:5], 1.0 / DH))
    pool(lambda g: g.memset(B.ccol[:, 5:6], 1e-6))
    pool(lambda g: g.memset(B.ccol[:, 6:7], 2.0 * PI))
    pool(lambda g: g.memset(B.ccol[:, 8:9], -0.5))
    pool(lambda g: g.memset(B.sign_col[:, :], 1.0))
    tsign = pool(lambda g: g.memset(B.sign_col[0:64, :], -1.0))
    K.consts_tok = tsign

    if need_rope:
        tp = pool(lambda g: g.iota(B.rowi[:, :], [[1, T]], base=0,
                                   channel_multiplier=0))
        tpos = K.emit("dve", lambda v: v.tensor_copy(
            B.row_f[:, 0:T], B.rowi[:, :]), waits=[tp], inc="dve")
        K.rd_done("rowi", tpos)
        tpc = pool(lambda g: g.iota(B.tabi[:, 0:1], [[0, 1]], base=0,
                                    channel_multiplier=1))
        tc1 = K.emit("dve", lambda v: v.tensor_copy(
            B.ccol[:, 7:8], B.tabi[:, 0:1]), waits=[tpc, K.consts_tok],
            inc="dve")
        tc2 = K.emit("dve", lambda v: v.tensor_single_scalar(
            B.ccol[64:128, 7:8], B.ccol[64:128, 7:8], -64.0, ALU.add),
            waits=[tc1], inc="dve")
        tc3 = K.emit("act", lambda sc: sc.activation(
            B.ccol[:, 7:8], B.ccol[:, 7:8], AF.Exp,
            bias=0.0, scale=B.ccol[:, 2:3]), waits=[tc2], inc="act")
        inv2pi = 1.0 / (2.0 * PI)
        prev_td = None
        for s in range(NB):
            sl = slice(s * S, (s + 1) * S)
            tpi = K.emit("pool", lambda g, s=s: g.iota(
                B.tabi[:, :], [[1, S]], base=s * S, channel_multiplier=0),
                waits=[tc1, prev_td], inc="pool")
            tg1 = K.emit("dve", lambda v: v.tensor_copy(
                B.tabg[:, :], B.tabi[:, :]), waits=[tpi], inc="dve")
            tang = K.emit("dve", lambda v: v.tensor_single_scalar(
                B.tabg[:, :], B.tabg[:, :], B.ccol[:, 7:8], ALU.mult),
                waits=[tg1, tc3], inc="dve")

            def redsin(dstap, shift, wprev):
                t1 = K.emit("dve", lambda v, shift=shift: v.tensor_scalar(
                    B.tabf[:, :], B.tabg[:, :], shift, inv2pi,
                    ALU.add, ALU.mult), waits=[tang, wprev], inc="dve")
                t2 = K.emit("dve", lambda v: v.tensor_copy(
                    B.tabi[:, :], B.tabf[:, :]), waits=[t1], inc="dve")
                t3 = K.emit("dve", lambda v: v.tensor_tensor(
                    B.tabf[:, :], B.tabf[:, :], B.tabi[:, :], ALU.subtract),
                    waits=[t2], inc="dve")
                t4 = K.emit("act", lambda sc, dstap=dstap: sc.activation(
                    dstap, B.tabf[:, :], AF.Sin, bias=0.0,
                    scale=B.ccol[:, 6:7]), waits=[t3], inc="act")
                return t4
            tb = redsin(B.sful[:, sl], 0.0, None)
            td = redsin(B.cful[:, sl], 0.5 * PI, tb)
            prev_td = td
            tsg_ = K.emit("dve", lambda v, sl=sl: v.tensor_single_scalar(
                B.sful[:, sl], B.sful[:, sl], B.sign_col[:, :], ALU.mult),
                waits=[tb, tsign], inc="dve")
            K.rope_tok = td
            K.rd_done("x_blk", td)
            K.rd_done("x_blk", tsg_)


def emit_ln_block(K, B, PS, tok_x, dst_slices, dst_res):
    w0 = K.wr_waits("slot0")
    t_cp = K.emit("act", lambda sc: sc.activation(
        B.stage[:, :], B.x_blk[:, :], AF.Copy, bias=0.0, scale=1.0),
        waits=[tok_x] + w0, inc="act")
    K.rd_done("x_blk", t_cp)
    mt = None
    wps = K.wr_waits("pstat")
    for c in range(DC):
        mt = K.mm(PS.pstat[:, :], B.ones_col[:, :],
                  B.stage[:, c * S:(c + 1) * S],
                  start=(c == 0), stop=(c == DC - 1),
                  waits=([t_cp, K.consts_tok] + wps) if c == 0 else [t_cp],
                  inc="pe" if c == DC - 1 else None)
    tB = K.emit("dve", lambda v: v.tensor_copy(
        B.row_f[:, S:2 * S], PS.pstat[:, :]), waits=[mt], inc="dve")
    t_sq1 = K.emit("act", lambda sc: sc.activation(
        B.stage[:, :], B.x_blk[:, :], AF.Square, bias=0.0, scale=1.0),
        waits=[mt], inc="act")
    K.rd_done("x_blk", t_sq1)
    for c in range(DC):
        mt = K.mm(PS.pstat[:, :], B.ones_col[:, :],
                  B.stage[:, c * S:(c + 1) * S],
                  start=(c == 0), stop=(c == DC - 1),
                  waits=[t_sq1, tB], inc="pe" if c == DC - 1 else None)
    K.wr_done("slot0", mt)
    tA = K.emit("dve", lambda v: v.tensor_copy(B.row_f[:, 0:S], PS.pstat[:, :]),
                waits=[mt], inc="dve")
    K.wr_done("pstat", tA)
    t2 = K.emit("dve", lambda v: v.tensor_single_scalar(
        B.row_f[:, 2 * S:3 * S], B.row_f[:, S:2 * S], 1.0 / D, ALU.mult),
        waits=[tB], inc="dve")
    t3 = K.emit("dve", lambda v: v.tensor_tensor(
        B.row_f[:, 3 * S:4 * S], B.row_f[:, 2 * S:3 * S],
        B.row_f[:, 2 * S:3 * S], ALU.mult), waits=[t2], inc="dve")
    t4 = K.emit("dve", lambda v: v.scalar_tensor_tensor(
        B.row_f[:, 3 * S:4 * S], B.row_f[:, 0:S], 1.0 / D,
        B.row_f[:, 3 * S:4 * S], ALU.mult, ALU.subtract), waits=[t3, tA], inc="dve")
    t5 = K.emit("act", lambda sc: sc.activation(
        B.row_f[:, 0:S], B.row_f[:, 3 * S:4 * S], AF.Ln,
        bias=B.ccol[0:1, 3:4], scale=1.0), waits=[t4], inc="act")
    t6 = K.emit("act", lambda sc: sc.activation(
        B.row_f[:, 0:S], B.row_f[:, 0:S], AF.Exp,
        bias=0.0, scale=B.ccol[0:1, 8:9]), waits=[t5], inc="act")
    t7 = K.emit("dve", lambda v: v.tensor_copy(B.row_bf[:, 0:S],
                                               B.row_f[:, 0:S]),
                waits=[t6], inc="dve")
    t8 = K.emit("dve", lambda v: v.tensor_tensor(
        B.row_f[:, S:2 * S], B.row_f[:, 2 * S:3 * S], B.row_f[:, 0:S],
        ALU.mult), waits=[t6], inc="dve")
    t9 = K.emit("dve", lambda v: v.tensor_single_scalar(
        B.row_bf[:, S:2 * S], B.row_f[:, S:2 * S], -1.0, ALU.mult),
        waits=[t8], inc="dve")
    tb1 = K.mm(PS.P3[:, :], B.ones_row[:, :], B.row_bf[:, 0:S], True, True,
               waits=[t7] + K.wr_waits("P3"), inc="pe")
    tb2 = K.mm(PS.P4[:, :], B.ones_row[:, :], B.row_bf[:, S:2 * S], True, True,
               waits=[t9] + K.wr_waits("P4"), inc="pe")
    out = []
    wdst = K.wr_waits(dst_res) if dst_res else []
    for c in range(DC):
        dsl = dst_slices[c]
        ta = K.emit("dve", lambda v, c=c, dsl=dsl: v.tensor_tensor(
            dsl, B.x_blk[:, c * S:(c + 1) * S], PS.P3[:, :], ALU.mult),
            waits=[tb1] + (wdst if c == 0 else []), inc="dve")
        tbb = K.emit("dve", lambda v, dsl=dsl: v.tensor_tensor(
            dsl, dsl, PS.P4[:, :], ALU.add), waits=[tb2, ta], inc="dve")
        K.rd_done("x_blk", tbb)
        out.append(tbb)
    K.wr_done("P3", out[-1])
    K.wr_done("P4", out[-1])
    return out


def emit_attention(K, B, PS, kind, Wap, xsrc, xdst):
    nc = K.nc
    windowed = kind in ("a0a", "a0b")
    use_rope = kind in ("a0a", "a0b")
    use_bound = kind in ("a0a", "a0b", "a1b")
    ost = Wap["ost"]

    tnrm = None
    if kind in ("a0b", "a1a"):
        tnrm = K.dma("sp", "xd", B.nrmw[:, :], Wap["nrm"],
                     waits=K.wr_waits("nrmw"))

    ln_toks = [[None] * DC for _ in range(NB)]
    wbig = K.wr_waits("big0")
    for b in range(NB):
        tok_x = K.dma("sp", "xd", B.x_blk[:, :], xsrc[b],
                      waits=K.wr_waits("x_blk") + (wbig if b == 0 else []))
        if getattr(K, "dbg", False) and b == 3 and not getattr(K, "_tapped", 0):
            K._tapped = 1
            K.tap("xblk3", B.x_blk[:, :], [tok_x])
        dst = [B.x_lnT[:, c * T + b * S: c * T + b * S + S] for c in range(DC)]
        toks = emit_ln_block(K, B, PS, tok_x, dst, None)
        ln_toks[b] = toks
        for tk_ in toks:
            K.wr_done("big0", tk_, append=True)

    if getattr(K, "ln_only", False):
        tst = K.dma("sp", "xd", xdst[3], B.x_blk[:, :],
                    waits=ln_toks[3])
        K.rd_done("x_blk", tst)
        return tst

    otok_prev = [None] * H
    for h in range(H):
        stat_toks = []

        def wslot(idx, par):
            o = (idx * 2 + par) * DC * 128 if idx < 2 else 4 * DC * 128
            return B.wring[:, o:o + DC * 128]
        par = h % 2
        res_q, res_k, res_v = f"wr0{par}", f"wr1{par}", "wr20"
        tkw = K.dma("pool", "wd", wslot(1, par), Wap["wk"][h],
                    waits=K.wr_waits(res_k))
        tqw = K.dma("pool", "wd", wslot(0, par), Wap["wq"][h],
                    waits=K.wr_waits(res_q))
        tvw = K.dma("pool", "wd", wslot(2, par), Wap["wv"][h],
                    waits=K.wr_waits(res_v))
        for _ in range(3):
            if getattr(K, "bg", None):
                K.bg.pop(0)()

        def proj(mat, wtok, wap, res_w, dst, dres):
            wdst = K.wr_waits(dres)
            for s in range(NB):
                bank = ["P0", "P1", "P2"][s % 3]
                Pa = getattr(PS, bank)
                wb = K.wr_waits(bank)
                mt = None
                for c in range(DC):
                    mt = K.mm(Pa[:, :], wap[:, c * 128:(c + 1) * 128],
                              B.x_lnT[:, c * T + s * S: c * T + s * S + S],
                              start=(c == 0), stop=(c == DC - 1),
                              waits=([wtok, ln_toks[s][c]] + (wb + wdst if c == 0 else [])),
                              inc="pe" if c == DC - 1 else None)
                K.rd_done(res_w, mt)
                K.rd_done("big0", mt)
                epilogue(mat, s, Pa, bank, mt, dst, dres)

        def sqstat(src_ap, wtok, seg):
            wsq = K.wr_waits("sqm")
            ts_ = K.emit("act", lambda sc: sc.activation(
                B.sqm[:, :], src_ap, AF.Square, bias=0.0, scale=1.0),
                waits=[wtok] + wsq, inc="act")
            mt = K.mm(PS.pstat[:, :], B.ones_col[:, :], B.sqm[:, :], True, True,
                      waits=[ts_] + K.wr_waits("pstat"), inc="pe")
            K.wr_done("sqm", mt)
            te = K.emit("dve", lambda v, seg=seg: v.tensor_copy(
                B.row_f[:, seg * S:(seg + 1) * S], PS.pstat[:, :]),
                waits=[mt], inc="dve")
            K.wr_done("pstat", te)
            stat_toks.append(te)
            return te

        def rope(src_ap, s, dst, dres, extra=()):
            sl = slice(s * S, (s + 1) * S)
            wsw = K.wr_waits("P4")
            tsw = K.mm(PS.P4[:, :], B.pmat[:, :], src_ap, True, True,
                       waits=list(extra) + wsw + [K.consts_tok], inc="pe")
            t_a = K.emit("dve", lambda v: v.tensor_tensor(
                B.qt2[:, sl], PS.P4[:, :], B.sful[:, sl], ALU.mult),
                waits=[tsw] + K.wr_waits("qt2"), inc="dve")
            K.wr_done("P4", t_a)
            t_b = K.emit("dve", lambda v: v.tensor_tensor(
                dst[:, sl], src_ap, B.cful[:, sl], ALU.mult),
                waits=list(extra) + [K.rope_tok] +
                (K.wr_waits(dres) if s == 0 else []),
                inc="dve")
            t_c = K.emit("dve", lambda v: v.tensor_tensor(
                dst[:, sl], dst[:, sl], B.qt2[:, sl], ALU.add),
                waits=[t_a, t_b], inc="dve")
            K.wr_done("qt2", t_c)
            K.wr_done(dres, t_c, append=(s > 0))
            return t_c

        def epilogue(mat, s, Pa, bank, mt, dst, dres):
            sl = slice(s * S, (s + 1) * S)
            seg = s
            if kind == "a1b":
                tcp = K.emit("act", lambda sc: sc.activation(
                    dst[:, sl], Pa[:, :], AF.Copy, bias=0.0, scale=1.0),
                    waits=[mt] + (K.wr_waits(dres) if s == 0 else []),
                    inc="act")
                K.wr_done(dres, tcp, append=(s > 0))
                if mat != "v":
                    te = sqstat(Pa[:, :], mt, seg)
                    K.rd_done(bank, te)
                K.rd_done(bank, tcp)
            elif kind == "a0a":
                if mat != "v":
                    te = sqstat(Pa[:, :], mt, seg)
                    K.rd_done(bank, te)
                    tcp = K.emit("act", lambda sc: sc.activation(
                        B.qt1[:, sl], Pa[:, :], AF.Copy, bias=0.0, scale=1.0),
                        waits=[mt] + (K.wr_waits("qt1") if s == 0 else []),
                        inc="act")
                    K.rd_done(bank, tcp)
                    K.wr_done("qt1", tcp, append=(s > 0))
                    rope(B.qt1[:, sl], s, dst, dres, extra=(tcp,))
                else:
                    tcp = K.emit("act", lambda sc: sc.activation(
                        dst[:, sl], Pa[:, :], AF.Copy, bias=0.0, scale=1.0),
                        waits=[mt] + (K.wr_waits(dres) if s == 0 else []),
                        inc="act")
                    K.wr_done(dres, tcp, append=(s > 0))
                    K.rd_done(bank, tcp)
            elif kind == "a0b":
                if mat != "v":
                    wsq = K.wr_waits("sqm")
                    ts_ = K.emit("act", lambda sc: sc.activation(
                        B.sqm[:, :], Pa[:, :], AF.Square, bias=0.0, scale=1.0),
                        waits=[mt] + wsq, inc="act")
                    mt2 = K.mm(PS.pstat[:, :], B.ones_col[:, :], B.sqm[:, :],
                               True, True, waits=[ts_] + K.wr_waits("pstat"),
                               inc="pe")
                    K.wr_done("sqm", mt2)
                    te = K.emit("dve", lambda v: v.tensor_copy(
                        B.row_f[:, 4 * S:5 * S], PS.pstat[:, :]),
                        waits=[mt2], inc="dve")
                    K.wr_done("pstat", te)
                    t5_ = K.emit("act", lambda sc: sc.activation(
                        B.row_f[:, 4 * S:5 * S], B.row_f[:, 4 * S:5 * S],
                        AF.Ln, bias=B.ccol[0:1, 5:6], scale=B.ccol[0:1, 4:5]),
                        waits=[te], inc="act")
                    t6_ = K.emit("act", lambda sc: sc.activation(
                        B.row_f[:, 4 * S:5 * S], B.row_f[:, 4 * S:5 * S],
                        AF.Exp, bias=0.0, scale=B.ccol[0:1, 8:9]),
                        waits=[t5_], inc="act")
                    t7_ = K.emit("dve", lambda v: v.tensor_copy(
                        B.row_bf[:, 0:S], B.row_f[:, 4 * S:5 * S]),
                        waits=[t6_], inc="dve")
                    tb_ = K.mm(PS.P3[:, :], B.ones_row[:, :], B.row_bf[:, 0:S],
                               True, True, waits=[t7_] + K.wr_waits("P3"),
                               inc="pe")
                    tbc = K.emit("act", lambda sc: sc.activation(
                        B.bc2[:, :], PS.P3[:, :], AF.Copy, bias=0.0, scale=1.0),
                        waits=[tb_] + K.wr_waits("bc2"), inc="act")
                    K.wr_done("P3", tbc)
                    mi = 0 if mat == "q" else 1
                    tn = K.emit("dve", lambda v, mi=mi: v.scalar_tensor_tensor(
                        B.qt1[:, sl], Pa[:, :], B.nrmw[:, mi:mi + 1],
                        B.bc2[:, :], ALU.mult, ALU.mult),
                        waits=[tnrm, tbc, mt] + (K.wr_waits("qt1") if s == 0 else []),
                        inc="dve")
                    K.wr_done("qt1", tn, append=(s > 0))
                    K.wr_done("bc2", tn)
                    K.rd_done(bank, tn)
                    te2 = sqstat(B.qt1[:, sl], tn, seg)
                    rope(B.qt1[:, sl], s, dst, dres, extra=(tn,))
                else:
                    tcp = K.emit("act", lambda sc: sc.activation(
                        dst[:, sl], Pa[:, :], AF.Copy, bias=0.0, scale=1.0),
                        waits=[mt] + (K.wr_waits(dres) if s == 0 else []),
                        inc="act")
                    K.wr_done(dres, tcp, append=(s > 0))
                    K.rd_done(bank, tcp)
            elif kind == "a1a":
                if mat != "v":
                    tu = K.emit("dve", lambda v: v.tensor_scalar(
                        B.qt1[:, sl], Pa[:, :], B.nrmw[:, 0:1], B.nrmw[:, 1:2],
                        ALU.mult, ALU.add),
                        waits=[tnrm, mt] + (K.wr_waits("qt1") if s == 0 else []),
                        inc="dve")
                    K.wr_done("qt1", tu, append=(s > 0))
                    K.rd_done(bank, tu)
                    wsq = K.wr_waits("sqm")
                    ts_ = K.emit("act", lambda sc: sc.activation(
                        B.sqm[:, :], B.qt1[:, sl], AF.Square, bias=0.0,
                        scale=1.0), waits=[tu] + wsq, inc="act")
                    mtA = K.mm(PS.pstat[:, :], B.ones_col[:, :], B.sqm[:, :],
                               True, True, waits=[ts_] + K.wr_waits("pstat"),
                               inc="pe")
                    K.wr_done("sqm", mtA)
                    teA = K.emit("dve", lambda v: v.tensor_copy(
                        B.row_f[:, 4 * S:5 * S], PS.pstat[:, :]),
                        waits=[mtA], inc="dve")
                    mtB = K.mm(PS.pstat[:, :], B.ones_col[:, :], B.qt1[:, sl],
                               True, True, waits=[tu, teA], inc="pe")
                    teB = K.emit("dve", lambda v: v.tensor_copy(
                        B.row_f[:, 5 * S:6 * S], PS.pstat[:, :]),
                        waits=[mtB], inc="dve")
                    K.wr_done("pstat", teB)
                    r1 = K.emit("dve", lambda v: v.tensor_single_scalar(
                        B.row_f[:, 5 * S:6 * S], B.row_f[:, 5 * S:6 * S],
                        1.0 / DH, ALU.mult), waits=[teB], inc="dve")
                    r2 = K.emit("dve", lambda v: v.tensor_tensor(
                        B.row_f[:, 0:S], B.row_f[:, 5 * S:6 * S],
                        B.row_f[:, 5 * S:6 * S], ALU.mult),
                        waits=[r1], inc="dve")
                    r3 = K.emit("dve", lambda v: v.scalar_tensor_tensor(
                        B.row_f[:, 0:S], B.row_f[:, 4 * S:5 * S], 1.0 / DH,
                        B.row_f[:, 0:S], ALU.mult, ALU.subtract),
                        waits=[r2, teA], inc="dve")
                    r4 = K.emit("act", lambda sc: sc.activation(
                        B.row_f[:, 0:S], B.row_f[:, 0:S], AF.Ln,
                        bias=B.ccol[0:1, 3:4], scale=1.0), waits=[r3], inc="act")
                    r5 = K.emit("act", lambda sc: sc.activation(
                        B.row_f[:, 0:S], B.row_f[:, 0:S], AF.Exp,
                        bias=0.0, scale=B.ccol[0:1, 8:9]), waits=[r4],
                        inc="act")
                    qscale = SCL if mat == "q" else 1.0
                    r6 = K.emit("dve", lambda v, qs=qscale: v.tensor_single_scalar(
                        B.row_bf[:, 0:S], B.row_f[:, 0:S], qs, ALU.mult),
                        waits=[r5], inc="dve")
                    r7 = K.emit("dve", lambda v: v.tensor_tensor(
                        B.row_f[:, S:2 * S], B.row_f[:, 5 * S:6 * S],
                        B.row_f[:, 0:S], ALU.mult), waits=[r5], inc="dve")
                    qs2 = -qscale
                    r8 = K.emit("dve", lambda v, qs=qs2: v.tensor_single_scalar(
                        B.row_bf[:, S:2 * S], B.row_f[:, S:2 * S], qs,
                        ALU.mult), waits=[r7], inc="dve")
                    tb1 = K.mm(PS.P3[:, :], B.ones_row[:, :], B.row_bf[:, 0:S],
                               True, True, waits=[r6] + K.wr_waits("P3"),
                               inc="pe")
                    tb2 = K.mm(PS.P4[:, :], B.ones_row[:, :],
                               B.row_bf[:, S:2 * S], True, True,
                               waits=[r8] + K.wr_waits("P4"), inc="pe")
                    tbc = K.emit("act", lambda sc: sc.activation(
                        B.bc2[:, :], PS.P3[:, :], AF.Copy, bias=0.0, scale=1.0),
                        waits=[tb1] + K.wr_waits("bc2"), inc="act")
                    K.wr_done("P3", tbc)
                    w1 = K.emit("dve", lambda v: v.tensor_tensor(
                        dst[:, sl], B.qt1[:, sl], B.bc2[:, :], ALU.mult),
                        waits=[tbc, tu] + (K.wr_waits(dres) if s == 0 else []),
                        inc="dve")
                    K.wr_done("bc2", w1)
                    w2 = K.emit("dve", lambda v: v.tensor_tensor(
                        dst[:, sl], dst[:, sl], PS.P4[:, :], ALU.add),
                        waits=[tb2, w1], inc="dve")
                    K.wr_done("P4", w2)
                    K.wr_done(dres, w2, append=(s > 0))
                else:
                    tcp = K.emit("act", lambda sc: sc.activation(
                        dst[:, sl], Pa[:, :], AF.Copy, bias=0.0, scale=1.0),
                        waits=[mt] + (K.wr_waits(dres) if s == 0 else []),
                        inc="act")
                    K.wr_done(dres, tcp, append=(s > 0))
                    K.rd_done(bank, tcp)

        proj("k", tkw, wslot(1, par), res_k, B.kh, "kh")
        if use_bound:
            tm1 = K.emit("dve", lambda v: v.reduce_max(
                B.col_f[0:1, 28:29], B.row_f[:, 0:4 * S], mybir.AxisListType.X),
                inc="dve")
            tm2 = K.emit("dve", lambda v: v.tensor_single_scalar(
                B.col_f[0:1, 29:30], B.col_f[0:1, 28:29], 1.0 / DH, ALU.mult),
                waits=[tm1], inc="dve")
            tm3 = K.emit("dve", lambda v: v.tensor_copy(
                B.row_bf[0:1, S:S + 1], B.col_f[0:1, 29:30]),
                waits=[tm2], inc="dve")
            tb_ = K.mm(PS.P3[:, 0:1], B.ones_row[:, :], B.row_bf[0:1, S:S + 1],
                       True, True, waits=[tm3] + K.wr_waits("P3"), inc="pe")
            tk2 = K.emit("act", lambda sc: sc.activation(
                B.col_f[:, 30:31], PS.P3[:, 0:1], AF.Copy, bias=0.0,
                scale=1.0), waits=[tb_], inc="act")
            K.wr_done("P3", tk2)
        proj("q", tqw, wslot(0, par), res_q, B.qh, "qh")
        if use_bound:
            wtr = K.wr_waits("P3")
            ttr = None
            for i in range(NT_TILES):
                ttr = K.mm(PS.P3[:, i:i + 1],
                           B.row_f[0:1, i * 128:(i + 1) * 128],
                           B.identf[0:1, 0:1], True, True,
                           waits=(wtr + [stat_toks[-1], K.consts_tok])
                           if i == 0 else [],
                           inc="pe" if i == NT_TILES - 1 else None)
            tcv = K.emit("act", lambda sc: sc.activation(
                B.col_f[:, 0:NT_TILES], PS.P3[:, 0:NT_TILES], AF.Copy,
                bias=0.0, scale=1.0), waits=[ttr], inc="act")
            K.wr_done("P3", tcv)
            tnb1 = K.emit("act", lambda sc: sc.activation(
                B.col_f[:, 0:NT_TILES], B.col_f[:, 0:NT_TILES], AF.Sqrt,
                bias=0.0, scale=B.col_f[:, 30:31]),
                waits=[tcv, tk2], inc="act")
            tnb = K.emit("dve", lambda v: v.tensor_single_scalar(
                B.col_f[:, 0:NT_TILES], B.col_f[:, 0:NT_TILES], -1.0,
                ALU.mult), waits=[tnb1], inc="dve")
        else:
            tnb = None
        proj("v", tvw, wslot(2, par), res_v, B.vt, "vt")
        vtk = None
        for g4 in range(4):
            ptr = PS.ptrA if g4 % 2 == 0 else PS.ptrB
            pres = "ptrA" if g4 % 2 == 0 else "ptrB"
            wp = K.wr_waits(pres)
            tt_ = None
            for k4 in range(4):
                tt = g4 * 4 + k4
                tt_ = K.tr(ptr[:, k4 * 128:(k4 + 1) * 128],
                           B.vt[:, tt * 128:(tt + 1) * 128], B.ident[:, :],
                           waits=(wp + [K.consts_tok] +
                                  K.rd_waits("vt") if k4 == 0 else []),
                           inc="pe" if k4 == 3 else None)
            vtk = K.emit("dve", lambda v, g4=g4, ptr=ptr: v.tensor_copy(
                B.vsb[:, g4 * S:(g4 + 1) * S], ptr[:, :]),
                waits=[tt_] + (K.wr_waits("vsb") if g4 == 0 else []),
                inc="dve")
            K.wr_done(pres, vtk)
            K.wr_done("vsb", vtk, append=(g4 > 0))

        if getattr(K, "proj_only", False):
            continue
        def emit_ptpv(st):
            i, jlo, nj, tp = st
            ptk = []
            for g4 in range((nj + 3) // 4):
                ptr = PS.ptrA if g4 % 2 == 0 else PS.ptrB
                pres = "ptrA" if g4 % 2 == 0 else "ptrB"
                wp = K.wr_waits(pres)
                n4 = min(4, nj - g4 * 4)
                tt_ = None
                for k4 in range(n4):
                    jj = g4 * 4 + k4
                    tt_ = K.tr(ptr[:, k4 * 128:(k4 + 1) * 128],
                               B.P[:, jj * 128:(jj + 1) * 128], B.ident[:, :],
                               waits=[tp] + (wp if k4 == 0 else []),
                               inc="pe" if k4 == n4 - 1 else None)
                tev = K.emit("dve", lambda v, g4=g4, n4=n4, ptr=ptr:
                             v.tensor_copy(
                    B.PT[:, g4 * S:g4 * S + n4 * 128], ptr[:, 0:n4 * 128]),
                    waits=[tt_] + (K.wr_waits("slot1PT") if g4 == 0 else []),
                    inc="dve")
                K.wr_done(pres, tev)
                K.wr_done("slot1PT", tev, append=(g4 > 0))
                ptk.append(tev)
            w4 = K.wr_waits("P4")
            mo = None
            for jj in range(nj):
                j = jlo + jj
                mo = K.mm(PS.P4[:, 0:128], B.vsb[:, j * 128:(j + 1) * 128],
                          B.PT[:, jj * 128:(jj + 1) * 128],
                          start=(jj == 0), stop=(jj == nj - 1),
                          waits=[ptk[jj // 4], vtk] + (w4 if jj == 0 else []),
                          inc="pe" if jj == nj - 1 else None)
            K.rd_done("slot1PT", mo)
            K.rd_done("vsb", mo)
            to = K.emit("act", lambda sc, i=i: sc.activation(
                B.osb[:, i * 128:(i + 1) * 128], PS.P4[:, 0:128], AF.Copy,
                bias=0.0, scale=1.0),
                waits=[mo] + (K.wr_waits("osb") if i == 0 else []), inc="act")
            K.wr_done("P4", to)
            K.wr_done("osb", to, append=(i > 0))

        prev_tile = None
        for i in range(NT_TILES):
            jlo = max(0, i - WTILES) if windowed else 0
            nj = i - jlo + 1
            ncols = nj * 128
            nsp = (ncols + S - 1) // S
            par2 = i % 2
            banks = ["P0", "P1"] if par2 == 0 else ["P2", "P3"]
            dp0 = 16 if par2 == 0 else 22
            dnc, drc = (20, 21) if par2 == 0 else (26, 27)
            dloc = (i - jlo) * 128
            sd, off = dloc // S, dloc % S
            etoks = []
            wE = K.wr_waits("slot1E") if kind != "a1a" else []
            wP = K.wr_waits("slot1P")
            first_grp_toks = None
            for sp_i in range(nsp):
                lc = sp_i * S
                w = min(S, ncols - lc)
                bname = banks[sp_i % 2]
                Pa = getattr(PS, bname)
                wb = K.wr_waits(bname)
                extra = []
                if sp_i >= 2:
                    extra = [etoks[sp_i - 2]]
                mt = K.mm(Pa[:, 0:w], B.qh[:, i * 128:(i + 1) * 128],
                          B.kh[:, jlo * 128 + lc: jlo * 128 + lc + w],
                          True, True,
                          waits=wb + extra + (K.rd_waits("qh") +
                                              K.rd_waits("kh")
                                              if sp_i == 0 else []), inc="pe")
                mtk = mt
                if sp_i == sd:
                    if kind == "a1a":
                        mtk = K.emit("dve", lambda v, Pa=Pa, off=off:
                                     v.tensor_tensor(
                            Pa[:, off:off + 128], Pa[:, off:off + 128],
                            B.m_tril01[:, :], ALU.mult),
                            waits=[mt, K.consts_tok], inc="dve")
                    else:
                        mtk = K.emit("dve", lambda v, Pa=Pa, off=off:
                                     v.tensor_tensor(
                            Pa[:, off:off + 128], Pa[:, off:off + 128],
                            B.m_tril_neg[:, :], ALU.add),
                            waits=[mt, K.consts_tok], inc="dve")
                if sp_i == 0 and windowed and i >= WTILES:
                    mtk = K.emit("dve", lambda v, Pa=Pa: v.tensor_tensor(
                        Pa[:, 0:128], Pa[:, 0:128],
                        B.m_triu_neg[:, :], ALU.add),
                        waits=[mt, mtk, K.consts_tok], inc="dve")
                if kind == "a1a":
                    te = K.emit("act", lambda sc, Pa=Pa, lc=lc, w=w, sp_i=sp_i,
                                dp0=dp0: sc.activation(
                        B.P[:, lc:lc + w], Pa[:, 0:w], AF.Square,
                        bias=0.0, scale=1.0,
                        accum_out=B.col_f[:, dp0 + sp_i:dp0 + sp_i + 1]),
                        waits=[mtk] + wP, inc="act")
                else:
                    te = K.emit("act", lambda sc, Pa=Pa, lc=lc, w=w, sp_i=sp_i,
                                dp0=dp0, i=i: sc.activation(
                        B.E[:, lc:lc + w], Pa[:, 0:w], AF.Exp,
                        bias=B.col_f[:, i:i + 1], scale=B.ccol[:, 0:1],
                        accum_out=B.col_f[:, dp0 + sp_i:dp0 + sp_i + 1]),
                        waits=[mtk, tnb, K.consts_tok] + wE, inc="act")
                etoks.append(te)
                K.wr_done(bname, te)
                if sp_i == 1 and prev_tile is not None:
                    emit_ptpv(prev_tile)
                    prev_tile = None
            K.rd_done("qh", mt)
            K.rd_done("kh", mt)
            if prev_tile is not None:
                emit_ptpv(prev_tile)
                prev_tile = None
            td1 = K.emit("dve", lambda v, nsp=nsp, dp0=dp0, dnc=dnc:
                         v.reduce_sum(
                B.col_f[:, dnc:dnc + 1], B.col_f[:, dp0:dp0 + nsp],
                mybir.AxisListType.X), waits=etoks, inc="dve")
            epsv = 1e-5 if kind == "a1a" else 1e-30
            td2 = K.emit("dve", lambda v, e=epsv, dnc=dnc:
                         v.tensor_single_scalar(
                B.col_f[:, dnc:dnc + 1], B.col_f[:, dnc:dnc + 1], e, ALU.add),
                waits=[td1], inc="dve")
            td3 = K.emit("dve", lambda v, dnc=dnc, drc=drc: v.reciprocal(
                B.col_f[:, drc:drc + 1], B.col_f[:, dnc:dnc + 1]),
                waits=[td2], inc="dve")
            if kind == "a1a":
                tp = K.emit("dve", lambda v, nc_=ncols, drc=drc:
                            v.tensor_single_scalar(
                    B.P[:, 0:nc_], B.P[:, 0:nc_], B.col_f[:, drc:drc + 1],
                    ALU.mult), waits=[td3] + etoks + wP, inc="dve")
            else:
                tp = K.emit("dve", lambda v, nc_=ncols, drc=drc:
                            v.tensor_single_scalar(
                    B.P[:, 0:nc_], B.E[:, 0:nc_], B.col_f[:, drc:drc + 1],
                    ALU.mult), waits=[td3] + etoks + wP, inc="dve")
                for te in etoks:
                    K.wr_done("slot1E", tp, append=True)
            K.wr_done("slot1P", tp)
            prev_tile = (i, jlo, nj, tp)
        emit_ptpv(prev_tile)
        prev_tile = None

        if getattr(K, "dbg", False) and h == 0:
            K.tap("qh", B.qh[:, :], K.rd_waits("qh"))
            K.tap("kh", B.kh[:, :], K.rd_waits("kh"))
            K.tap("vsb", B.vsb[:, :], K.rd_waits("vsb"))
            K.tap("rowf", B.row_f[:, :], [self_last_dve(K)])
            K.tap("colf", B.col_f[:, :], [self_last_dve(K), self_last_act(K)])
            K.tap("E", B.E[:, :], [self_last_act(K)])
            K.tap("P", B.P[:, :], [self_last_dve(K)])
            K.tap("osb", B.osb[:, :], K.rd_waits("osb"))
            K.tap("xln", B.x_lnT[:, :], K.rd_waits("big0"))
            K.tap("cful", B.cful[:, :], [self_last_act(K)])
            K.tap("sful", B.sful[:, :], [self_last_dve(K)])
        tso = K.dma("sp", "xd", ost[h], B.osb[:, :],
                    waits=K.rd_waits("osb"))
        K.rd_done("osb", tso)
        otok_prev[h] = tso

    if getattr(K, "proj_only", False):
        tst = K.dma("sp", "xd", xdst[3], B.x_blk[:, :],
                    waits=[self_last_dve(K), self_last_act(K)])
        K.rd_done("x_blk", tst)
        return tst

    for ts in range(NB):
        tosl = K.dma("sp", "xd", B.osl[:, :].rearrange("p (h u) -> p h u", h=H),
                     ost[:, :, ts * S:(ts + 1) * S].rearrange("h p u -> p h u"),
                     waits=otok_prev + K.wr_waits("slot0"))
        tox = K.dma("sp", "xd", B.x_blk[:, :], xsrc[ts],
                    waits=K.wr_waits("x_blk"))
        rtoks = []
        for dc in range(DC):
            wslot_o = B.wring[:, (dc % 2) * DC * 128:(dc % 2 + 1) * DC * 128]
            res_o = f"wro{dc % 2}"
            tww = K.dma("pool", "wd", wslot_o, Wap["wo"][dc],
                        waits=K.wr_waits(res_o))
            Pa = PS.P0 if dc % 2 == 0 else PS.P1
            bank = "P0" if dc % 2 == 0 else "P1"
            wb = K.wr_waits(bank)
            mt = None
            for c in range(DC):
                mt = K.mm(Pa[:, :], wslot_o[:, c * 128:(c + 1) * 128],
                          B.osl[:, c * S:(c + 1) * S],
                          start=(c == 0), stop=(c == DC - 1),
                          waits=[tww, tosl] + (wb if c == 0 else []),
                          inc="pe" if c == DC - 1 else None)
            K.rd_done(res_o, mt)
            K.rd_done("slot0", mt)
            tr_ = K.emit("dve", lambda v, dc=dc, Pa=Pa: v.tensor_tensor(
                B.x_blk[:, dc * S:(dc + 1) * S],
                B.x_blk[:, dc * S:(dc + 1) * S], Pa[:, :], ALU.add),
                waits=[mt, tox], inc="dve")
            K.wr_done(bank, tr_)
            rtoks.append(tr_)
        tst = K.dma("sp", "xd", xdst[ts], B.x_blk[:, :], waits=rtoks)
        K.rd_done("x_blk", tst)
        for r in rtoks:
            K.rd_done("x_blk", r)
    return tst


NT_TILES = 16
WTILES = 8


def self_last_dve(K):
    return ("dve", K.cnt.get("dve", 0))


def self_last_act(K):
    return ("act", K.cnt.get("act", 0))


def emit_ffn(K, B, PS, kind, Wap, xsrc, xdst, pfx=""):
    is0 = (kind == "ffn0")
    pc = getattr(K, "pc", {})
    tsg = K.dma("sp", "xd", B.sgc[:, :], Wap["sg"], waits=K.wr_waits("sgc"))
    tsu = None
    if is0:
        tsu = K.dma("sp", "xd", B.suc[:, :], Wap["su"], waits=K.wr_waits("suc"))
    tsd = K.dma("sp", "xd", B.sdc[:, :], Wap["sd"], waits=K.wr_waits("sdc"))

    tst = None
    for b in range(NB):
        tok_x = K.dma("sp", "xd", B.x_blk[:, :], xsrc[b],
                      waits=K.wr_waits("x_blk"))
        dst = [B.x_lnB[:, c * S:(c + 1) * S] for c in range(DC)]
        ln_toks = emit_ln_block(K, B, PS, tok_x, dst, "slot0")
        for tk_ in ln_toks:
            K.wr_done("slot0", tk_, append=True)

        wfuse = K.wr_waits("fuse")
        ftoks = []
        for ic in range(IC):
            par = ic % 2
            gsl = B.wgu[:, (0 + par) * DC * 128:(1 + par) * DC * 128]
            usl = B.wgu[:, (2 + par) * DC * 128:(3 + par) * DC * 128]
            gres, ures = f"wg{par}", f"wu{par}"
            gsrc = Wap["g_blob"][ic]
            gpc = pc.get((pfx, "g", ic))
            if gpc is not None:
                gsrc = Wap["g_bf"][ic]
            tgw = K.dma("pool", "wd", gsl, gsrc,
                        waits=K.wr_waits(gres) + [gpc])
            tuw = None
            if is0:
                tuw = K.dma("pool", "wd", usl, Wap["u_blob"][ic],
                            waits=K.wr_waits(ures))
            Pg = PS.P0 if par == 0 else PS.P2
            gbank = "P0" if par == 0 else "P2"
            Pu = PS.P1 if par == 0 else PS.P3
            ubank = "P1" if par == 0 else "P3"
            wgb = K.wr_waits(gbank)
            mg = None
            for c in range(DC):
                mg = K.mm(Pg[:, :], gsl[:, c * 128:(c + 1) * 128], dst[c],
                          start=(c == 0), stop=(c == DC - 1),
                          waits=[tgw, ln_toks[c]] + (wgb if c == 0 else []),
                          inc="pe" if c == DC - 1 else None)
            K.rd_done(gres, mg)
            K.rd_done("slot0", mg)
            if is0:
                wub = K.wr_waits(ubank)
                mu = None
                for c in range(DC):
                    mu = K.mm(Pu[:, :], usl[:, c * 128:(c + 1) * 128], dst[c],
                              start=(c == 0), stop=(c == DC - 1),
                              waits=[tuw, ln_toks[c]] + (wub if c == 0 else []),
                              inc="pe" if c == DC - 1 else None)
                K.rd_done(ures, mu)
                K.rd_done("slot0", mu)
                stg = B.sqm if par == 0 else B.bc2
                sres = "sqm" if par == 0 else "bc2"
                tsl = K.emit("act", lambda sc, Pg=Pg, stg=stg, ic=ic:
                             sc.activation(
                    stg[:, :], Pg[:, :], AF.Silu, bias=0.0,
                    scale=B.sgc[:, ic:ic + 1]),
                    waits=[mg, tsg] + K.wr_waits(sres), inc="act")
                K.rd_done(gbank, tsl)
                tf = K.emit("dve", lambda v, Pu=Pu, stg=stg, ic=ic:
                            v.scalar_tensor_tensor(
                    B.fuse[:, ic * S:(ic + 1) * S], Pu[:, :],
                    B.suc[:, ic:ic + 1], stg[:, :], ALU.mult, ALU.mult),
                    waits=[mu, tsu, tsl] + (wfuse if ic == 0 else []),
                    inc="dve")
                K.rd_done(ubank, tf)
                K.wr_done(sres, tf)
            else:
                tf = K.emit("act", lambda sc, Pg=Pg, ic=ic: sc.activation(
                    B.fuse[:, ic * S:(ic + 1) * S], Pg[:, :], AF.Gelu,
                    bias=B.sgc[:, ic:ic + 1], scale=1.0),
                    waits=[mg, tsg] + (wfuse if ic == 0 else []), inc="act")
                K.rd_done(gbank, tf)
            ftoks.append(tf)
            K.wr_done("fuse", tf, append=(ic > 0))

        rtoks = []
        for oc in range(DC):
            par = oc % 2
            dsl = B.wdn[:, par * IC * 128:(par + 1) * IC * 128]
            dres = f"wd{par}"
            dsrc = Wap["d_blob"][oc]
            dpc = pc.get((pfx, "d", oc))
            if dpc is not None:
                dsrc = Wap["d_bf"][oc]
            tdw = K.dma("pool", "wd", dsl, dsrc,
                        waits=K.wr_waits(dres) + [dpc])
            Pd = PS.P0 if par == 0 else PS.P1
            dbank = "P0" if par == 0 else "P1"
            wdb = K.wr_waits(dbank)
            if getattr(K, "dbg", False) and oc == DC - 1 and b == 3 and                     not getattr(K, "_dtapped", 0):
                K._dtapped = 1
                K.tap("fuse_end", B.fuse[:, 48 * S:64 * S], [tdw] + wdb)
                K.tap("wd15", dsl, [tdw])
            md = None
            for c in range(IC):
                md = K.mm(Pd[:, :], dsl[:, c * 128:(c + 1) * 128],
                          B.fuse[:, c * S:(c + 1) * S],
                          start=(c == 0), stop=(c == IC - 1),
                          waits=[tdw, ftoks[c]] + (wdb if c == 0 else []),
                          inc="pe" if c == IC - 1 else None)
            K.rd_done(dres, md)
            K.rd_done("fuse", md)
            if is0:
                tr_ = K.emit("dve", lambda v, oc=oc, Pd=Pd:
                             v.scalar_tensor_tensor(
                    B.x_blk[:, oc * S:(oc + 1) * S], Pd[:, :],
                    B.sdc[:, oc:oc + 1], B.x_blk[:, oc * S:(oc + 1) * S],
                    ALU.mult, ALU.add), waits=[md, tsd, tok_x], inc="dve")
            else:
                tr_ = K.emit("dve", lambda v, oc=oc, Pd=Pd:
                             v.scalar_tensor_tensor(
                    B.x_blk[:, oc * S:(oc + 1) * S], Pd[:, :],
                    B.sdc[:, oc:oc + 1], B.x_blk[:, oc * S:(oc + 1) * S],
                    ALU.add, ALU.add), waits=[md, tsd, tok_x], inc="dve")
            K.wr_done(dbank, tr_)
            rtoks.append(tr_)
        if getattr(K, "dbg", False) and b == 3 and not getattr(K, "_ftapped", 0):
            K._ftapped = 1
            K.tap("fxblk3", B.x_blk[:, :], rtoks)
        tst = K.dma("sp", "xd", xdst[b], B.x_blk[:, :], waits=rtoks)
        K.rd_done("x_blk", tst)
        for r in rtoks:
            K.rd_done("x_blk", r)
    return tst


SUBLAYERS = [
    ("a0a", "L0aa"), ("ffn0", "L0f"), ("a0b", "L0ab"), ("ffn0", "L0f"),
    ("a1a", "L1aa"), ("ffn1", "L1f"), ("a1b", "L1ab"), ("ffn1", "L1f"),
]


def build_program(sub_idx_list):
    K = Prog()
    nc = K.nc
    B = alloc_sbuf(nc)
    PS = alloc_psum(nc)
    need_rope = any(SUBLAYERS[i][0] in ("a0a", "a0b") for i in sub_idx_list)
    need_attn = any(SUBLAYERS[i][0].startswith("a") for i in sub_idx_list)

    xin = nc.dram_tensor("xin", [NB, 128, DC * S], F32, kind="ExternalInput").ap()
    xout = nc.dram_tensor("xout", [NB, 128, DC * S], F32,
                          kind="ExternalOutput").ap()
    nsub = len(sub_idx_list)
    xst = xst2 = None
    if nsub > 1:
        xst = nc.dram_tensor("xst", [NB, 128, DC * S], F32, kind="Internal").ap()
    if nsub > 2:
        xst2 = nc.dram_tensor("xst2", [NB, 128, DC * S], F32,
                              kind="Internal").ap()
    ost = None
    if need_attn:
        ost = nc.dram_tensor("ost", [H, 128, T], BF16, kind="Internal").ap()

    Waps = {}
    for i in sub_idx_list:
        kind, pfx = SUBLAYERS[i]
        if pfx in Waps:
            continue
        W = {}
        if kind.startswith("a"):
            for nm in ("wq", "wk", "wv", "wo"):
                W[nm] = nc.dram_tensor(f"{pfx}_{nm}", [H, 128, DC * 128], F32,
                                       kind="ExternalInput").ap()
            if kind in ("a0b", "a1a"):
                W["nrm"] = nc.dram_tensor(f"{pfx}_nrm", [128, 2], F32,
                                          kind="ExternalInput").ap()
            W["ost"] = ost
        elif kind == "ffn0":
            W["g_blob"] = nc.dram_tensor(f"{pfx}_g", [IC, 128, DC * 128], I8,
                                         kind="ExternalInput").ap()
            W["u_blob"] = nc.dram_tensor(f"{pfx}_u", [IC, 128, DC * 128], I8,
                                         kind="ExternalInput").ap()
            W["d_blob"] = nc.dram_tensor(f"{pfx}_d", [DC, 128, IC * 128], I8,
                                         kind="ExternalInput").ap()
            W["sg"] = nc.dram_tensor(f"{pfx}_sg", [128, IC], F32,
                                     kind="ExternalInput").ap()
            W["su"] = nc.dram_tensor(f"{pfx}_su", [128, IC], F32,
                                     kind="ExternalInput").ap()
            W["sd"] = nc.dram_tensor(f"{pfx}_sd", [128, DC], F32,
                                     kind="ExternalInput").ap()
        else:
            W["g_blob"] = nc.dram_tensor(f"{pfx}_g", [IC, 128, DC * 128], F32,
                                         kind="ExternalInput").ap()
            W["d_blob"] = nc.dram_tensor(f"{pfx}_d", [DC, 128, IC * 128], F32,
                                         kind="ExternalInput").ap()
            W["g_bf"] = nc.dram_tensor(f"{pfx}_gbf", [IC, 128, DC * 128],
                                       BF16, kind="Internal").ap()
            W["d_bf"] = nc.dram_tensor(f"{pfx}_dbf", [DC, 128, IC * 128],
                                       BF16, kind="Internal").ap()
            W["sg"] = nc.dram_tensor(f"{pfx}_sg", [128, IC], F32,
                                     kind="ExternalInput").ap()
            W["sd"] = nc.dram_tensor(f"{pfx}_sd", [128, DC], F32,
                                     kind="ExternalInput").ap()
        Waps[pfx] = W

    emit_prelude(K, B, PS, need_rope)
    K.bg = []
    K.pc = {}
    for pfx, W in Waps.items():
        if "g_bf" not in W:
            continue

        def mk(key, dst, srcb, j):
            def run(key=key, dst=dst, srcb=srcb, j=j):
                K.pc[key] = K.dma("pool", "wd", dst[j], srcb[j])
            return run
        for ic in range(IC):
            K.bg.append(mk((pfx, "g", ic), W["g_bf"], W["g_blob"], ic))
        for oc in range(DC):
            K.bg.append(mk((pfx, "d", oc), W["d_bf"], W["d_blob"], oc))
    last_tok = None
    for j, i in enumerate(sub_idx_list):
        kind, pfx = SUBLAYERS[i]
        src = xin if j == 0 else (xst if j % 2 == 1 else xst2)
        dst = xout if j == nsub - 1 else (xst if j % 2 == 0 else xst2)
        if kind.startswith("a"):
            last_tok = emit_attention(K, B, PS, kind, Waps[pfx], src, dst)
        else:
            last_tok = emit_ffn(K, B, PS, kind, Waps[pfx], src, dst, pfx)
    nc = K.finalize([last_tok])
    return nc


def proj_blob(W):
    din, dout = W.shape
    ndc, noc = din // 128, dout // 128
    return np.ascontiguousarray(
        W.reshape(ndc, 128, noc, 128).transpose(2, 1, 0, 3)
        .reshape(noc, 128, ndc * 128))


def x_to_blocks(xc):
    xT = xc.T
    return np.ascontiguousarray(
        xT.reshape(DC, 128, NB, S).transpose(2, 1, 0, 3)
        .reshape(NB, 128, DC * S))


def x_from_blocks(xb):
    xT = xb.reshape(NB, 128, DC, S).transpose(2, 1, 0, 3).reshape(D, T)
    return np.ascontiguousarray(xT.T)


def colmaj(v):
    return np.ascontiguousarray(v.reshape(-1, 128).T)


def prep_weights(inputs):
    P = {}
    P["L0aa"] = {
        "wq": proj_blob(inputs["L0_attn_a_wq"]),
        "wk": proj_blob(inputs["L0_attn_a_wk"]),
        "wv": proj_blob(inputs["L0_attn_a_wv"]),
        "wo": proj_blob(inputs["L0_attn_a_wo"]),
    }
    P["L0ab"] = {
        "wq": proj_blob(np.ascontiguousarray(inputs["L0_attn_b_Wq"].T)),
        "wk": proj_blob(np.ascontiguousarray(inputs["L0_attn_b_Wk"].T)),
        "wv": proj_blob(np.ascontiguousarray(inputs["L0_attn_b_Wv"].T)),
        "wo": proj_blob(np.ascontiguousarray(inputs["L0_attn_b_Wo"].T)),
        "nrm": np.ascontiguousarray(np.stack(
            [inputs["L0_attn_b_q_norm_w"], inputs["L0_attn_b_k_norm_w"]],
            axis=1)),
    }
    P["L0f"] = {
        "g_blob": proj_blob(inputs["L0_ffn_w_gate"]),
        "u_blob": proj_blob(inputs["L0_ffn_w_up"]),
        "d_blob": proj_blob(inputs["L0_ffn_w_down"]),
        "sg": colmaj(inputs["L0_ffn_s_gate"]),
        "su": colmaj(inputs["L0_ffn_s_up"]),
        "sd": colmaj(inputs["L0_ffn_s_down"]),
    }
    P["L1aa"] = {
        "wq": proj_blob(np.ascontiguousarray(inputs["L1_attn_a_Wq"].T)),
        "wk": proj_blob(np.ascontiguousarray(inputs["L1_attn_a_Wk"].T)),
        "wv": proj_blob(np.ascontiguousarray(inputs["L1_attn_a_Wv"].T)),
        "wo": proj_blob(np.ascontiguousarray(inputs["L1_attn_a_Wo"].T)),
        "nrm": np.ascontiguousarray(np.stack(
            [inputs["L1_attn_a_gamma"], inputs["L1_attn_a_beta"]], axis=1)),
    }
    P["L1f"] = {
        "g_blob": proj_blob(np.ascontiguousarray(inputs["L1_ffn_W0"].T)),
        "d_blob": proj_blob(np.ascontiguousarray(inputs["L1_ffn_W1"].T)),
        "sg": colmaj(inputs["L1_ffn_bias0"]),
        "sd": colmaj(inputs["L1_ffn_bias1"]),
    }
    P["L1ab"] = {
        "wq": proj_blob(np.ascontiguousarray(
            inputs["L1_attn_b_W_Q"].transpose(2, 0, 1).reshape(D, D))),
        "wk": proj_blob(np.ascontiguousarray(
            inputs["L1_attn_b_W_K"].transpose(2, 0, 1).reshape(D, D))),
        "wv": proj_blob(np.ascontiguousarray(
            inputs["L1_attn_b_W_V"].transpose(2, 0, 1).reshape(D, D))),
        "wo": proj_blob(np.ascontiguousarray(
            inputs["L1_attn_b_W_O"].transpose(0, 2, 1).reshape(D, D))),
    }
    return P


def in_map_for(sub_idx_list, P, xblk):
    m = {"xin": xblk}
    seen = set()
    for i in sub_idx_list:
        kind, pfx = SUBLAYERS[i]
        if pfx in seen:
            continue
        seen.add(pfx)
        W = P[pfx]
        if kind.startswith("a"):
            m[f"{pfx}_wq"] = W["wq"]
            m[f"{pfx}_wk"] = W["wk"]
            m[f"{pfx}_wv"] = W["wv"]
            m[f"{pfx}_wo"] = W["wo"]
            if "nrm" in W:
                m[f"{pfx}_nrm"] = W["nrm"]
        else:
            m[f"{pfx}_g"] = W["g_blob"]
            m[f"{pfx}_d"] = W["d_blob"]
            m[f"{pfx}_sg"] = W["sg"]
            m[f"{pfx}_sd"] = W["sd"]
            if "u_blob" in W:
                m[f"{pfx}_u"] = W["u_blob"]
                m[f"{pfx}_su"] = W["su"]
    return m


def run_groups(inputs, groups):
    P = prep_weights(inputs)
    x = np.asarray(inputs["L0_attn_a_hidden_states"], np.float32)
    ncores = x.shape[0]
    xblk = [x_to_blocks(x[c]) for c in range(ncores)]
    for grp in groups:
        nc = build_program(grp)
        in_maps = [in_map_for(grp, P, xblk[c]) for c in range(ncores)]
        res = run_bass_kernel_spmd(nc, in_maps, core_ids=list(range(ncores)))
        xblk = [np.asarray(res.results[c]["xout"], np.float32)
                for c in range(ncores)]
    return np.stack([x_from_blocks(xblk[c]) for c in range(ncores)])


def kernel(**inputs):
    out = run_groups(inputs, [list(range(8))])
    return out.astype(np.float32)


# version 08 (final)
# speedup vs baseline: 1.2064x; 1.0553x over previous
import math
import numpy as np
import concourse.bass as bass
import concourse.mybir as mybir
from concourse.bass_utils import run_bass_kernel_spmd

F32 = mybir.dt.float32
BF16 = mybir.dt.bfloat16
I8 = mybir.dt.int8
I32 = mybir.dt.int32
AF = mybir.ActivationFunctionType
ALU = mybir.AluOpType

T = 2048
D = 2048
DC = 16
H = 16
DH = 128
NB = 4
SPAN = 512
IC = 64
SCL = 1.0 / math.sqrt(DH)
PI = math.pi
NEG = -1.0e30
SBUF_LIMIT = 212863
S = SPAN

ENGS = ("pe", "act", "dve", "pool", "sp")

ALIAS = {
    "big0": ["fuse"], "fuse": ["big0"],
    "slot0": [],
    "slot1E": ["wgu"], "slot1P": ["wgu"], "slot1PT": ["wgu"],
    "wgu": ["slot1E", "slot1P", "slot1PT"],
    "qh": ["wdn"], "kh": ["wdn"], "vt": ["wdn"], "vsb": ["wdn"],
    "qt1": ["wdn"], "qt2": ["wdn"], "osb": ["wdn"],
    "wdn": ["qh", "kh", "vt", "vsb", "qt1", "qt2", "osb"],
    "x_blk": ["rowi"], "rowi": ["x_blk"],
    "wr00": ["wro0"], "wro0": ["wr00"],
    "wr01": ["wro1"], "wro1": ["wr01"],
}


class Prog:
    def __init__(self):
        self.nc = bass.Bass("TRN2", target_bir_lowering=False)
        self.ops = {e: [] for e in ENGS}
        self.sems = {}
        self.cnt = {}
        self.waited = {e: {} for e in ENGS}
        self.rsrc = {}

    def _sem(self, name):
        if name not in self.sems:
            self.sems[name] = self.nc.alloc_semaphore(name=name)
            self.cnt[name] = 0
        return self.sems[name]

    def emit(self, eng, fn, waits=(), inc=None, by=1):
        real = []
        for wtok in waits:
            if wtok is None:
                continue
            n, v = wtok
            self._sem(n)
            if n == eng and eng == "pe":
                continue
            if self.waited[eng].get(n, -1) >= v:
                continue
            self.waited[eng][n] = v
            real.append((n, v))
        tok = None
        if inc is not None:
            self._sem(inc)
            self.cnt[inc] += by
            tok = (inc, self.cnt[inc])
        sems = self.sems

        def run(e, fn=fn, real=tuple(real), inc=inc, by=by):
            for n, v in real:
                e.wait_ge(sems[n], v)
            ins = fn(e)
            if inc is not None:
                ins.then_inc(sems[inc], by)
        self.ops[eng].append(run)
        return tok

    def mm(self, out, lhsT, rhs, start, stop, waits=(), inc=None):
        return self.emit(
            "pe", lambda t, o=out, a=lhsT, b=rhs, s1=start, s2=stop:
            t.matmul(o, a, b, start=s1, stop=s2), waits, inc)

    def tr(self, out, in_, ident, waits=(), inc=None):
        return self.emit(
            "pe", lambda t, o=out, i=in_, d=ident: t.transpose(o, i, d),
            waits, inc)

    def dma(self, q, sem, out, in_, waits=()):
        return self.emit(
            q, lambda e, o=out, i=in_: e.dma_start(out=o, in_=i),
            waits, sem, by=16)

    def _grp(self, n):
        return [n] + ALIAS.get(n, [])

    def wr_waits(self, n):
        out = []
        for n2 in self._grp(n):
            r = self.rsrc.get(n2)
            if r:
                out += r[0] + r[1]
                self.rsrc[n2] = ([], [])
        return out

    def wr_done(self, n, tok, append=False):
        r = self.rsrc.setdefault(n, ([], []))
        if append:
            r[0].append(tok)
        else:
            self.rsrc[n] = ([tok], [])

    def rd_waits(self, n):
        return list(self.rsrc.get(n, ([], []))[0])

    def rd_done(self, n, tok):
        self.rsrc.setdefault(n, ([], []))[1].append(tok)

    def tap(self, name, ap, waits):
        shape = [int(c) for _, c in ap.ap]
        dt_ = ap.tensor.dtype
        d = self.nc.dram_tensor(f"dbg_{name}", shape, dt_,
                                kind="ExternalOutput").ap()
        return self.dma("sp", "xd", d, ap, waits=waits)

    def finalize(self, final_waits):
        sems = self.sems
        for wtok in final_waits:
            if wtok is None:
                continue
            n, v = wtok
            self._sem(n)
            if self.waited["sp"].get(n, -1) >= v:
                continue
            self.waited["sp"][n] = v

            def run(e, n=n, v=v):
                e.wait_ge(sems[n], v)
            self.ops["sp"].append(run)
        nc = self.nc
        with nc.Block() as block:
            @block.tensor
            def _(t):
                for f in self.ops["pe"]:
                    f(t)

            @block.scalar
            def _(sc):
                for f in self.ops["act"]:
                    f(sc)

            @block.vector
            def _(v):
                for f in self.ops["dve"]:
                    f(v)

            @block.gpsimd
            def _(g):
                for f in self.ops["pool"]:
                    f(g)

            @block.sync
            def _(s):
                for f in self.ops["sp"]:
                    f(s)
        return nc


class Bufs:
    pass


def alloc_sbuf(nc):
    B = Bufs()
    base = (nc.sbuf_base + 63) // 64 * 64
    cur = [base]

    def at(name, shape, dtype, offset=None):
        bpe = {F32: 4, BF16: 2, I8: 1, I32: 4}[dtype]
        nbytes = int(np.prod(shape[1:])) * bpe
        if offset is None:
            offset = cur[0]
        cur[0] = max(cur[0], offset + ((nbytes + 63) // 64) * 64)
        h = nc.alloc_sbuf_tensor_at(name, shape, dtype, offset=offset)
        setattr(B, name, h.ap())
        return offset

    at("ones_col", [128, 1], BF16)
    at("ones_row", [1, 128], BF16)
    at("ident", [128, 128], BF16)
    at("identf", [128, 128], F32)
    at("pmat", [128, 128], BF16)
    at("m_tril_neg", [128, 128], F32)
    at("m_triu_neg", [128, 128], F32)
    at("m_tril01", [128, 128], F32)
    at("sign_col", [128, 1], F32)
    at("cful", [128, T], BF16)
    at("sful", [128, T], BF16)
    at("row_f", [1, 6 * S], F32)
    at("row_bf", [1, 2 * S + 64], BF16)
    at("col_f", [128, 32], F32)
    at("ccol", [128, 12], F32)
    at("sqm", [128, S], BF16)
    at("bc2", [128, S], BF16)
    at("sgc", [128, IC], F32)
    at("suc", [128, IC], F32)
    at("sdc", [128, DC], F32)
    at("nrmw", [128, 2], F32)

    xblk = at("x_blk", [128, DC * S], F32)
    at("rowi", [1, T], I32, xblk)
    at("tabf", [128, S], F32, xblk + 16384)
    at("ifr_row", [1, 128], F32, xblk + 20480)
    at("tabi", [128, S], I32, xblk + 24576)
    at("tabg", [128, S], F32, xblk + 26624)
    big0 = at("x_lnT", [128, DC * T], BF16)
    at("fuse", [128, IC * S], BF16, big0)
    slot0 = at("osl", [128, DC * S], BF16)
    at("x_lnB", [128, DC * S], BF16, slot0)
    at("stage", [128, DC * S], BF16, slot0)
    slot1 = at("E", [128, T], F32)
    at("P", [128, T], BF16)
    at("PT", [128, T], BF16)
    at("wgu", [128, 4 * DC * 128], BF16, slot1)
    slot2 = at("qh", [128, T], BF16)
    at("kh", [128, T], BF16)
    at("vt", [128, T], BF16)
    at("vsb", [128, T], BF16)
    at("qt1", [128, T], BF16)
    at("qt2", [128, T], BF16)
    at("osb", [128, T], BF16)
    at("wdn", [128, 2 * IC * 128], BF16, slot2)
    at("wring", [128, 5 * DC * 128], BF16)
    assert cur[0] <= nc.sbuf_top, f"SBUF over budget: {cur[0]} > {nc.sbuf_top}"
    B.end = cur[0]
    return B


def alloc_psum(nc):
    PS = Bufs()
    for i in range(5):
        setattr(PS, f"P{i}", nc.alloc_psum_tensor(f"P{i}", [128, S], F32).ap())
    PS.pstat = nc.alloc_psum_tensor("pstat", [1, S], F32).ap()
    PS.ptrA = nc.alloc_psum_tensor("ptrA", [128, S], BF16).ap()
    PS.ptrB = nc.alloc_psum_tensor("ptrB", [128, S], BF16).ap()
    return PS


def emit_prelude(K, B, PS, need_rope):
    K.rope_tok = None
    tk = None

    def pool(fn):
        nonlocal tk
        tk = K.emit("pool", fn, inc="pool")
        return tk

    pool(lambda g: g.memset(B.ones_col[:, :], 1.0))
    pool(lambda g: g.memset(B.ones_row[:, :], 1.0))
    pool(lambda g: g.memset(B.ident[:, :], 1.0))
    pool(lambda g: g.affine_select(
        B.ident[:, :], B.ident[:, :], [[1, 128]], ALU.is_equal, 0.0,
        base=0, channel_multiplier=-1))
    pool(lambda g: g.memset(B.identf[:, :], 1.0))
    pool(lambda g: g.affine_select(
        B.identf[:, :], B.identf[:, :], [[1, 128]], ALU.is_equal, 0.0,
        base=0, channel_multiplier=-1))
    pool(lambda g: g.memset(B.pmat[:, :], 1.0))
    pool(lambda g: g.affine_select(
        B.pmat[:, 0:64], B.pmat[:, 0:64], [[1, 64]], ALU.is_equal, 0.0,
        base=64, channel_multiplier=-1))
    pool(lambda g: g.affine_select(
        B.pmat[:, 64:128], B.pmat[:, 64:128], [[1, 64]], ALU.is_equal, 0.0,
        base=0, channel_multiplier=-1))
    pool(lambda g: g.memset(B.m_tril_neg[:, :], 0.0))
    pool(lambda g: g.affine_select(
        B.m_tril_neg[:, :], B.m_tril_neg[:, :], [[-1, 128]], ALU.is_ge, NEG,
        base=0, channel_multiplier=1))
    pool(lambda g: g.memset(B.m_triu_neg[:, :], 0.0))
    pool(lambda g: g.affine_select(
        B.m_triu_neg[:, :], B.m_triu_neg[:, :], [[1, 128]], ALU.is_ge, NEG,
        base=-1, channel_multiplier=-1))
    pool(lambda g: g.memset(B.m_tril01[:, :], 1.0))
    pool(lambda g: g.affine_select(
        B.m_tril01[:, :], B.m_tril01[:, :], [[-1, 128]], ALU.is_ge, 0.0,
        base=0, channel_multiplier=1))
    pool(lambda g: g.memset(B.ccol[:, 0:1], SCL))
    pool(lambda g: g.memset(B.ccol[:, 1:2], -PI))
    pool(lambda g: g.memset(B.ccol[:, 2:3], -2.0 * math.log(10000.0) / DH))
    pool(lambda g: g.memset(B.ccol[:, 3:4], 1e-5))
    pool(lambda g: g.memset(B.ccol[:, 4:5], 1.0 / DH))
    pool(lambda g: g.memset(B.ccol[:, 5:6], 1e-6))
    pool(lambda g: g.memset(B.ccol[:, 6:7], 2.0 * PI))
    pool(lambda g: g.memset(B.ccol[:, 8:9], -0.5))
    pool(lambda g: g.memset(B.sign_col[:, :], 1.0))
    tsign = pool(lambda g: g.memset(B.sign_col[0:64, :], -1.0))
    K.consts_tok = tsign

    if need_rope:
        tp = pool(lambda g: g.iota(B.rowi[:, :], [[1, T]], base=0,
                                   channel_multiplier=0))
        tpos = K.emit("dve", lambda v: v.tensor_copy(
            B.row_f[:, 0:T], B.rowi[:, :]), waits=[tp], inc="dve")
        K.rd_done("rowi", tpos)
        tpc = pool(lambda g: g.iota(B.tabi[:, 0:1], [[0, 1]], base=0,
                                    channel_multiplier=1))
        tc1 = K.emit("dve", lambda v: v.tensor_copy(
            B.ccol[:, 7:8], B.tabi[:, 0:1]), waits=[tpc, K.consts_tok],
            inc="dve")
        tc2 = K.emit("dve", lambda v: v.tensor_single_scalar(
            B.ccol[64:128, 7:8], B.ccol[64:128, 7:8], -64.0, ALU.add),
            waits=[tc1], inc="dve")
        tc3 = K.emit("act", lambda sc: sc.activation(
            B.ccol[:, 7:8], B.ccol[:, 7:8], AF.Exp,
            bias=0.0, scale=B.ccol[:, 2:3]), waits=[tc2], inc="act")
        inv2pi = 1.0 / (2.0 * PI)
        prev_td = None
        for s in range(NB):
            sl = slice(s * S, (s + 1) * S)
            tpi = K.emit("pool", lambda g, s=s: g.iota(
                B.tabi[:, :], [[1, S]], base=s * S, channel_multiplier=0),
                waits=[tc1, prev_td], inc="pool")
            tg1 = K.emit("dve", lambda v: v.tensor_copy(
                B.tabg[:, :], B.tabi[:, :]), waits=[tpi], inc="dve")
            tang = K.emit("dve", lambda v: v.tensor_single_scalar(
                B.tabg[:, :], B.tabg[:, :], B.ccol[:, 7:8], ALU.mult),
                waits=[tg1, tc3], inc="dve")

            def redsin(dstap, shift, wprev):
                t1 = K.emit("dve", lambda v, shift=shift: v.tensor_scalar(
                    B.tabf[:, :], B.tabg[:, :], shift, inv2pi,
                    ALU.add, ALU.mult), waits=[tang, wprev], inc="dve")
                t2 = K.emit("dve", lambda v: v.tensor_copy(
                    B.tabi[:, :], B.tabf[:, :]), waits=[t1], inc="dve")
                t3 = K.emit("dve", lambda v: v.tensor_tensor(
                    B.tabf[:, :], B.tabf[:, :], B.tabi[:, :], ALU.subtract),
                    waits=[t2], inc="dve")
                t4 = K.emit("act", lambda sc, dstap=dstap: sc.activation(
                    dstap, B.tabf[:, :], AF.Sin, bias=0.0,
                    scale=B.ccol[:, 6:7]), waits=[t3], inc="act")
                return t4
            tb = redsin(B.sful[:, sl], 0.0, None)
            td = redsin(B.cful[:, sl], 0.5 * PI, tb)
            prev_td = td
            tsg_ = K.emit("dve", lambda v, sl=sl: v.tensor_single_scalar(
                B.sful[:, sl], B.sful[:, sl], B.sign_col[:, :], ALU.mult),
                waits=[tb, tsign], inc="dve")
            K.rope_tok = td
            K.rd_done("x_blk", td)
            K.rd_done("x_blk", tsg_)


def emit_ln_block(K, B, PS, tok_x, dst_slices, dst_res):
    w0 = K.wr_waits("slot0")
    t_cp = K.emit("act", lambda sc: sc.activation(
        B.stage[:, :], B.x_blk[:, :], AF.Copy, bias=0.0, scale=1.0),
        waits=[tok_x] + w0, inc="act")
    K.rd_done("x_blk", t_cp)
    mt = None
    wps = K.wr_waits("pstat")
    for c in range(DC):
        mt = K.mm(PS.pstat[:, :], B.ones_col[:, :],
                  B.stage[:, c * S:(c + 1) * S],
                  start=(c == 0), stop=(c == DC - 1),
                  waits=([t_cp, K.consts_tok] + wps) if c == 0 else [t_cp],
                  inc="pe" if c == DC - 1 else None)
    tB = K.emit("dve", lambda v: v.tensor_copy(
        B.row_f[:, S:2 * S], PS.pstat[:, :]), waits=[mt], inc="dve")
    t_sq1 = K.emit("act", lambda sc: sc.activation(
        B.stage[:, :], B.x_blk[:, :], AF.Square, bias=0.0, scale=1.0),
        waits=[mt], inc="act")
    K.rd_done("x_blk", t_sq1)
    for c in range(DC):
        mt = K.mm(PS.pstat[:, :], B.ones_col[:, :],
                  B.stage[:, c * S:(c + 1) * S],
                  start=(c == 0), stop=(c == DC - 1),
                  waits=[t_sq1, tB], inc="pe" if c == DC - 1 else None)
    K.wr_done("slot0", mt)
    tA = K.emit("dve", lambda v: v.tensor_copy(B.row_f[:, 0:S], PS.pstat[:, :]),
                waits=[mt], inc="dve")
    K.wr_done("pstat", tA)
    t2 = K.emit("dve", lambda v: v.tensor_single_scalar(
        B.row_f[:, 2 * S:3 * S], B.row_f[:, S:2 * S], 1.0 / D, ALU.mult),
        waits=[tB], inc="dve")
    t3 = K.emit("dve", lambda v: v.tensor_tensor(
        B.row_f[:, 3 * S:4 * S], B.row_f[:, 2 * S:3 * S],
        B.row_f[:, 2 * S:3 * S], ALU.mult), waits=[t2], inc="dve")
    t4 = K.emit("dve", lambda v: v.scalar_tensor_tensor(
        B.row_f[:, 3 * S:4 * S], B.row_f[:, 0:S], 1.0 / D,
        B.row_f[:, 3 * S:4 * S], ALU.mult, ALU.subtract), waits=[t3, tA], inc="dve")
    t5 = K.emit("act", lambda sc: sc.activation(
        B.row_f[:, 0:S], B.row_f[:, 3 * S:4 * S], AF.Ln,
        bias=B.ccol[0:1, 3:4], scale=1.0), waits=[t4], inc="act")
    t6 = K.emit("act", lambda sc: sc.activation(
        B.row_f[:, 0:S], B.row_f[:, 0:S], AF.Exp,
        bias=0.0, scale=B.ccol[0:1, 8:9]), waits=[t5], inc="act")
    t7 = K.emit("dve", lambda v: v.tensor_copy(B.row_bf[:, 0:S],
                                               B.row_f[:, 0:S]),
                waits=[t6], inc="dve")
    t8 = K.emit("dve", lambda v: v.tensor_tensor(
        B.row_f[:, S:2 * S], B.row_f[:, 2 * S:3 * S], B.row_f[:, 0:S],
        ALU.mult), waits=[t6], inc="dve")
    t9 = K.emit("dve", lambda v: v.tensor_single_scalar(
        B.row_bf[:, S:2 * S], B.row_f[:, S:2 * S], -1.0, ALU.mult),
        waits=[t8], inc="dve")
    tb1 = K.mm(PS.P3[:, :], B.ones_row[:, :], B.row_bf[:, 0:S], True, True,
               waits=[t7] + K.wr_waits("P3"), inc="pe")
    tb2 = K.mm(PS.P4[:, :], B.ones_row[:, :], B.row_bf[:, S:2 * S], True, True,
               waits=[t9] + K.wr_waits("P4"), inc="pe")
    out = []
    wdst = K.wr_waits(dst_res) if dst_res else []
    for c in range(DC):
        dsl = dst_slices[c]
        ta = K.emit("dve", lambda v, c=c, dsl=dsl: v.tensor_tensor(
            dsl, B.x_blk[:, c * S:(c + 1) * S], PS.P3[:, :], ALU.mult),
            waits=[tb1] + (wdst if c == 0 else []), inc="dve")
        tbb = K.emit("dve", lambda v, dsl=dsl: v.tensor_tensor(
            dsl, dsl, PS.P4[:, :], ALU.add), waits=[tb2, ta], inc="dve")
        K.rd_done("x_blk", tbb)
        out.append(tbb)
    K.wr_done("P3", out[-1])
    K.wr_done("P4", out[-1])
    return out


def emit_attention(K, B, PS, kind, Wap, xsrc, xdst):
    nc = K.nc
    windowed = kind in ("a0a", "a0b")
    use_rope = kind in ("a0a", "a0b")
    use_bound = kind in ("a0a", "a0b", "a1b")
    ost = Wap["ost"]

    tnrm = None
    if kind in ("a0b", "a1a"):
        tnrm = K.dma("sp", "xd", B.nrmw[:, :], Wap["nrm"],
                     waits=K.wr_waits("nrmw"))

    ln_toks = [[None] * DC for _ in range(NB)]
    wbig = K.wr_waits("big0")
    for b in range(NB):
        tok_x = K.dma("sp", "xd", B.x_blk[:, :], xsrc[b],
                      waits=K.wr_waits("x_blk") + (wbig if b == 0 else []))
        if getattr(K, "dbg", False) and b == 3 and not getattr(K, "_tapped", 0):
            K._tapped = 1
            K.tap("xblk3", B.x_blk[:, :], [tok_x])
        dst = [B.x_lnT[:, c * T + b * S: c * T + b * S + S] for c in range(DC)]
        toks = emit_ln_block(K, B, PS, tok_x, dst, None)
        ln_toks[b] = toks
        for tk_ in toks:
            K.wr_done("big0", tk_, append=True)

    if getattr(K, "ln_only", False):
        tst = K.dma("sp", "xd", xdst[3], B.x_blk[:, :],
                    waits=ln_toks[3])
        K.rd_done("x_blk", tst)
        return tst

    otok_prev = [None] * H
    for h in range(H):
        stat_toks = []

        def wslot(idx, par):
            o = (idx * 2 + par) * DC * 128 if idx < 2 else 4 * DC * 128
            return B.wring[:, o:o + DC * 128]
        par = h % 2
        res_q, res_k, res_v = f"wr0{par}", f"wr1{par}", "wr20"
        tkw = K.dma("pool", "wd", wslot(1, par), Wap["wk"][h],
                    waits=K.wr_waits(res_k))
        tqw = K.dma("pool", "wd", wslot(0, par), Wap["wq"][h],
                    waits=K.wr_waits(res_q))
        tvw = K.dma("pool", "wd", wslot(2, par), Wap["wv"][h],
                    waits=K.wr_waits(res_v))
        for _ in range(3):
            if getattr(K, "bg", None):
                K.bg.pop(0)()

        mats = [("k", tkw, wslot(1, par), res_k, B.kh, "kh"),
                ("q", tqw, wslot(0, par), res_q, B.qh, "qh"),
                ("v", tvw, wslot(2, par), res_v, B.vt, "vt")]
        seq = [(mi, s) for mi in range(3) for s in range(NB)]

        def mm_group(gi):
            mi, s = seq[gi]
            mat, wtok, wap, res_w, dst, dres = mats[mi]
            bank = ["P0", "P1", "P2"][gi % 3]
            Pa = getattr(PS, bank)
            wb = K.wr_waits(bank)
            wdst = K.wr_waits(dres) if s == 0 else []
            mt = None
            for c in range(DC):
                mt = K.mm(Pa[:, :], wap[:, c * 128:(c + 1) * 128],
                          B.x_lnT[:, c * T + s * S: c * T + s * S + S],
                          start=(c == 0), stop=(c == DC - 1),
                          waits=[wtok, ln_toks[s][c]] +
                          (wb + wdst if c == 0 else []),
                          inc="pe" if c == DC - 1 else None)
            K.rd_done(res_w, mt)
            K.rd_done("big0", mt)
            return {"gi": gi, "mat": mat, "s": s, "bank": bank, "Pa": Pa,
                    "mt": mt, "dst": dst, "dres": dres,
                    "sl": slice(s * S, (s + 1) * S)}

        def seg_rms(gi):
            return (4 + gi % 2) * S

        def do_rope(st, src_ap, extra):
            sl, dst, dres, s = st["sl"], st["dst"], st["dres"], st["s"]
            wsw = K.wr_waits("P4")
            tsw = K.mm(PS.P4[:, :], B.pmat[:, :], src_ap, True, True,
                       waits=list(extra) + wsw + [K.consts_tok], inc="pe")
            t_a = K.emit("dve", lambda v: v.tensor_tensor(
                B.qt2[:, sl], PS.P4[:, :], B.sful[:, sl], ALU.mult),
                waits=[tsw] + K.wr_waits("qt2"), inc="dve")
            K.wr_done("P4", t_a)
            t_b = K.emit("dve", lambda v: v.tensor_tensor(
                dst[:, sl], src_ap, B.cful[:, sl], ALU.mult),
                waits=list(extra) + [K.rope_tok] +
                (K.wr_waits(dres) if s == 0 else []), inc="dve")
            t_c = K.emit("dve", lambda v: v.tensor_tensor(
                dst[:, sl], dst[:, sl], B.qt2[:, sl], ALU.add),
                waits=[t_a, t_b], inc="dve")
            K.wr_done("qt2", t_c)
            K.wr_done(dres, t_c, append=(s > 0))
            return t_c

        def sqstat_st(src_ap, wtok, seg):
            wsq = K.wr_waits("sqm")
            ts_ = K.emit("act", lambda sc: sc.activation(
                B.sqm[:, :], src_ap, AF.Square, bias=0.0, scale=1.0),
                waits=[wtok] + wsq, inc="act")
            mt2 = K.mm(PS.pstat[:, :], B.ones_col[:, :], B.sqm[:, :],
                       True, True, waits=[ts_] + K.wr_waits("pstat"),
                       inc="pe")
            K.wr_done("sqm", mt2)
            te = K.emit("dve", lambda v, seg=seg: v.tensor_copy(
                B.row_f[:, seg:seg + S], PS.pstat[:, :]),
                waits=[mt2], inc="dve")
            K.wr_done("pstat", te)
            stat_toks.append(te)
            return ts_, te

        def stage1(st):
            mat, s, Pa, bank, mt = st["mat"], st["s"], st["Pa"], st["bank"], st["mt"]
            sl, dst, dres = st["sl"], st["dst"], st["dres"]
            if mat == "v" or kind == "a1b":
                tcp = K.emit("act", lambda sc, Pa=Pa, dst=dst, sl=sl:
                             sc.activation(
                    dst[:, sl], Pa[:, :], AF.Copy, bias=0.0, scale=1.0),
                    waits=[mt] + (K.wr_waits(dres) if s == 0 else []),
                    inc="act")
                K.wr_done(dres, tcp, append=(s > 0))
                K.rd_done(bank, tcp)
                if mat != "v" and kind == "a1b":
                    ts_, te = sqstat_st(Pa[:, :], mt, s * S)
                    K.rd_done(bank, ts_)
                return
            if kind == "a0a":
                ts_, te = sqstat_st(Pa[:, :], mt, s * S)
                K.rd_done(bank, ts_)
                tcp = K.emit("act", lambda sc, Pa=Pa, sl=sl: sc.activation(
                    B.qt1[:, sl], Pa[:, :], AF.Copy, bias=0.0, scale=1.0),
                    waits=[mt], inc="act")
                K.rd_done(bank, tcp)
                st["tcp"] = tcp
            elif kind == "a0b":
                seg = seg_rms(st["gi"])
                wsq = K.wr_waits("sqm")
                ts_ = K.emit("act", lambda sc, Pa=Pa: sc.activation(
                    B.sqm[:, :], Pa[:, :], AF.Square, bias=0.0, scale=1.0),
                    waits=[mt] + wsq, inc="act")
                mt2 = K.mm(PS.pstat[:, :], B.ones_col[:, :], B.sqm[:, :],
                           True, True, waits=[ts_] + K.wr_waits("pstat"),
                           inc="pe")
                K.wr_done("sqm", mt2)
                te = K.emit("dve", lambda v, seg=seg: v.tensor_copy(
                    B.row_f[:, seg:seg + S], PS.pstat[:, :]),
                    waits=[mt2], inc="dve")
                K.wr_done("pstat", te)
                t5_ = K.emit("act", lambda sc, seg=seg: sc.activation(
                    B.row_f[:, seg:seg + S], B.row_f[:, seg:seg + S],
                    AF.Ln, bias=B.ccol[0:1, 5:6], scale=B.ccol[0:1, 4:5]),
                    waits=[te], inc="act")
                t6_ = K.emit("act", lambda sc, seg=seg: sc.activation(
                    B.row_f[:, seg:seg + S], B.row_f[:, seg:seg + S],
                    AF.Exp, bias=0.0, scale=B.ccol[0:1, 8:9]),
                    waits=[t5_], inc="act")
                rbseg = (st["gi"] % 2) * S
                t7_ = K.emit("dve", lambda v, seg=seg, rbseg=rbseg:
                             v.tensor_copy(
                    B.row_bf[:, rbseg:rbseg + S], B.row_f[:, seg:seg + S]),
                    waits=[t6_], inc="dve")
                st["t7"] = t7_
                st["rbseg"] = rbseg
            elif kind == "a1a":
                tu = K.emit("dve", lambda v, Pa=Pa, sl=sl: v.tensor_scalar(
                    B.qt1[:, sl], Pa[:, :], B.nrmw[:, 0:1], B.nrmw[:, 1:2],
                    ALU.mult, ALU.add),
                    waits=[tnrm, mt] + (K.wr_waits("qt1") if s == 0 else []),
                    inc="dve")
                K.wr_done("qt1", tu, append=(s > 0))
                K.rd_done(bank, tu)
                ts_, teA = sqstat_st(B.qt1[:, sl], tu, seg_rms(st["gi"]))
                st["tu"] = tu
                st["teA"] = teA

        def stage2(st):
            mat, s, Pa, bank = st["mat"], st["s"], st["Pa"], st["bank"]
            sl, dst, dres = st["sl"], st["dst"], st["dres"]
            if mat == "v" or kind == "a1b":
                return
            if kind == "a0a":
                do_rope(st, B.qt1[:, sl], extra=(st["tcp"],))
            elif kind == "a0b":
                tb_ = K.mm(PS.P3[:, :], B.ones_row[:, :],
                           B.row_bf[0:1, st["rbseg"]:st["rbseg"] + S],
                           True, True, waits=[st["t7"]] + K.wr_waits("P3"),
                           inc="pe")
                tbc = K.emit("act", lambda sc: sc.activation(
                    B.bc2[:, :], PS.P3[:, :], AF.Copy, bias=0.0, scale=1.0),
                    waits=[tb_] + K.wr_waits("bc2"), inc="act")
                K.wr_done("P3", tbc)
                mi = 0 if mat == "q" else 1
                tn = K.emit("dve", lambda v, Pa=Pa, sl=sl, mi=mi:
                            v.scalar_tensor_tensor(
                    B.qt1[:, sl], Pa[:, :], B.nrmw[:, mi:mi + 1],
                    B.bc2[:, :], ALU.mult, ALU.mult),
                    waits=[tnrm, tbc, st["mt"]] +
                    (K.wr_waits("qt1") if s == 0 else []), inc="dve")
                K.wr_done("qt1", tn, append=(s > 0))
                K.wr_done("bc2", tn)
                K.rd_done(bank, tn)
                st["tn"] = tn
            elif kind == "a1a":
                mtB = K.mm(PS.pstat[:, :], B.ones_col[:, :], B.qt1[:, sl],
                           True, True, waits=[st["tu"]] + K.wr_waits("pstat"),
                           inc="pe")
                teB = K.emit("dve", lambda v: v.tensor_copy(
                    B.row_f[:, 0:S], PS.pstat[:, :]), waits=[mtB], inc="dve")
                K.wr_done("pstat", teB)
                segA = seg_rms(st["gi"])
                r1 = K.emit("dve", lambda v: v.tensor_single_scalar(
                    B.row_f[:, 0:S], B.row_f[:, 0:S], 1.0 / DH, ALU.mult),
                    waits=[teB], inc="dve")
                r2 = K.emit("dve", lambda v: v.tensor_tensor(
                    B.row_f[:, S:2 * S], B.row_f[:, 0:S], B.row_f[:, 0:S],
                    ALU.mult), waits=[r1], inc="dve")
                r3 = K.emit("dve", lambda v, segA=segA: v.scalar_tensor_tensor(
                    B.row_f[:, S:2 * S], B.row_f[:, segA:segA + S], 1.0 / DH,
                    B.row_f[:, S:2 * S], ALU.mult, ALU.subtract),
                    waits=[r2, st["teA"]], inc="dve")
                r4 = K.emit("act", lambda sc: sc.activation(
                    B.row_f[:, S:2 * S], B.row_f[:, S:2 * S], AF.Ln,
                    bias=B.ccol[0:1, 3:4], scale=1.0), waits=[r3], inc="act")
                r5 = K.emit("act", lambda sc: sc.activation(
                    B.row_f[:, S:2 * S], B.row_f[:, S:2 * S], AF.Exp,
                    bias=0.0, scale=B.ccol[0:1, 8:9]), waits=[r4], inc="act")
                qs = SCL if mat == "q" else 1.0
                r6 = K.emit("dve", lambda v, qs=qs: v.tensor_single_scalar(
                    B.row_bf[:, 0:S], B.row_f[:, S:2 * S], qs, ALU.mult),
                    waits=[r5], inc="dve")
                r7 = K.emit("dve", lambda v: v.tensor_tensor(
                    B.row_f[:, 0:S], B.row_f[:, 0:S], B.row_f[:, S:2 * S],
                    ALU.mult), waits=[r5, r2], inc="dve")
                r8 = K.emit("dve", lambda v, qs=-qs: v.tensor_single_scalar(
                    B.row_bf[:, S:2 * S], B.row_f[:, 0:S], qs, ALU.mult),
                    waits=[r7], inc="dve")
                tb1 = K.mm(PS.P3[:, :], B.ones_row[:, :], B.row_bf[0:1, 0:S],
                           True, True, waits=[r6] + K.wr_waits("P3"),
                           inc="pe")
                tb2 = K.mm(PS.P4[:, :], B.ones_row[:, :],
                           B.row_bf[0:1, S:2 * S], True, True,
                           waits=[r8] + K.wr_waits("P4"), inc="pe")
                tbc = K.emit("act", lambda sc: sc.activation(
                    B.bc2[:, :], PS.P3[:, :], AF.Copy, bias=0.0, scale=1.0),
                    waits=[tb1] + K.wr_waits("bc2"), inc="act")
                K.wr_done("P3", tbc)
                w1 = K.emit("dve", lambda v, sl=sl, dst=dst: v.tensor_tensor(
                    dst[:, sl], B.qt1[:, sl], B.bc2[:, :], ALU.mult),
                    waits=[tbc, st["tu"]] +
                    (K.wr_waits(dres) if s == 0 else []), inc="dve")
                K.wr_done("bc2", w1)
                w2 = K.emit("dve", lambda v, sl=sl, dst=dst: v.tensor_tensor(
                    dst[:, sl], dst[:, sl], PS.P4[:, :], ALU.add),
                    waits=[tb2, w1], inc="dve")
                K.wr_done("P4", w2)
                K.wr_done(dres, w2, append=(s > 0))

        def stage3(st):
            mat, sl = st["mat"], st["sl"]
            if mat == "v" or kind in ("a1b", "a1a", "a0a"):
                return
            ts2, te2 = sqstat_st(B.qt1[:, sl], st["tn"], st["s"] * S)
            do_rope(st, B.qt1[:, sl], extra=(st["tn"],))

        bound_stage = 3 if kind == "a0b" else 1

        def emit_bound_k():
            tm1 = K.emit("dve", lambda v: v.reduce_max(
                B.col_f[0:1, 28:29], B.row_f[:, 0:4 * S],
                mybir.AxisListType.X), waits=stat_toks[-4:], inc="dve")
            tm2 = K.emit("dve", lambda v: v.tensor_single_scalar(
                B.col_f[0:1, 29:30], B.col_f[0:1, 28:29], 1.0 / DH, ALU.mult),
                waits=[tm1], inc="dve")
            tm3 = K.emit("dve", lambda v: v.tensor_copy(
                B.row_bf[0:1, 2 * S:2 * S + 1], B.col_f[0:1, 29:30]),
                waits=[tm2], inc="dve")
            tb_ = K.mm(PS.P3[:, 0:1], B.ones_row[:, :],
                       B.row_bf[0:1, 2 * S:2 * S + 1], True, True,
                       waits=[tm3] + K.wr_waits("P3"), inc="pe")
            tk2 = K.emit("act", lambda sc: sc.activation(
                B.col_f[:, 30:31], PS.P3[:, 0:1], AF.Copy, bias=0.0,
                scale=1.0), waits=[tb_], inc="act")
            K.wr_done("P3", tk2)
            return tk2

        def emit_bound_q(tk2):
            wtr = K.wr_waits("P3")
            ttr = None
            for i in range(NT_TILES):
                ttr = K.mm(PS.P3[:, i:i + 1],
                           B.row_f[0:1, i * 128:(i + 1) * 128],
                           B.identf[0:1, 0:1], True, True,
                           waits=(wtr + [stat_toks[-1], K.consts_tok])
                           if i == 0 else [],
                           inc="pe" if i == NT_TILES - 1 else None)
            tcv = K.emit("act", lambda sc: sc.activation(
                B.col_f[:, 0:NT_TILES], PS.P3[:, 0:NT_TILES], AF.Copy,
                bias=0.0, scale=1.0), waits=[ttr], inc="act")
            K.wr_done("P3", tcv)
            tnb1 = K.emit("act", lambda sc: sc.activation(
                B.col_f[:, 0:NT_TILES], B.col_f[:, 0:NT_TILES], AF.Sqrt,
                bias=0.0, scale=B.col_f[:, 30:31]),
                waits=[tcv, tk2], inc="act")
            return K.emit("dve", lambda v: v.tensor_single_scalar(
                B.col_f[:, 0:NT_TILES], B.col_f[:, 0:NT_TILES], -1.0,
                ALU.mult), waits=[tnb1], inc="dve")

        tk2 = None
        tnb = None
        pend = []
        for gi in range(len(seq) + 3):
            if gi < len(seq):
                pend.append(mm_group(gi))
            if gi >= 1 and gi - 1 < len(seq):
                stage1(pend[gi - 1])
                if use_bound and bound_stage == 1:
                    if seq[gi - 1] == (0, NB - 1):
                        tk2 = emit_bound_k()
                    if seq[gi - 1] == (1, NB - 1):
                        tnb = emit_bound_q(tk2)
            if gi >= 2 and gi - 2 < len(seq):
                stage2(pend[gi - 2])
            if gi >= 3 and gi - 3 < len(seq):
                stage3(pend[gi - 3])
                if use_bound and bound_stage == 3:
                    if seq[gi - 3] == (0, NB - 1):
                        tk2 = emit_bound_k()
                    if seq[gi - 3] == (1, NB - 1):
                        tnb = emit_bound_q(tk2)

        vtk = None
        for g4 in range(4):
            ptr = PS.ptrA if g4 % 2 == 0 else PS.ptrB
            pres = "ptrA" if g4 % 2 == 0 else "ptrB"
            wp = K.wr_waits(pres)
            tt_ = None
            for k4 in range(4):
                tt = g4 * 4 + k4
                tt_ = K.tr(ptr[:, k4 * 128:(k4 + 1) * 128],
                           B.vt[:, tt * 128:(tt + 1) * 128], B.ident[:, :],
                           waits=(wp + [K.consts_tok] +
                                  K.rd_waits("vt") if k4 == 0 else []),
                           inc="pe" if k4 == 3 else None)
            vtk = K.emit("dve", lambda v, g4=g4, ptr=ptr: v.tensor_copy(
                B.vsb[:, g4 * S:(g4 + 1) * S], ptr[:, :]),
                waits=[tt_] + (K.wr_waits("vsb") if g4 == 0 else []),
                inc="dve")
            K.wr_done(pres, vtk)
            K.wr_done("vsb", vtk, append=(g4 > 0))

        if getattr(K, "proj_only", False):
            continue
        def emit_ptpv(st):
            i, jlo, nj, tp = st
            ptk = []
            for g4 in range((nj + 3) // 4):
                ptr = PS.ptrA if g4 % 2 == 0 else PS.ptrB
                pres = "ptrA" if g4 % 2 == 0 else "ptrB"
                wp = K.wr_waits(pres)
                n4 = min(4, nj - g4 * 4)
                tt_ = None
                for k4 in range(n4):
                    jj = g4 * 4 + k4
                    tt_ = K.tr(ptr[:, k4 * 128:(k4 + 1) * 128],
                               B.P[:, jj * 128:(jj + 1) * 128], B.ident[:, :],
                               waits=[tp] + (wp if k4 == 0 else []),
                               inc="pe" if k4 == n4 - 1 else None)
                tev = K.emit("dve", lambda v, g4=g4, n4=n4, ptr=ptr:
                             v.tensor_copy(
                    B.PT[:, g4 * S:g4 * S + n4 * 128], ptr[:, 0:n4 * 128]),
                    waits=[tt_] + (K.wr_waits("slot1PT") if g4 == 0 else []),
                    inc="dve")
                K.wr_done(pres, tev)
                K.wr_done("slot1PT", tev, append=(g4 > 0))
                ptk.append(tev)
            w4 = K.wr_waits("P4")
            mo = None
            for jj in range(nj):
                j = jlo + jj
                mo = K.mm(PS.P4[:, 0:128], B.vsb[:, j * 128:(j + 1) * 128],
                          B.PT[:, jj * 128:(jj + 1) * 128],
                          start=(jj == 0), stop=(jj == nj - 1),
                          waits=[ptk[jj // 4], vtk] + (w4 if jj == 0 else []),
                          inc="pe" if jj == nj - 1 else None)
            K.rd_done("slot1PT", mo)
            K.rd_done("vsb", mo)
            to = K.emit("act", lambda sc, i=i: sc.activation(
                B.osb[:, i * 128:(i + 1) * 128], PS.P4[:, 0:128], AF.Copy,
                bias=0.0, scale=1.0),
                waits=[mo] + (K.wr_waits("osb") if i == 0 else []), inc="act")
            K.wr_done("P4", to)
            K.wr_done("osb", to, append=(i > 0))

        prev_tile = None
        for i in range(NT_TILES):
            jlo = max(0, i - WTILES) if windowed else 0
            nj = i - jlo + 1
            ncols = nj * 128
            nsp = (ncols + S - 1) // S
            par2 = i % 2
            banks = ["P0", "P1"] if par2 == 0 else ["P2", "P3"]
            dp0 = 16 if par2 == 0 else 22
            dnc, drc = (20, 21) if par2 == 0 else (26, 27)
            dloc = (i - jlo) * 128
            sd, off = dloc // S, dloc % S
            etoks = []
            wE = K.wr_waits("slot1E") if kind != "a1a" else []
            wP = K.wr_waits("slot1P")
            first_grp_toks = None
            for sp_i in range(nsp):
                lc = sp_i * S
                w = min(S, ncols - lc)
                bname = banks[sp_i % 2]
                Pa = getattr(PS, bname)
                wb = K.wr_waits(bname)
                extra = []
                if sp_i >= 2:
                    extra = [etoks[sp_i - 2]]
                mt = K.mm(Pa[:, 0:w], B.qh[:, i * 128:(i + 1) * 128],
                          B.kh[:, jlo * 128 + lc: jlo * 128 + lc + w],
                          True, True,
                          waits=wb + extra + (K.rd_waits("qh") +
                                              K.rd_waits("kh")
                                              if sp_i == 0 else []), inc="pe")
                mtk = mt
                if sp_i == sd:
                    if kind == "a1a":
                        mtk = K.emit("dve", lambda v, Pa=Pa, off=off:
                                     v.tensor_tensor(
                            Pa[:, off:off + 128], Pa[:, off:off + 128],
                            B.m_tril01[:, :], ALU.mult),
                            waits=[mt, K.consts_tok], inc="dve")
                    else:
                        mtk = K.emit("dve", lambda v, Pa=Pa, off=off:
                                     v.tensor_tensor(
                            Pa[:, off:off + 128], Pa[:, off:off + 128],
                            B.m_tril_neg[:, :], ALU.add),
                            waits=[mt, K.consts_tok], inc="dve")
                if sp_i == 0 and windowed and i >= WTILES:
                    mtk = K.emit("dve", lambda v, Pa=Pa: v.tensor_tensor(
                        Pa[:, 0:128], Pa[:, 0:128],
                        B.m_triu_neg[:, :], ALU.add),
                        waits=[mt, mtk, K.consts_tok], inc="dve")
                if kind == "a1a":
                    te = K.emit("act", lambda sc, Pa=Pa, lc=lc, w=w, sp_i=sp_i,
                                dp0=dp0: sc.activation(
                        B.P[:, lc:lc + w], Pa[:, 0:w], AF.Square,
                        bias=0.0, scale=1.0,
                        accum_out=B.col_f[:, dp0 + sp_i:dp0 + sp_i + 1]),
                        waits=[mtk] + wP, inc="act")
                else:
                    te = K.emit("act", lambda sc, Pa=Pa, lc=lc, w=w, sp_i=sp_i,
                                dp0=dp0, i=i: sc.activation(
                        B.E[:, lc:lc + w], Pa[:, 0:w], AF.Exp,
                        bias=B.col_f[:, i:i + 1], scale=B.ccol[:, 0:1],
                        accum_out=B.col_f[:, dp0 + sp_i:dp0 + sp_i + 1]),
                        waits=[mtk, tnb, K.consts_tok] + wE, inc="act")
                etoks.append(te)
                K.wr_done(bname, te)
                if sp_i == 1 and prev_tile is not None:
                    emit_ptpv(prev_tile)
                    prev_tile = None
            K.rd_done("qh", mt)
            K.rd_done("kh", mt)
            if prev_tile is not None:
                emit_ptpv(prev_tile)
                prev_tile = None
            td1 = K.emit("dve", lambda v, nsp=nsp, dp0=dp0, dnc=dnc:
                         v.reduce_sum(
                B.col_f[:, dnc:dnc + 1], B.col_f[:, dp0:dp0 + nsp],
                mybir.AxisListType.X), waits=etoks, inc="dve")
            epsv = 1e-5 if kind == "a1a" else 1e-30
            td2 = K.emit("dve", lambda v, e=epsv, dnc=dnc:
                         v.tensor_single_scalar(
                B.col_f[:, dnc:dnc + 1], B.col_f[:, dnc:dnc + 1], e, ALU.add),
                waits=[td1], inc="dve")
            td3 = K.emit("dve", lambda v, dnc=dnc, drc=drc: v.reciprocal(
                B.col_f[:, drc:drc + 1], B.col_f[:, dnc:dnc + 1]),
                waits=[td2], inc="dve")
            if kind == "a1a":
                tp = K.emit("dve", lambda v, nc_=ncols, drc=drc:
                            v.tensor_single_scalar(
                    B.P[:, 0:nc_], B.P[:, 0:nc_], B.col_f[:, drc:drc + 1],
                    ALU.mult), waits=[td3] + etoks + wP, inc="dve")
            else:
                tp = K.emit("dve", lambda v, nc_=ncols, drc=drc:
                            v.tensor_single_scalar(
                    B.P[:, 0:nc_], B.E[:, 0:nc_], B.col_f[:, drc:drc + 1],
                    ALU.mult), waits=[td3] + etoks + wP, inc="dve")
                for te in etoks:
                    K.wr_done("slot1E", tp, append=True)
            K.wr_done("slot1P", tp)
            prev_tile = (i, jlo, nj, tp)
        emit_ptpv(prev_tile)
        prev_tile = None

        if getattr(K, "dbg", False) and h == 0:
            K.tap("qh", B.qh[:, :], K.rd_waits("qh"))
            K.tap("kh", B.kh[:, :], K.rd_waits("kh"))
            K.tap("vsb", B.vsb[:, :], K.rd_waits("vsb"))
            K.tap("rowf", B.row_f[:, :], [self_last_dve(K)])
            K.tap("colf", B.col_f[:, :], [self_last_dve(K), self_last_act(K)])
            K.tap("E", B.E[:, :], [self_last_act(K)])
            K.tap("P", B.P[:, :], [self_last_dve(K)])
            K.tap("osb", B.osb[:, :], K.rd_waits("osb"))
            K.tap("xln", B.x_lnT[:, :], K.rd_waits("big0"))
            K.tap("cful", B.cful[:, :], [self_last_act(K)])
            K.tap("sful", B.sful[:, :], [self_last_dve(K)])
        tso = K.dma("sp", "xd", ost[h], B.osb[:, :],
                    waits=K.rd_waits("osb"))
        K.rd_done("osb", tso)
        otok_prev[h] = tso

    if getattr(K, "proj_only", False):
        tst = K.dma("sp", "xd", xdst[3], B.x_blk[:, :],
                    waits=[self_last_dve(K), self_last_act(K)])
        K.rd_done("x_blk", tst)
        return tst

    for ts in range(NB):
        tosl = K.dma("sp", "xd", B.osl[:, :].rearrange("p (h u) -> p h u", h=H),
                     ost[:, :, ts * S:(ts + 1) * S].rearrange("h p u -> p h u"),
                     waits=otok_prev + K.wr_waits("slot0"))
        tox = K.dma("sp", "xd", B.x_blk[:, :], xsrc[ts],
                    waits=K.wr_waits("x_blk"))
        rtoks = []
        for dc in range(DC):
            wslot_o = B.wring[:, (dc % 2) * DC * 128:(dc % 2 + 1) * DC * 128]
            res_o = f"wro{dc % 2}"
            tww = K.dma("pool", "wd", wslot_o, Wap["wo"][dc],
                        waits=K.wr_waits(res_o))
            Pa = PS.P0 if dc % 2 == 0 else PS.P1
            bank = "P0" if dc % 2 == 0 else "P1"
            wb = K.wr_waits(bank)
            mt = None
            for c in range(DC):
                mt = K.mm(Pa[:, :], wslot_o[:, c * 128:(c + 1) * 128],
                          B.osl[:, c * S:(c + 1) * S],
                          start=(c == 0), stop=(c == DC - 1),
                          waits=[tww, tosl] + (wb if c == 0 else []),
                          inc="pe" if c == DC - 1 else None)
            K.rd_done(res_o, mt)
            K.rd_done("slot0", mt)
            tr_ = K.emit("dve", lambda v, dc=dc, Pa=Pa: v.tensor_tensor(
                B.x_blk[:, dc * S:(dc + 1) * S],
                B.x_blk[:, dc * S:(dc + 1) * S], Pa[:, :], ALU.add),
                waits=[mt, tox], inc="dve")
            K.wr_done(bank, tr_)
            rtoks.append(tr_)
        tst = K.dma("sp", "xd", xdst[ts], B.x_blk[:, :], waits=rtoks)
        K.rd_done("x_blk", tst)
        for r in rtoks:
            K.rd_done("x_blk", r)
    return tst


NT_TILES = 16
WTILES = 8


def self_last_dve(K):
    return ("dve", K.cnt.get("dve", 0))


def self_last_act(K):
    return ("act", K.cnt.get("act", 0))


def emit_ffn(K, B, PS, kind, Wap, xsrc, xdst, pfx=""):
    is0 = (kind == "ffn0")
    pc = getattr(K, "pc", {})
    tsg = K.dma("sp", "xd", B.sgc[:, :], Wap["sg"], waits=K.wr_waits("sgc"))
    tsu = None
    if is0:
        tsu = K.dma("sp", "xd", B.suc[:, :], Wap["su"], waits=K.wr_waits("suc"))
    tsd = K.dma("sp", "xd", B.sdc[:, :], Wap["sd"], waits=K.wr_waits("sdc"))

    tst = None
    for b in range(NB):
        tok_x = K.dma("sp", "xd", B.x_blk[:, :], xsrc[b],
                      waits=K.wr_waits("x_blk"))
        dst = [B.x_lnB[:, c * S:(c + 1) * S] for c in range(DC)]
        ln_toks = emit_ln_block(K, B, PS, tok_x, dst, "slot0")
        for tk_ in ln_toks:
            K.wr_done("slot0", tk_, append=True)

        wfuse = K.wr_waits("fuse")
        ftoks = []
        for ic in range(IC):
            par = ic % 2
            gsl = B.wgu[:, (0 + par) * DC * 128:(1 + par) * DC * 128]
            usl = B.wgu[:, (2 + par) * DC * 128:(3 + par) * DC * 128]
            gres, ures = f"wg{par}", f"wu{par}"
            gsrc = Wap["g_blob"][ic]
            gpc = pc.get((pfx, "g", ic))
            if gpc is not None:
                gsrc = Wap["g_bf"][ic]
            tgw = K.dma("pool", "wd", gsl, gsrc,
                        waits=K.wr_waits(gres) + [gpc])
            tuw = None
            if is0:
                tuw = K.dma("pool", "wd", usl, Wap["u_blob"][ic],
                            waits=K.wr_waits(ures))
            Pg = PS.P0 if par == 0 else PS.P2
            gbank = "P0" if par == 0 else "P2"
            Pu = PS.P1 if par == 0 else PS.P3
            ubank = "P1" if par == 0 else "P3"
            wgb = K.wr_waits(gbank)
            mg = None
            for c in range(DC):
                mg = K.mm(Pg[:, :], gsl[:, c * 128:(c + 1) * 128], dst[c],
                          start=(c == 0), stop=(c == DC - 1),
                          waits=[tgw, ln_toks[c]] + (wgb if c == 0 else []),
                          inc="pe" if c == DC - 1 else None)
            K.rd_done(gres, mg)
            K.rd_done("slot0", mg)
            if is0:
                wub = K.wr_waits(ubank)
                mu = None
                for c in range(DC):
                    mu = K.mm(Pu[:, :], usl[:, c * 128:(c + 1) * 128], dst[c],
                              start=(c == 0), stop=(c == DC - 1),
                              waits=[tuw, ln_toks[c]] + (wub if c == 0 else []),
                              inc="pe" if c == DC - 1 else None)
                K.rd_done(ures, mu)
                K.rd_done("slot0", mu)
                stg = B.sqm if par == 0 else B.bc2
                sres = "sqm" if par == 0 else "bc2"
                tsl = K.emit("act", lambda sc, Pg=Pg, stg=stg, ic=ic:
                             sc.activation(
                    stg[:, :], Pg[:, :], AF.Silu, bias=0.0,
                    scale=B.sgc[:, ic:ic + 1]),
                    waits=[mg, tsg] + K.wr_waits(sres), inc="act")
                K.rd_done(gbank, tsl)
                tf = K.emit("dve", lambda v, Pu=Pu, stg=stg, ic=ic:
                            v.scalar_tensor_tensor(
                    B.fuse[:, ic * S:(ic + 1) * S], Pu[:, :],
                    B.suc[:, ic:ic + 1], stg[:, :], ALU.mult, ALU.mult),
                    waits=[mu, tsu, tsl] + (wfuse if ic == 0 else []),
                    inc="dve")
                K.rd_done(ubank, tf)
                K.wr_done(sres, tf)
            else:
                tf = K.emit("act", lambda sc, Pg=Pg, ic=ic: sc.activation(
                    B.fuse[:, ic * S:(ic + 1) * S], Pg[:, :], AF.Gelu,
                    bias=B.sgc[:, ic:ic + 1], scale=1.0),
                    waits=[mg, tsg] + (wfuse if ic == 0 else []), inc="act")
                K.rd_done(gbank, tf)
            ftoks.append(tf)
            K.wr_done("fuse", tf, append=(ic > 0))

        rtoks = []
        for oc in range(DC):
            par = oc % 2
            dsl = B.wdn[:, par * IC * 128:(par + 1) * IC * 128]
            dres = f"wd{par}"
            dsrc = Wap["d_blob"][oc]
            dpc = pc.get((pfx, "d", oc))
            if dpc is not None:
                dsrc = Wap["d_bf"][oc]
            tdw = K.dma("pool", "wd", dsl, dsrc,
                        waits=K.wr_waits(dres) + [dpc])
            Pd = PS.P0 if par == 0 else PS.P1
            dbank = "P0" if par == 0 else "P1"
            wdb = K.wr_waits(dbank)
            if getattr(K, "dbg", False) and oc == DC - 1 and b == 3 and                     not getattr(K, "_dtapped", 0):
                K._dtapped = 1
                K.tap("fuse_end", B.fuse[:, 48 * S:64 * S], [tdw] + wdb)
                K.tap("wd15", dsl, [tdw])
            md = None
            for c in range(IC):
                md = K.mm(Pd[:, :], dsl[:, c * 128:(c + 1) * 128],
                          B.fuse[:, c * S:(c + 1) * S],
                          start=(c == 0), stop=(c == IC - 1),
                          waits=[tdw, ftoks[c]] + (wdb if c == 0 else []),
                          inc="pe" if c == IC - 1 else None)
            K.rd_done(dres, md)
            K.rd_done("fuse", md)
            if is0:
                tr_ = K.emit("dve", lambda v, oc=oc, Pd=Pd:
                             v.scalar_tensor_tensor(
                    B.x_blk[:, oc * S:(oc + 1) * S], Pd[:, :],
                    B.sdc[:, oc:oc + 1], B.x_blk[:, oc * S:(oc + 1) * S],
                    ALU.mult, ALU.add), waits=[md, tsd, tok_x], inc="dve")
            else:
                tr_ = K.emit("dve", lambda v, oc=oc, Pd=Pd:
                             v.scalar_tensor_tensor(
                    B.x_blk[:, oc * S:(oc + 1) * S], Pd[:, :],
                    B.sdc[:, oc:oc + 1], B.x_blk[:, oc * S:(oc + 1) * S],
                    ALU.add, ALU.add), waits=[md, tsd, tok_x], inc="dve")
            K.wr_done(dbank, tr_)
            rtoks.append(tr_)
        if getattr(K, "dbg", False) and b == 3 and not getattr(K, "_ftapped", 0):
            K._ftapped = 1
            K.tap("fxblk3", B.x_blk[:, :], rtoks)
        tst = K.dma("sp", "xd", xdst[b], B.x_blk[:, :], waits=rtoks)
        K.rd_done("x_blk", tst)
        for r in rtoks:
            K.rd_done("x_blk", r)
    return tst


SUBLAYERS = [
    ("a0a", "L0aa"), ("ffn0", "L0f"), ("a0b", "L0ab"), ("ffn0", "L0f"),
    ("a1a", "L1aa"), ("ffn1", "L1f"), ("a1b", "L1ab"), ("ffn1", "L1f"),
]


def build_program(sub_idx_list):
    K = Prog()
    nc = K.nc
    B = alloc_sbuf(nc)
    PS = alloc_psum(nc)
    need_rope = any(SUBLAYERS[i][0] in ("a0a", "a0b") for i in sub_idx_list)
    need_attn = any(SUBLAYERS[i][0].startswith("a") for i in sub_idx_list)

    xin = nc.dram_tensor("xin", [NB, 128, DC * S], F32, kind="ExternalInput").ap()
    xout = nc.dram_tensor("xout", [NB, 128, DC * S], F32,
                          kind="ExternalOutput").ap()
    nsub = len(sub_idx_list)
    xst = xst2 = None
    if nsub > 1:
        xst = nc.dram_tensor("xst", [NB, 128, DC * S], F32, kind="Internal").ap()
    if nsub > 2:
        xst2 = nc.dram_tensor("xst2", [NB, 128, DC * S], F32,
                              kind="Internal").ap()
    ost = None
    if need_attn:
        ost = nc.dram_tensor("ost", [H, 128, T], BF16, kind="Internal").ap()

    Waps = {}
    for i in sub_idx_list:
        kind, pfx = SUBLAYERS[i]
        if pfx in Waps:
            continue
        W = {}
        if kind.startswith("a"):
            for nm in ("wq", "wk", "wv", "wo"):
                W[nm] = nc.dram_tensor(f"{pfx}_{nm}", [H, 128, DC * 128], F32,
                                       kind="ExternalInput").ap()
            if kind in ("a0b", "a1a"):
                W["nrm"] = nc.dram_tensor(f"{pfx}_nrm", [128, 2], F32,
                                          kind="ExternalInput").ap()
            W["ost"] = ost
        elif kind == "ffn0":
            W["g_blob"] = nc.dram_tensor(f"{pfx}_g", [IC, 128, DC * 128], I8,
                                         kind="ExternalInput").ap()
            W["u_blob"] = nc.dram_tensor(f"{pfx}_u", [IC, 128, DC * 128], I8,
                                         kind="ExternalInput").ap()
            W["d_blob"] = nc.dram_tensor(f"{pfx}_d", [DC, 128, IC * 128], I8,
                                         kind="ExternalInput").ap()
            W["sg"] = nc.dram_tensor(f"{pfx}_sg", [128, IC], F32,
                                     kind="ExternalInput").ap()
            W["su"] = nc.dram_tensor(f"{pfx}_su", [128, IC], F32,
                                     kind="ExternalInput").ap()
            W["sd"] = nc.dram_tensor(f"{pfx}_sd", [128, DC], F32,
                                     kind="ExternalInput").ap()
        else:
            W["g_blob"] = nc.dram_tensor(f"{pfx}_g", [IC, 128, DC * 128], F32,
                                         kind="ExternalInput").ap()
            W["d_blob"] = nc.dram_tensor(f"{pfx}_d", [DC, 128, IC * 128], F32,
                                         kind="ExternalInput").ap()
            W["g_bf"] = nc.dram_tensor(f"{pfx}_gbf", [IC, 128, DC * 128],
                                       BF16, kind="Internal").ap()
            W["d_bf"] = nc.dram_tensor(f"{pfx}_dbf", [DC, 128, IC * 128],
                                       BF16, kind="Internal").ap()
            W["sg"] = nc.dram_tensor(f"{pfx}_sg", [128, IC], F32,
                                     kind="ExternalInput").ap()
            W["sd"] = nc.dram_tensor(f"{pfx}_sd", [128, DC], F32,
                                     kind="ExternalInput").ap()
        Waps[pfx] = W

    emit_prelude(K, B, PS, need_rope)
    K.bg = []
    K.pc = {}
    for pfx, W in Waps.items():
        if "g_bf" not in W:
            continue

        def mk(key, dst, srcb, j):
            def run(key=key, dst=dst, srcb=srcb, j=j):
                K.pc[key] = K.dma("pool", "wd", dst[j], srcb[j])
            return run
        for ic in range(IC):
            K.bg.append(mk((pfx, "g", ic), W["g_bf"], W["g_blob"], ic))
        for oc in range(DC):
            K.bg.append(mk((pfx, "d", oc), W["d_bf"], W["d_blob"], oc))
    last_tok = None
    for j, i in enumerate(sub_idx_list):
        kind, pfx = SUBLAYERS[i]
        src = xin if j == 0 else (xst if j % 2 == 1 else xst2)
        dst = xout if j == nsub - 1 else (xst if j % 2 == 0 else xst2)
        if kind.startswith("a"):
            last_tok = emit_attention(K, B, PS, kind, Waps[pfx], src, dst)
        else:
            last_tok = emit_ffn(K, B, PS, kind, Waps[pfx], src, dst, pfx)
    nc = K.finalize([last_tok])
    return nc


def proj_blob(W):
    din, dout = W.shape
    ndc, noc = din // 128, dout // 128
    return np.ascontiguousarray(
        W.reshape(ndc, 128, noc, 128).transpose(2, 1, 0, 3)
        .reshape(noc, 128, ndc * 128))


def x_to_blocks(xc):
    xT = xc.T
    return np.ascontiguousarray(
        xT.reshape(DC, 128, NB, S).transpose(2, 1, 0, 3)
        .reshape(NB, 128, DC * S))


def x_from_blocks(xb):
    xT = xb.reshape(NB, 128, DC, S).transpose(2, 1, 0, 3).reshape(D, T)
    return np.ascontiguousarray(xT.T)


def colmaj(v):
    return np.ascontiguousarray(v.reshape(-1, 128).T)


def prep_weights(inputs):
    P = {}
    P["L0aa"] = {
        "wq": proj_blob(inputs["L0_attn_a_wq"]),
        "wk": proj_blob(inputs["L0_attn_a_wk"]),
        "wv": proj_blob(inputs["L0_attn_a_wv"]),
        "wo": proj_blob(inputs["L0_attn_a_wo"]),
    }
    P["L0ab"] = {
        "wq": proj_blob(np.ascontiguousarray(inputs["L0_attn_b_Wq"].T)),
        "wk": proj_blob(np.ascontiguousarray(inputs["L0_attn_b_Wk"].T)),
        "wv": proj_blob(np.ascontiguousarray(inputs["L0_attn_b_Wv"].T)),
        "wo": proj_blob(np.ascontiguousarray(inputs["L0_attn_b_Wo"].T)),
        "nrm": np.ascontiguousarray(np.stack(
            [inputs["L0_attn_b_q_norm_w"], inputs["L0_attn_b_k_norm_w"]],
            axis=1)),
    }
    P["L0f"] = {
        "g_blob": proj_blob(inputs["L0_ffn_w_gate"]),
        "u_blob": proj_blob(inputs["L0_ffn_w_up"]),
        "d_blob": proj_blob(inputs["L0_ffn_w_down"]),
        "sg": colmaj(inputs["L0_ffn_s_gate"]),
        "su": colmaj(inputs["L0_ffn_s_up"]),
        "sd": colmaj(inputs["L0_ffn_s_down"]),
    }
    P["L1aa"] = {
        "wq": proj_blob(np.ascontiguousarray(inputs["L1_attn_a_Wq"].T)),
        "wk": proj_blob(np.ascontiguousarray(inputs["L1_attn_a_Wk"].T)),
        "wv": proj_blob(np.ascontiguousarray(inputs["L1_attn_a_Wv"].T)),
        "wo": proj_blob(np.ascontiguousarray(inputs["L1_attn_a_Wo"].T)),
        "nrm": np.ascontiguousarray(np.stack(
            [inputs["L1_attn_a_gamma"], inputs["L1_attn_a_beta"]], axis=1)),
    }
    P["L1f"] = {
        "g_blob": proj_blob(np.ascontiguousarray(inputs["L1_ffn_W0"].T)),
        "d_blob": proj_blob(np.ascontiguousarray(inputs["L1_ffn_W1"].T)),
        "sg": colmaj(inputs["L1_ffn_bias0"]),
        "sd": colmaj(inputs["L1_ffn_bias1"]),
    }
    P["L1ab"] = {
        "wq": proj_blob(np.ascontiguousarray(
            inputs["L1_attn_b_W_Q"].transpose(2, 0, 1).reshape(D, D))),
        "wk": proj_blob(np.ascontiguousarray(
            inputs["L1_attn_b_W_K"].transpose(2, 0, 1).reshape(D, D))),
        "wv": proj_blob(np.ascontiguousarray(
            inputs["L1_attn_b_W_V"].transpose(2, 0, 1).reshape(D, D))),
        "wo": proj_blob(np.ascontiguousarray(
            inputs["L1_attn_b_W_O"].transpose(0, 2, 1).reshape(D, D))),
    }
    return P


def in_map_for(sub_idx_list, P, xblk):
    m = {"xin": xblk}
    seen = set()
    for i in sub_idx_list:
        kind, pfx = SUBLAYERS[i]
        if pfx in seen:
            continue
        seen.add(pfx)
        W = P[pfx]
        if kind.startswith("a"):
            m[f"{pfx}_wq"] = W["wq"]
            m[f"{pfx}_wk"] = W["wk"]
            m[f"{pfx}_wv"] = W["wv"]
            m[f"{pfx}_wo"] = W["wo"]
            if "nrm" in W:
                m[f"{pfx}_nrm"] = W["nrm"]
        else:
            m[f"{pfx}_g"] = W["g_blob"]
            m[f"{pfx}_d"] = W["d_blob"]
            m[f"{pfx}_sg"] = W["sg"]
            m[f"{pfx}_sd"] = W["sd"]
            if "u_blob" in W:
                m[f"{pfx}_u"] = W["u_blob"]
                m[f"{pfx}_su"] = W["su"]
    return m


def run_groups(inputs, groups):
    P = prep_weights(inputs)
    x = np.asarray(inputs["L0_attn_a_hidden_states"], np.float32)
    ncores = x.shape[0]
    xblk = [x_to_blocks(x[c]) for c in range(ncores)]
    for grp in groups:
        nc = build_program(grp)
        in_maps = [in_map_for(grp, P, xblk[c]) for c in range(ncores)]
        res = run_bass_kernel_spmd(nc, in_maps, core_ids=list(range(ncores)))
        xblk = [np.asarray(res.results[c]["xout"], np.float32)
                for c in range(ncores)]
    return np.stack([x_from_blocks(xblk[c]) for c in range(ncores)])


def kernel(**inputs):
    out = run_groups(inputs, [list(range(8))])
    return out.astype(np.float32)


# version 09
# speedup vs baseline: 1.2136x; 1.0059x over previous
import math
import numpy as np
import concourse.bass as bass
import concourse.mybir as mybir
from concourse.bass_utils import run_bass_kernel_spmd

F32 = mybir.dt.float32
BF16 = mybir.dt.bfloat16
I8 = mybir.dt.int8
I32 = mybir.dt.int32
AF = mybir.ActivationFunctionType
ALU = mybir.AluOpType

T = 2048
D = 2048
DC = 16
H = 16
DH = 128
NB = 4
SPAN = 512
IC = 64
SCL = 1.0 / math.sqrt(DH)
PI = math.pi
NEG = -1.0e30
SBUF_LIMIT = 212863
S = SPAN

ENGS = ("pe", "act", "dve", "pool", "sp")

ALIAS = {
    "big0": ["fuse"], "fuse": ["big0"],
    "slot0": [],
    "slot1E": ["wgu", "stage2"], "slot1P": ["wgu", "stage2"],
    "slot1PT": ["wgu", "stage2"],
    "wgu": ["slot1E", "slot1P", "slot1PT", "stage2"],
    "stage2": ["slot1E", "slot1P", "slot1PT", "wgu"],
    "qh": ["wdn"], "kh": ["wdn"], "vt": ["wdn"], "vsb": ["wdn"],
    "qt1": ["wdn"], "qt2": ["wdn"], "osb": ["wdn"],
    "wdn": ["qh", "kh", "vt", "vsb", "qt1", "qt2", "osb"],
    "x_blk": ["rowi"], "rowi": ["x_blk"],
    "wr00": ["wro0"], "wro0": ["wr00"],
    "wr01": ["wro1"], "wro1": ["wr01"],
}


class Prog:
    def __init__(self):
        self.nc = bass.Bass("TRN2", target_bir_lowering=False)
        self.ops = {e: [] for e in ENGS}
        self.sems = {}
        self.cnt = {}
        self.waited = {e: {} for e in ENGS}
        self.rsrc = {}

    def _sem(self, name):
        if name not in self.sems:
            self.sems[name] = self.nc.alloc_semaphore(name=name)
            self.cnt[name] = 0
        return self.sems[name]

    def emit(self, eng, fn, waits=(), inc=None, by=1):
        real = []
        for wtok in waits:
            if wtok is None:
                continue
            n, v = wtok
            self._sem(n)
            if n == eng and eng == "pe":
                continue
            if self.waited[eng].get(n, -1) >= v:
                continue
            self.waited[eng][n] = v
            real.append((n, v))
        tok = None
        if inc is not None:
            self._sem(inc)
            self.cnt[inc] += by
            tok = (inc, self.cnt[inc])
        sems = self.sems

        def run(e, fn=fn, real=tuple(real), inc=inc, by=by):
            for n, v in real:
                e.wait_ge(sems[n], v)
            ins = fn(e)
            if inc is not None:
                ins.then_inc(sems[inc], by)
        self.ops[eng].append(run)
        return tok

    def mm(self, out, lhsT, rhs, start, stop, waits=(), inc=None):
        return self.emit(
            "pe", lambda t, o=out, a=lhsT, b=rhs, s1=start, s2=stop:
            t.matmul(o, a, b, start=s1, stop=s2), waits, inc)

    def tr(self, out, in_, ident, waits=(), inc=None):
        return self.emit(
            "pe", lambda t, o=out, i=in_, d=ident: t.transpose(o, i, d),
            waits, inc)

    def dma(self, q, sem, out, in_, waits=()):
        return self.emit(
            q, lambda e, o=out, i=in_: e.dma_start(out=o, in_=i),
            waits, sem, by=16)

    def _grp(self, n):
        return [n] + ALIAS.get(n, [])

    def wr_waits(self, n):
        out = []
        for n2 in self._grp(n):
            r = self.rsrc.get(n2)
            if r:
                out += r[0] + r[1]
                self.rsrc[n2] = ([], [])
        return out

    def wr_done(self, n, tok, append=False):
        r = self.rsrc.setdefault(n, ([], []))
        if append:
            r[0].append(tok)
        else:
            self.rsrc[n] = ([tok], [])

    def rd_waits(self, n):
        return list(self.rsrc.get(n, ([], []))[0])

    def rd_done(self, n, tok):
        self.rsrc.setdefault(n, ([], []))[1].append(tok)

    def tap(self, name, ap, waits):
        shape = [int(c) for _, c in ap.ap]
        dt_ = ap.tensor.dtype
        d = self.nc.dram_tensor(f"dbg_{name}", shape, dt_,
                                kind="ExternalOutput").ap()
        return self.dma("sp", "xd", d, ap, waits=waits)

    def finalize(self, final_waits):
        sems = self.sems
        for wtok in final_waits:
            if wtok is None:
                continue
            n, v = wtok
            self._sem(n)
            if self.waited["sp"].get(n, -1) >= v:
                continue
            self.waited["sp"][n] = v

            def run(e, n=n, v=v):
                e.wait_ge(sems[n], v)
            self.ops["sp"].append(run)
        nc = self.nc
        with nc.Block() as block:
            @block.tensor
            def _(t):
                for f in self.ops["pe"]:
                    f(t)

            @block.scalar
            def _(sc):
                for f in self.ops["act"]:
                    f(sc)

            @block.vector
            def _(v):
                for f in self.ops["dve"]:
                    f(v)

            @block.gpsimd
            def _(g):
                for f in self.ops["pool"]:
                    f(g)

            @block.sync
            def _(s):
                for f in self.ops["sp"]:
                    f(s)
        return nc


class Bufs:
    pass


def alloc_sbuf(nc):
    B = Bufs()
    base = (nc.sbuf_base + 63) // 64 * 64
    cur = [base]

    def at(name, shape, dtype, offset=None):
        bpe = {F32: 4, BF16: 2, I8: 1, I32: 4}[dtype]
        nbytes = int(np.prod(shape[1:])) * bpe
        if offset is None:
            offset = cur[0]
        cur[0] = max(cur[0], offset + ((nbytes + 63) // 64) * 64)
        h = nc.alloc_sbuf_tensor_at(name, shape, dtype, offset=offset)
        setattr(B, name, h.ap())
        return offset

    at("ones_col", [128, 1], BF16)
    at("ones_row", [1, 128], BF16)
    at("ident", [128, 128], BF16)
    at("identf", [128, 128], F32)
    at("pmat", [128, 128], BF16)
    at("m_tril_neg", [128, 128], F32)
    at("m_triu_neg", [128, 128], F32)
    at("m_tril01", [128, 128], F32)
    at("sign_col", [128, 1], F32)
    at("cful", [128, T], BF16)
    at("sful", [128, T], BF16)
    at("row_f", [1, 6 * S], F32)
    at("row_bf", [1, 2 * S + 64], BF16)
    at("col_f", [128, 32], F32)
    at("ccol", [128, 12], F32)
    at("sqm", [128, S], BF16)
    at("bc2", [128, S], BF16)
    at("sgc", [128, IC], F32)
    at("suc", [128, IC], F32)
    at("sdc", [128, DC], F32)
    at("nrmw", [128, 2], F32)

    xblk = at("x_blk", [128, DC * S], F32)
    at("rowi", [1, T], I32, xblk)
    at("tabf", [128, S], F32, xblk + 16384)
    at("ifr_row", [1, 128], F32, xblk + 20480)
    at("tabi", [128, S], I32, xblk + 24576)
    at("tabg", [128, S], F32, xblk + 26624)
    big0 = at("x_lnT", [128, DC * T], BF16)
    at("fuse", [128, IC * S], BF16, big0)
    slot0 = at("osl", [128, DC * S], BF16)
    at("x_lnB", [128, DC * S], BF16, slot0)
    at("stage", [128, DC * S], BF16, slot0)
    slot1 = at("E", [128, T], F32)
    at("P", [128, T], BF16)
    at("PT", [128, T], BF16)
    at("wgu", [128, 4 * DC * 128], BF16, slot1)
    at("stage2", [128, DC * S], BF16, slot1)
    slot2 = at("qh", [128, T], BF16)
    at("kh", [128, T], BF16)
    at("vt", [128, T], BF16)
    at("vsb", [128, T], BF16)
    at("qt1", [128, T], BF16)
    at("qt2", [128, T], BF16)
    at("osb", [128, T], BF16)
    at("wdn", [128, 2 * IC * 128], BF16, slot2)
    at("wring", [128, 5 * DC * 128], BF16)
    assert cur[0] <= nc.sbuf_top, f"SBUF over budget: {cur[0]} > {nc.sbuf_top}"
    B.end = cur[0]
    return B


def alloc_psum(nc):
    PS = Bufs()
    for i in range(5):
        setattr(PS, f"P{i}", nc.alloc_psum_tensor(f"P{i}", [128, S], F32).ap())
    PS.pstat = nc.alloc_psum_tensor("pstat", [1, S], F32).ap()
    PS.ptrA = nc.alloc_psum_tensor("ptrA", [128, S], BF16).ap()
    PS.ptrB = nc.alloc_psum_tensor("ptrB", [128, S], BF16).ap()
    return PS


def emit_prelude(K, B, PS, need_rope):
    K.rope_tok = None
    tk = None

    def pool(fn):
        nonlocal tk
        tk = K.emit("pool", fn, inc="pool")
        return tk

    pool(lambda g: g.memset(B.ones_col[:, :], 1.0))
    pool(lambda g: g.memset(B.ones_row[:, :], 1.0))
    pool(lambda g: g.memset(B.ident[:, :], 1.0))
    pool(lambda g: g.affine_select(
        B.ident[:, :], B.ident[:, :], [[1, 128]], ALU.is_equal, 0.0,
        base=0, channel_multiplier=-1))
    pool(lambda g: g.memset(B.identf[:, :], 1.0))
    pool(lambda g: g.affine_select(
        B.identf[:, :], B.identf[:, :], [[1, 128]], ALU.is_equal, 0.0,
        base=0, channel_multiplier=-1))
    pool(lambda g: g.memset(B.pmat[:, :], 1.0))
    pool(lambda g: g.affine_select(
        B.pmat[:, 0:64], B.pmat[:, 0:64], [[1, 64]], ALU.is_equal, 0.0,
        base=64, channel_multiplier=-1))
    pool(lambda g: g.affine_select(
        B.pmat[:, 64:128], B.pmat[:, 64:128], [[1, 64]], ALU.is_equal, 0.0,
        base=0, channel_multiplier=-1))
    pool(lambda g: g.memset(B.m_tril_neg[:, :], 0.0))
    pool(lambda g: g.affine_select(
        B.m_tril_neg[:, :], B.m_tril_neg[:, :], [[-1, 128]], ALU.is_ge, NEG,
        base=0, channel_multiplier=1))
    pool(lambda g: g.memset(B.m_triu_neg[:, :], 0.0))
    pool(lambda g: g.affine_select(
        B.m_triu_neg[:, :], B.m_triu_neg[:, :], [[1, 128]], ALU.is_ge, NEG,
        base=-1, channel_multiplier=-1))
    pool(lambda g: g.memset(B.m_tril01[:, :], 1.0))
    pool(lambda g: g.affine_select(
        B.m_tril01[:, :], B.m_tril01[:, :], [[-1, 128]], ALU.is_ge, 0.0,
        base=0, channel_multiplier=1))
    pool(lambda g: g.memset(B.ccol[:, 0:1], SCL))
    pool(lambda g: g.memset(B.ccol[:, 1:2], -PI))
    pool(lambda g: g.memset(B.ccol[:, 2:3], -2.0 * math.log(10000.0) / DH))
    pool(lambda g: g.memset(B.ccol[:, 3:4], 1e-5))
    pool(lambda g: g.memset(B.ccol[:, 4:5], 1.0 / DH))
    pool(lambda g: g.memset(B.ccol[:, 5:6], 1e-6))
    pool(lambda g: g.memset(B.ccol[:, 6:7], 2.0 * PI))
    pool(lambda g: g.memset(B.ccol[:, 8:9], -0.5))
    pool(lambda g: g.memset(B.ccol[:, 9:10], 1e-5))
    pool(lambda g: g.memset(B.ccol[:, 10:11], 1e-30))
    pool(lambda g: g.memset(B.ccol[:, 11:12], -1.0))
    pool(lambda g: g.memset(B.sign_col[:, :], 1.0))
    tsign = pool(lambda g: g.memset(B.sign_col[0:64, :], -1.0))
    K.consts_tok = tsign

    if need_rope:
        tp = pool(lambda g: g.iota(B.rowi[:, :], [[1, T]], base=0,
                                   channel_multiplier=0))
        tpos = K.emit("dve", lambda v: v.tensor_copy(
            B.row_f[:, 0:T], B.rowi[:, :]), waits=[tp], inc="dve")
        K.rd_done("rowi", tpos)
        tpc = pool(lambda g: g.iota(B.tabi[:, 0:1], [[0, 1]], base=0,
                                    channel_multiplier=1))
        tc1 = K.emit("dve", lambda v: v.tensor_copy(
            B.ccol[:, 7:8], B.tabi[:, 0:1]), waits=[tpc, K.consts_tok],
            inc="dve")
        tc2 = K.emit("dve", lambda v: v.tensor_single_scalar(
            B.ccol[64:128, 7:8], B.ccol[64:128, 7:8], -64.0, ALU.add),
            waits=[tc1], inc="dve")
        tc3 = K.emit("act", lambda sc: sc.activation(
            B.ccol[:, 7:8], B.ccol[:, 7:8], AF.Exp,
            bias=0.0, scale=B.ccol[:, 2:3]), waits=[tc2], inc="act")
        inv2pi = 1.0 / (2.0 * PI)
        prev_td = None
        for s in range(NB):
            sl = slice(s * S, (s + 1) * S)
            tpi = K.emit("pool", lambda g, s=s: g.iota(
                B.tabi[:, :], [[1, S]], base=s * S, channel_multiplier=0),
                waits=[tc1, prev_td], inc="pool")
            tg1 = K.emit("dve", lambda v: v.tensor_copy(
                B.tabg[:, :], B.tabi[:, :]), waits=[tpi], inc="dve")
            tang = K.emit("dve", lambda v: v.tensor_single_scalar(
                B.tabg[:, :], B.tabg[:, :], B.ccol[:, 7:8], ALU.mult),
                waits=[tg1, tc3], inc="dve")

            def redsin(dstap, shift, wprev):
                t1 = K.emit("dve", lambda v, shift=shift: v.tensor_scalar(
                    B.tabf[:, :], B.tabg[:, :], shift, inv2pi,
                    ALU.add, ALU.mult), waits=[tang, wprev], inc="dve")
                t2 = K.emit("dve", lambda v: v.tensor_copy(
                    B.tabi[:, :], B.tabf[:, :]), waits=[t1], inc="dve")
                t3 = K.emit("dve", lambda v: v.tensor_tensor(
                    B.tabf[:, :], B.tabf[:, :], B.tabi[:, :], ALU.subtract),
                    waits=[t2], inc="dve")
                t4 = K.emit("act", lambda sc, dstap=dstap: sc.activation(
                    dstap, B.tabf[:, :], AF.Sin, bias=0.0,
                    scale=B.ccol[:, 6:7]), waits=[t3], inc="act")
                return t4
            tb = redsin(B.sful[:, sl], 0.0, None)
            td = redsin(B.cful[:, sl], 0.5 * PI, tb)
            prev_td = td
            tsg_ = K.emit("dve", lambda v, sl=sl: v.tensor_single_scalar(
                B.sful[:, sl], B.sful[:, sl], B.sign_col[:, :], ALU.mult),
                waits=[tb, tsign], inc="dve")
            K.rope_tok = td
            K.rd_done("x_blk", td)
            K.rd_done("x_blk", tsg_)


def emit_ln_block(K, B, PS, tok_x, dst_slices, dst_res, tok_x2=None,
                  fast=False):
    w0 = K.wr_waits("slot0")
    if fast:
        HS = 8 * S
        w2 = K.wr_waits("stage2")
        tcp0 = K.emit("dve", lambda v: v.tensor_copy(
            B.stage[:, 0:HS], B.x_blk[:, 0:HS]), waits=[tok_x] + w0,
            inc="dve")
        tsq0 = K.emit("act", lambda sc: sc.activation(
            B.stage2[:, 0:HS], B.x_blk[:, 0:HS], AF.Square, bias=0.0,
            scale=1.0), waits=[tok_x] + w2, inc="act")
        tcp1 = K.emit("dve", lambda v: v.tensor_copy(
            B.stage[:, HS:2 * HS], B.x_blk[:, HS:2 * HS]), waits=[tok_x2],
            inc="dve")
        tsq1 = K.emit("act", lambda sc: sc.activation(
            B.stage2[:, HS:2 * HS], B.x_blk[:, HS:2 * HS], AF.Square,
            bias=0.0, scale=1.0), waits=[tok_x2], inc="act")
        for tk_ in (tcp0, tsq0, tcp1, tsq1):
            K.rd_done("x_blk", tk_)
        wps = K.wr_waits("pstat")
        mtA = None
        for c in range(DC):
            tw = tcp0 if c < 8 else tcp1
            mtA = K.mm(PS.pstat[:, :], B.ones_col[:, :],
                       B.stage[:, c * S:(c + 1) * S],
                       start=(c == 0), stop=(c == DC - 1),
                       waits=([tw, K.consts_tok] + wps) if c in (0, 8)
                       else [tw], inc="pe" if c == DC - 1 else None)
        tB = K.emit("dve", lambda v: v.tensor_copy(
            B.row_f[:, S:2 * S], PS.pstat[:, :]), waits=[mtA], inc="dve")
        K.wr_done("pstat", tB)
        K.wr_done("slot0", mtA)
        wp3 = K.wr_waits("P3")
        mtB = None
        for c in range(DC):
            tw = tsq0 if c < 8 else tsq1
            mtB = K.mm(PS.P3[0:1, :], B.ones_col[:, :],
                       B.stage2[:, c * S:(c + 1) * S],
                       start=(c == 0), stop=(c == DC - 1),
                       waits=([tw] + wp3) if c in (0, 8) else [tw],
                       inc="pe" if c == DC - 1 else None)
        tA = K.emit("dve", lambda v: v.tensor_copy(
            B.row_f[:, 0:S], PS.P3[0:1, :]), waits=[mtB], inc="dve")
        K.wr_done("P3", tA)
        K.wr_done("stage2", mtB)
        mt = mtB
    else:
        t_cp = K.emit("act", lambda sc: sc.activation(
            B.stage[:, :], B.x_blk[:, :], AF.Copy, bias=0.0, scale=1.0),
            waits=[tok_x] + w0, inc="act")
        K.rd_done("x_blk", t_cp)
        mt = None
        wps = K.wr_waits("pstat")
        for c in range(DC):
            mt = K.mm(PS.pstat[:, :], B.ones_col[:, :],
                      B.stage[:, c * S:(c + 1) * S],
                      start=(c == 0), stop=(c == DC - 1),
                      waits=([t_cp, K.consts_tok] + wps) if c == 0 else [t_cp],
                      inc="pe" if c == DC - 1 else None)
        tB = K.emit("dve", lambda v: v.tensor_copy(
            B.row_f[:, S:2 * S], PS.pstat[:, :]), waits=[mt], inc="dve")
        t_sq1 = K.emit("act", lambda sc: sc.activation(
            B.stage[:, :], B.x_blk[:, :], AF.Square, bias=0.0, scale=1.0),
            waits=[mt], inc="act")
        K.rd_done("x_blk", t_sq1)
        for c in range(DC):
            mt = K.mm(PS.pstat[:, :], B.ones_col[:, :],
                      B.stage[:, c * S:(c + 1) * S],
                      start=(c == 0), stop=(c == DC - 1),
                      waits=[t_sq1, tB], inc="pe" if c == DC - 1 else None)
        K.wr_done("slot0", mt)
        tA = K.emit("dve", lambda v: v.tensor_copy(
            B.row_f[:, 0:S], PS.pstat[:, :]), waits=[mt], inc="dve")
        K.wr_done("pstat", tA)
    t2 = K.emit("dve", lambda v: v.tensor_single_scalar(
        B.row_f[:, 2 * S:3 * S], B.row_f[:, S:2 * S], 1.0 / D, ALU.mult),
        waits=[tB], inc="dve")
    t3 = K.emit("dve", lambda v: v.tensor_tensor(
        B.row_f[:, 3 * S:4 * S], B.row_f[:, 2 * S:3 * S],
        B.row_f[:, 2 * S:3 * S], ALU.mult), waits=[t2], inc="dve")
    t4 = K.emit("dve", lambda v: v.scalar_tensor_tensor(
        B.row_f[:, 3 * S:4 * S], B.row_f[:, 0:S], 1.0 / D,
        B.row_f[:, 3 * S:4 * S], ALU.mult, ALU.subtract), waits=[t3, tA], inc="dve")
    t5 = K.emit("act", lambda sc: sc.activation(
        B.row_f[:, 0:S], B.row_f[:, 3 * S:4 * S], AF.Ln,
        bias=B.ccol[0:1, 3:4], scale=1.0), waits=[t4], inc="act")
    t6 = K.emit("act", lambda sc: sc.activation(
        B.row_f[:, 0:S], B.row_f[:, 0:S], AF.Exp,
        bias=0.0, scale=B.ccol[0:1, 8:9]), waits=[t5], inc="act")
    t7 = K.emit("dve", lambda v: v.tensor_copy(B.row_bf[:, 0:S],
                                               B.row_f[:, 0:S]),
                waits=[t6], inc="dve")
    t8 = K.emit("dve", lambda v: v.tensor_tensor(
        B.row_f[:, S:2 * S], B.row_f[:, 2 * S:3 * S], B.row_f[:, 0:S],
        ALU.mult), waits=[t6], inc="dve")
    t9 = K.emit("dve", lambda v: v.tensor_single_scalar(
        B.row_bf[:, S:2 * S], B.row_f[:, S:2 * S], -1.0, ALU.mult),
        waits=[t8], inc="dve")
    tb1 = K.mm(PS.P3[:, :], B.ones_row[:, :], B.row_bf[:, 0:S], True, True,
               waits=[t7] + K.wr_waits("P3"), inc="pe")
    tb2 = K.mm(PS.P4[:, :], B.ones_row[:, :], B.row_bf[:, S:2 * S], True, True,
               waits=[t9] + K.wr_waits("P4"), inc="pe")
    out = []
    wdst = K.wr_waits(dst_res) if dst_res else []
    if fast:
        tevr = K.emit("act", lambda sc: sc.activation(
            B.bc2[:, :], PS.P3[:, :], AF.Copy, bias=0.0, scale=1.0),
            waits=[tb1] + K.wr_waits("bc2"), inc="act")
        K.wr_done("P3", tevr)
        tevn = K.emit("act", lambda sc: sc.activation(
            B.sqm[:, :], PS.P4[:, :], AF.Copy, bias=0.0, scale=1.0),
            waits=[tb2] + K.wr_waits("sqm"), inc="act")
        K.wr_done("P4", tevn)
        for c in range(DC):
            dsl = dst_slices[c]
            ta = K.emit("pool", lambda g, c=c, dsl=dsl: g.tensor_tensor(
                dsl, B.x_blk[:, c * S:(c + 1) * S], B.bc2[:, :], ALU.mult),
                waits=[tevr] + (wdst if c == 0 else []), inc="pool")
            tbb = K.emit("dve", lambda v, dsl=dsl: v.tensor_tensor(
                dsl, dsl, B.sqm[:, :], ALU.add), waits=[tevn, ta], inc="dve")
            K.rd_done("x_blk", ta)
            out.append(tbb)
        K.wr_done("bc2", out[-1])
        K.wr_done("sqm", out[-1])
        return out
    for c in range(DC):
        dsl = dst_slices[c]
        ta = K.emit("dve", lambda v, c=c, dsl=dsl: v.tensor_tensor(
            dsl, B.x_blk[:, c * S:(c + 1) * S], PS.P3[:, :], ALU.mult),
            waits=[tb1] + (wdst if c == 0 else []), inc="dve")
        tbb = K.emit("dve", lambda v, dsl=dsl: v.tensor_tensor(
            dsl, dsl, PS.P4[:, :], ALU.add), waits=[tb2, ta], inc="dve")
        K.rd_done("x_blk", tbb)
        out.append(tbb)
    K.wr_done("P3", out[-1])
    K.wr_done("P4", out[-1])
    return out


def emit_attention(K, B, PS, kind, Wap, xsrc, xdst):
    nc = K.nc
    windowed = kind in ("a0a", "a0b")
    use_rope = kind in ("a0a", "a0b")
    use_bound = kind in ("a0a", "a0b", "a1b")
    ost = Wap["ost"]

    tnrm = None
    if kind in ("a0b", "a1a"):
        tnrm = K.dma("sp", "xd", B.nrmw[:, :], Wap["nrm"],
                     waits=K.wr_waits("nrmw"))

    ln_toks = [[None] * DC for _ in range(NB)]
    wbig = K.wr_waits("big0")
    for b in range(NB):
        HS = 8 * S
        wx = K.wr_waits("x_blk") + (wbig if b == 0 else [])
        tok_x = K.dma("sp", "xd", B.x_blk[:, 0:HS], xsrc[b][:, 0:HS],
                      waits=wx)
        tok_x2 = K.dma("sp", "xd", B.x_blk[:, HS:2 * HS],
                       xsrc[b][:, HS:2 * HS])
        if getattr(K, "dbg", False) and b == 3 and not getattr(K, "_tapped", 0):
            K._tapped = 1
            K.tap("xblk3", B.x_blk[:, :], [tok_x, tok_x2])
        dst = [B.x_lnT[:, c * T + b * S: c * T + b * S + S] for c in range(DC)]
        toks = emit_ln_block(K, B, PS, tok_x, dst, None, tok_x2=tok_x2,
                             fast=True)
        ln_toks[b] = toks
        for tk_ in toks:
            K.wr_done("big0", tk_, append=True)

    if getattr(K, "ln_only", False):
        tst = K.dma("sp", "xd", xdst[3], B.x_blk[:, :],
                    waits=ln_toks[3])
        K.rd_done("x_blk", tst)
        return tst

    otok_prev = [None] * H
    for h in range(H):
        stat_toks = []

        def wslot(idx, par):
            o = (idx * 2 + par) * DC * 128 if idx < 2 else 4 * DC * 128
            return B.wring[:, o:o + DC * 128]
        par = h % 2
        res_q, res_k, res_v = f"wr0{par}", f"wr1{par}", "wr20"
        tkw = K.dma("pool", "wd", wslot(1, par), Wap["wk"][h],
                    waits=K.wr_waits(res_k))
        tqw = K.dma("pool", "wd", wslot(0, par), Wap["wq"][h],
                    waits=K.wr_waits(res_q))
        tvw = K.dma("pool", "wd", wslot(2, par), Wap["wv"][h],
                    waits=K.wr_waits(res_v))
        for _ in range(3):
            if getattr(K, "bg", None):
                K.bg.pop(0)()

        mats = [("k", tkw, wslot(1, par), res_k, B.kh, "kh"),
                ("q", tqw, wslot(0, par), res_q, B.qh, "qh"),
                ("v", tvw, wslot(2, par), res_v, B.vt, "vt")]
        seq = [(mi, s) for mi in range(3) for s in range(NB)]

        def mm_group(gi):
            mi, s = seq[gi]
            mat, wtok, wap, res_w, dst, dres = mats[mi]
            bank = ["P0", "P1", "P2"][gi % 3]
            Pa = getattr(PS, bank)
            wb = K.wr_waits(bank)
            wdst = K.wr_waits(dres) if s == 0 else []
            mt = None
            for c in range(DC):
                mt = K.mm(Pa[:, :], wap[:, c * 128:(c + 1) * 128],
                          B.x_lnT[:, c * T + s * S: c * T + s * S + S],
                          start=(c == 0), stop=(c == DC - 1),
                          waits=[wtok, ln_toks[s][c]] +
                          (wb + wdst if c == 0 else []),
                          inc="pe" if c == DC - 1 else None)
            K.rd_done(res_w, mt)
            K.rd_done("big0", mt)
            return {"gi": gi, "mat": mat, "s": s, "bank": bank, "Pa": Pa,
                    "mt": mt, "dst": dst, "dres": dres,
                    "sl": slice(s * S, (s + 1) * S)}

        def seg_rms(gi):
            return (4 + gi % 2) * S

        def do_rope(st, src_ap, extra):
            sl, dst, dres, s = st["sl"], st["dst"], st["dres"], st["s"]
            wsw = K.wr_waits("P4")
            tsw = K.mm(PS.P4[:, :], B.pmat[:, :], src_ap, True, True,
                       waits=list(extra) + wsw + [K.consts_tok], inc="pe")
            t_a = K.emit("dve", lambda v: v.tensor_tensor(
                B.qt2[:, sl], PS.P4[:, :], B.sful[:, sl], ALU.mult),
                waits=[tsw] + K.wr_waits("qt2"), inc="dve")
            K.wr_done("P4", t_a)
            t_b = K.emit("dve", lambda v: v.tensor_tensor(
                dst[:, sl], src_ap, B.cful[:, sl], ALU.mult),
                waits=list(extra) + [K.rope_tok] +
                (K.wr_waits(dres) if s == 0 else []), inc="dve")
            t_c = K.emit("dve", lambda v: v.tensor_tensor(
                dst[:, sl], dst[:, sl], B.qt2[:, sl], ALU.add),
                waits=[t_a, t_b], inc="dve")
            K.wr_done("qt2", t_c)
            K.wr_done(dres, t_c, append=(s > 0))
            return t_c

        def sqstat_st(src_ap, wtok, seg):
            wsq = K.wr_waits("sqm")
            ts_ = K.emit("act", lambda sc: sc.activation(
                B.sqm[:, :], src_ap, AF.Square, bias=0.0, scale=1.0),
                waits=[wtok] + wsq, inc="act")
            mt2 = K.mm(PS.pstat[:, :], B.ones_col[:, :], B.sqm[:, :],
                       True, True, waits=[ts_] + K.wr_waits("pstat"),
                       inc="pe")
            K.wr_done("sqm", mt2)
            te = K.emit("dve", lambda v, seg=seg: v.tensor_copy(
                B.row_f[:, seg:seg + S], PS.pstat[:, :]),
                waits=[mt2], inc="dve")
            K.wr_done("pstat", te)
            stat_toks.append(te)
            return ts_, te

        def stage1(st):
            mat, s, Pa, bank, mt = st["mat"], st["s"], st["Pa"], st["bank"], st["mt"]
            sl, dst, dres = st["sl"], st["dst"], st["dres"]
            if mat == "v" or kind == "a1b":
                tcp = K.emit("act", lambda sc, Pa=Pa, dst=dst, sl=sl:
                             sc.activation(
                    dst[:, sl], Pa[:, :], AF.Copy, bias=0.0, scale=1.0),
                    waits=[mt] + (K.wr_waits(dres) if s == 0 else []),
                    inc="act")
                K.wr_done(dres, tcp, append=(s > 0))
                K.rd_done(bank, tcp)
                if mat != "v" and kind == "a1b":
                    ts_, te = sqstat_st(Pa[:, :], mt, s * S)
                    K.rd_done(bank, ts_)
                return
            if kind == "a0a":
                ts_, te = sqstat_st(Pa[:, :], mt, s * S)
                K.rd_done(bank, ts_)
                tcp = K.emit("act", lambda sc, Pa=Pa, sl=sl: sc.activation(
                    B.qt1[:, sl], Pa[:, :], AF.Copy, bias=0.0, scale=1.0),
                    waits=[mt], inc="act")
                K.rd_done(bank, tcp)
                st["tcp"] = tcp
            elif kind == "a0b":
                seg = seg_rms(st["gi"])
                wsq = K.wr_waits("sqm")
                ts_ = K.emit("act", lambda sc, Pa=Pa: sc.activation(
                    B.sqm[:, :], Pa[:, :], AF.Square, bias=0.0, scale=1.0),
                    waits=[mt] + wsq, inc="act")
                mt2 = K.mm(PS.pstat[:, :], B.ones_col[:, :], B.sqm[:, :],
                           True, True, waits=[ts_] + K.wr_waits("pstat"),
                           inc="pe")
                K.wr_done("sqm", mt2)
                te = K.emit("dve", lambda v, seg=seg: v.tensor_copy(
                    B.row_f[:, seg:seg + S], PS.pstat[:, :]),
                    waits=[mt2], inc="dve")
                K.wr_done("pstat", te)
                t5_ = K.emit("act", lambda sc, seg=seg: sc.activation(
                    B.row_f[:, seg:seg + S], B.row_f[:, seg:seg + S],
                    AF.Ln, bias=B.ccol[0:1, 5:6], scale=B.ccol[0:1, 4:5]),
                    waits=[te], inc="act")
                t6_ = K.emit("act", lambda sc, seg=seg: sc.activation(
                    B.row_f[:, seg:seg + S], B.row_f[:, seg:seg + S],
                    AF.Exp, bias=0.0, scale=B.ccol[0:1, 8:9]),
                    waits=[t5_], inc="act")
                rbseg = (st["gi"] % 2) * S
                t7_ = K.emit("dve", lambda v, seg=seg, rbseg=rbseg:
                             v.tensor_copy(
                    B.row_bf[:, rbseg:rbseg + S], B.row_f[:, seg:seg + S]),
                    waits=[t6_], inc="dve")
                st["t7"] = t7_
                st["rbseg"] = rbseg
            elif kind == "a1a":
                tu = K.emit("dve", lambda v, Pa=Pa, sl=sl: v.tensor_scalar(
                    B.qt1[:, sl], Pa[:, :], B.nrmw[:, 0:1], B.nrmw[:, 1:2],
                    ALU.mult, ALU.add),
                    waits=[tnrm, mt] + (K.wr_waits("qt1") if s == 0 else []),
                    inc="dve")
                K.wr_done("qt1", tu, append=(s > 0))
                K.rd_done(bank, tu)
                ts_, teA = sqstat_st(B.qt1[:, sl], tu, seg_rms(st["gi"]))
                st["tu"] = tu
                st["teA"] = teA

        def stage2(st):
            mat, s, Pa, bank = st["mat"], st["s"], st["Pa"], st["bank"]
            sl, dst, dres = st["sl"], st["dst"], st["dres"]
            if mat == "v" or kind == "a1b":
                return
            if kind == "a0a":
                do_rope(st, B.qt1[:, sl], extra=(st["tcp"],))
            elif kind == "a0b":
                tb_ = K.mm(PS.P3[:, :], B.ones_row[:, :],
                           B.row_bf[0:1, st["rbseg"]:st["rbseg"] + S],
                           True, True, waits=[st["t7"]] + K.wr_waits("P3"),
                           inc="pe")
                tbc = K.emit("act", lambda sc: sc.activation(
                    B.bc2[:, :], PS.P3[:, :], AF.Copy, bias=0.0, scale=1.0),
                    waits=[tb_] + K.wr_waits("bc2"), inc="act")
                K.wr_done("P3", tbc)
                mi = 0 if mat == "q" else 1
                tn = K.emit("dve", lambda v, Pa=Pa, sl=sl, mi=mi:
                            v.scalar_tensor_tensor(
                    B.qt1[:, sl], Pa[:, :], B.nrmw[:, mi:mi + 1],
                    B.bc2[:, :], ALU.mult, ALU.mult),
                    waits=[tnrm, tbc, st["mt"]] +
                    (K.wr_waits("qt1") if s == 0 else []), inc="dve")
                K.wr_done("qt1", tn, append=(s > 0))
                K.wr_done("bc2", tn)
                K.rd_done(bank, tn)
                st["tn"] = tn
            elif kind == "a1a":
                mtB = K.mm(PS.pstat[:, :], B.ones_col[:, :], B.qt1[:, sl],
                           True, True, waits=[st["tu"]] + K.wr_waits("pstat"),
                           inc="pe")
                teB = K.emit("dve", lambda v: v.tensor_copy(
                    B.row_f[:, 0:S], PS.pstat[:, :]), waits=[mtB], inc="dve")
                K.wr_done("pstat", teB)
                segA = seg_rms(st["gi"])
                r1 = K.emit("dve", lambda v: v.tensor_single_scalar(
                    B.row_f[:, 0:S], B.row_f[:, 0:S], 1.0 / DH, ALU.mult),
                    waits=[teB], inc="dve")
                r2 = K.emit("dve", lambda v: v.tensor_tensor(
                    B.row_f[:, S:2 * S], B.row_f[:, 0:S], B.row_f[:, 0:S],
                    ALU.mult), waits=[r1], inc="dve")
                r3 = K.emit("dve", lambda v, segA=segA: v.scalar_tensor_tensor(
                    B.row_f[:, S:2 * S], B.row_f[:, segA:segA + S], 1.0 / DH,
                    B.row_f[:, S:2 * S], ALU.mult, ALU.subtract),
                    waits=[r2, st["teA"]], inc="dve")
                r4 = K.emit("act", lambda sc: sc.activation(
                    B.row_f[:, S:2 * S], B.row_f[:, S:2 * S], AF.Ln,
                    bias=B.ccol[0:1, 3:4], scale=1.0), waits=[r3], inc="act")
                r5 = K.emit("act", lambda sc: sc.activation(
                    B.row_f[:, S:2 * S], B.row_f[:, S:2 * S], AF.Exp,
                    bias=0.0, scale=B.ccol[0:1, 8:9]), waits=[r4], inc="act")
                qs = SCL if mat == "q" else 1.0
                r6 = K.emit("dve", lambda v, qs=qs: v.tensor_single_scalar(
                    B.row_bf[:, 0:S], B.row_f[:, S:2 * S], qs, ALU.mult),
                    waits=[r5], inc="dve")
                r7 = K.emit("dve", lambda v: v.tensor_tensor(
                    B.row_f[:, 0:S], B.row_f[:, 0:S], B.row_f[:, S:2 * S],
                    ALU.mult), waits=[r5, r2], inc="dve")
                r8 = K.emit("dve", lambda v, qs=-qs: v.tensor_single_scalar(
                    B.row_bf[:, S:2 * S], B.row_f[:, 0:S], qs, ALU.mult),
                    waits=[r7], inc="dve")
                tb1 = K.mm(PS.P3[:, :], B.ones_row[:, :], B.row_bf[0:1, 0:S],
                           True, True, waits=[r6] + K.wr_waits("P3"),
                           inc="pe")
                tb2 = K.mm(PS.P4[:, :], B.ones_row[:, :],
                           B.row_bf[0:1, S:2 * S], True, True,
                           waits=[r8] + K.wr_waits("P4"), inc="pe")
                tbc = K.emit("act", lambda sc: sc.activation(
                    B.bc2[:, :], PS.P3[:, :], AF.Copy, bias=0.0, scale=1.0),
                    waits=[tb1] + K.wr_waits("bc2"), inc="act")
                K.wr_done("P3", tbc)
                w1 = K.emit("dve", lambda v, sl=sl, dst=dst: v.tensor_tensor(
                    dst[:, sl], B.qt1[:, sl], B.bc2[:, :], ALU.mult),
                    waits=[tbc, st["tu"]] +
                    (K.wr_waits(dres) if s == 0 else []), inc="dve")
                K.wr_done("bc2", w1)
                w2 = K.emit("dve", lambda v, sl=sl, dst=dst: v.tensor_tensor(
                    dst[:, sl], dst[:, sl], PS.P4[:, :], ALU.add),
                    waits=[tb2, w1], inc="dve")
                K.wr_done("P4", w2)
                K.wr_done(dres, w2, append=(s > 0))

        def stage3(st):
            mat, sl = st["mat"], st["sl"]
            if mat == "v" or kind in ("a1b", "a1a", "a0a"):
                return
            ts2, te2 = sqstat_st(B.qt1[:, sl], st["tn"], st["s"] * S)
            do_rope(st, B.qt1[:, sl], extra=(st["tn"],))

        bound_stage = 3 if kind == "a0b" else 1

        def emit_bound_k():
            tm1 = K.emit("dve", lambda v: v.reduce_max(
                B.col_f[0:1, 28:29], B.row_f[:, 0:4 * S],
                mybir.AxisListType.X), waits=stat_toks[-4:], inc="dve")
            tm2 = K.emit("dve", lambda v: v.tensor_single_scalar(
                B.col_f[0:1, 29:30], B.col_f[0:1, 28:29], 1.0 / DH, ALU.mult),
                waits=[tm1], inc="dve")
            tm3 = K.emit("dve", lambda v: v.tensor_copy(
                B.row_bf[0:1, 2 * S:2 * S + 1], B.col_f[0:1, 29:30]),
                waits=[tm2], inc="dve")
            tb_ = K.mm(PS.P3[:, 0:1], B.ones_row[:, :],
                       B.row_bf[0:1, 2 * S:2 * S + 1], True, True,
                       waits=[tm3] + K.wr_waits("P3"), inc="pe")
            tk2 = K.emit("act", lambda sc: sc.activation(
                B.col_f[:, 30:31], PS.P3[:, 0:1], AF.Copy, bias=0.0,
                scale=1.0), waits=[tb_], inc="act")
            K.wr_done("P3", tk2)
            return tk2

        def emit_bound_q(tk2):
            wtr = K.wr_waits("P3")
            ttr = None
            for i in range(NT_TILES):
                ttr = K.mm(PS.P3[:, i:i + 1],
                           B.row_f[0:1, i * 128:(i + 1) * 128],
                           B.identf[0:1, 0:1], True, True,
                           waits=(wtr + [stat_toks[-1], K.consts_tok])
                           if i == 0 else [],
                           inc="pe" if i == NT_TILES - 1 else None)
            tcv = K.emit("act", lambda sc: sc.activation(
                B.col_f[:, 0:NT_TILES], PS.P3[:, 0:NT_TILES], AF.Copy,
                bias=0.0, scale=1.0), waits=[ttr], inc="act")
            K.wr_done("P3", tcv)
            tnb1 = K.emit("act", lambda sc: sc.activation(
                B.col_f[:, 0:NT_TILES], B.col_f[:, 0:NT_TILES], AF.Sqrt,
                bias=0.0, scale=B.col_f[:, 30:31]),
                waits=[tcv, tk2], inc="act")
            return K.emit("dve", lambda v: v.tensor_single_scalar(
                B.col_f[:, 0:NT_TILES], B.col_f[:, 0:NT_TILES], -1.0,
                ALU.mult), waits=[tnb1], inc="dve")

        tk2 = None
        tnb = None
        pend = []
        for gi in range(len(seq) + 3):
            if gi < len(seq):
                pend.append(mm_group(gi))
            if gi >= 1 and gi - 1 < len(seq):
                stage1(pend[gi - 1])
                if use_bound and bound_stage == 1:
                    if seq[gi - 1] == (0, NB - 1):
                        tk2 = emit_bound_k()
                    if seq[gi - 1] == (1, NB - 1):
                        tnb = emit_bound_q(tk2)
            if gi >= 2 and gi - 2 < len(seq):
                stage2(pend[gi - 2])
            if gi >= 3 and gi - 3 < len(seq):
                stage3(pend[gi - 3])
                if use_bound and bound_stage == 3:
                    if seq[gi - 3] == (0, NB - 1):
                        tk2 = emit_bound_k()
                    if seq[gi - 3] == (1, NB - 1):
                        tnb = emit_bound_q(tk2)

        vtk = None
        for g4 in range(4):
            ptr = PS.ptrA if g4 % 2 == 0 else PS.ptrB
            pres = "ptrA" if g4 % 2 == 0 else "ptrB"
            wp = K.wr_waits(pres)
            tt_ = None
            for k4 in range(4):
                tt = g4 * 4 + k4
                tt_ = K.tr(ptr[:, k4 * 128:(k4 + 1) * 128],
                           B.vt[:, tt * 128:(tt + 1) * 128], B.ident[:, :],
                           waits=(wp + [K.consts_tok] +
                                  K.rd_waits("vt") if k4 == 0 else []),
                           inc="pe" if k4 == 3 else None)
            vtk = K.emit("dve", lambda v, g4=g4, ptr=ptr: v.tensor_copy(
                B.vsb[:, g4 * S:(g4 + 1) * S], ptr[:, :]),
                waits=[tt_] + (K.wr_waits("vsb") if g4 == 0 else []),
                inc="dve")
            K.wr_done(pres, vtk)
            K.wr_done("vsb", vtk, append=(g4 > 0))

        if getattr(K, "proj_only", False):
            continue
        def emit_ptpv(st):
            i, jlo, nj, tp = st
            ptk = []
            for g4 in range((nj + 3) // 4):
                ptr = PS.ptrA if g4 % 2 == 0 else PS.ptrB
                pres = "ptrA" if g4 % 2 == 0 else "ptrB"
                wp = K.wr_waits(pres)
                n4 = min(4, nj - g4 * 4)
                tt_ = None
                for k4 in range(n4):
                    jj = g4 * 4 + k4
                    tt_ = K.tr(ptr[:, k4 * 128:(k4 + 1) * 128],
                               B.P[:, jj * 128:(jj + 1) * 128], B.ident[:, :],
                               waits=[tp] + (wp if k4 == 0 else []),
                               inc="pe" if k4 == n4 - 1 else None)
                tev = K.emit("dve", lambda v, g4=g4, n4=n4, ptr=ptr:
                             v.tensor_copy(
                    B.PT[:, g4 * S:g4 * S + n4 * 128], ptr[:, 0:n4 * 128]),
                    waits=[tt_] + (K.wr_waits("slot1PT") if g4 == 0 else []),
                    inc="dve")
                K.wr_done(pres, tev)
                K.wr_done("slot1PT", tev, append=(g4 > 0))
                ptk.append(tev)
            w4 = K.wr_waits("P4")
            mo = None
            for jj in range(nj):
                j = jlo + jj
                mo = K.mm(PS.P4[:, 0:128], B.vsb[:, j * 128:(j + 1) * 128],
                          B.PT[:, jj * 128:(jj + 1) * 128],
                          start=(jj == 0), stop=(jj == nj - 1),
                          waits=[ptk[jj // 4], vtk] + (w4 if jj == 0 else []),
                          inc="pe" if jj == nj - 1 else None)
            K.rd_done("slot1PT", mo)
            K.rd_done("vsb", mo)
            to = K.emit("act", lambda sc, i=i: sc.activation(
                B.osb[:, i * 128:(i + 1) * 128], PS.P4[:, 0:128], AF.Copy,
                bias=0.0, scale=1.0),
                waits=[mo] + (K.wr_waits("osb") if i == 0 else []), inc="act")
            K.wr_done("P4", to)
            K.wr_done("osb", to, append=(i > 0))

        prev_tile = None
        for i in range(NT_TILES):
            jlo = max(0, i - WTILES) if windowed else 0
            nj = i - jlo + 1
            ncols = nj * 128
            nsp = (ncols + S - 1) // S
            par2 = i % 2
            banks = ["P0", "P1"] if par2 == 0 else ["P2", "P3"]
            dp0 = 16 if par2 == 0 else 22
            dnc, drc = (20, 21) if par2 == 0 else (26, 27)
            dloc = (i - jlo) * 128
            sd, off = dloc // S, dloc % S
            etoks = []
            wE = K.wr_waits("slot1E") if kind != "a1a" else []
            wP = K.wr_waits("slot1P")
            first_grp_toks = None
            for sp_i in range(nsp):
                lc = sp_i * S
                w = min(S, ncols - lc)
                bname = banks[sp_i % 2]
                Pa = getattr(PS, bname)
                wb = K.wr_waits(bname)
                extra = []
                if sp_i >= 2:
                    extra = [etoks[sp_i - 2]]
                mt = K.mm(Pa[:, 0:w], B.qh[:, i * 128:(i + 1) * 128],
                          B.kh[:, jlo * 128 + lc: jlo * 128 + lc + w],
                          True, True,
                          waits=wb + extra + (K.rd_waits("qh") +
                                              K.rd_waits("kh")
                                              if sp_i == 0 else []), inc="pe")
                mtk = mt
                if sp_i == sd:
                    if kind == "a1a":
                        mtk = K.emit("dve", lambda v, Pa=Pa, off=off:
                                     v.tensor_tensor(
                            Pa[:, off:off + 128], Pa[:, off:off + 128],
                            B.m_tril01[:, :], ALU.mult),
                            waits=[mt, K.consts_tok], inc="dve")
                    else:
                        mtk = K.emit("dve", lambda v, Pa=Pa, off=off:
                                     v.tensor_tensor(
                            Pa[:, off:off + 128], Pa[:, off:off + 128],
                            B.m_tril_neg[:, :], ALU.add),
                            waits=[mt, K.consts_tok], inc="dve")
                if sp_i == 0 and windowed and i >= WTILES:
                    mtk = K.emit("dve", lambda v, Pa=Pa: v.tensor_tensor(
                        Pa[:, 0:128], Pa[:, 0:128],
                        B.m_triu_neg[:, :], ALU.add),
                        waits=[mt, mtk, K.consts_tok], inc="dve")
                if kind == "a1a":
                    te = K.emit("act", lambda sc, Pa=Pa, lc=lc, w=w, sp_i=sp_i,
                                dp0=dp0: sc.activation(
                        B.P[:, lc:lc + w], Pa[:, 0:w], AF.Square,
                        bias=0.0, scale=1.0,
                        accum_out=B.col_f[:, dp0 + sp_i:dp0 + sp_i + 1]),
                        waits=[mtk] + wP, inc="act")
                else:
                    te = K.emit("act", lambda sc, Pa=Pa, lc=lc, w=w, sp_i=sp_i,
                                dp0=dp0, i=i: sc.activation(
                        B.E[:, lc:lc + w], Pa[:, 0:w], AF.Exp,
                        bias=B.col_f[:, i:i + 1], scale=B.ccol[:, 0:1],
                        accum_out=B.col_f[:, dp0 + sp_i:dp0 + sp_i + 1]),
                        waits=[mtk, tnb, K.consts_tok] + wE, inc="act")
                etoks.append(te)
                K.wr_done(bname, te)
                if sp_i == 1 and prev_tile is not None:
                    emit_ptpv(prev_tile)
                    prev_tile = None
            K.rd_done("qh", mt)
            K.rd_done("kh", mt)
            if prev_tile is not None:
                emit_ptpv(prev_tile)
                prev_tile = None
            td1 = K.emit("dve", lambda v, nsp=nsp, dp0=dp0, dnc=dnc:
                         v.reduce_sum(
                B.col_f[:, dnc:dnc + 1], B.col_f[:, dp0:dp0 + nsp],
                mybir.AxisListType.X), waits=etoks, inc="dve")
            ecol = 9 if kind == "a1a" else 10
            td2 = K.emit("act", lambda sc, dnc=dnc, drc=drc, ecol=ecol:
                         sc.activation(
                B.col_f[:, drc:drc + 1], B.col_f[:, dnc:dnc + 1], AF.Ln,
                bias=B.ccol[:, ecol:ecol + 1], scale=1.0),
                waits=[td1], inc="act")
            td3 = K.emit("act", lambda sc, drc=drc: sc.activation(
                B.col_f[:, drc:drc + 1], B.col_f[:, drc:drc + 1], AF.Exp,
                bias=0.0, scale=B.ccol[:, 11:12]),
                waits=[td2], inc="act")
            if kind == "a1a":
                tp = K.emit("dve", lambda v, nc_=ncols, drc=drc:
                            v.tensor_single_scalar(
                    B.P[:, 0:nc_], B.P[:, 0:nc_], B.col_f[:, drc:drc + 1],
                    ALU.mult), waits=[td3] + etoks + wP, inc="dve")
            else:
                tp = K.emit("dve", lambda v, nc_=ncols, drc=drc:
                            v.tensor_single_scalar(
                    B.P[:, 0:nc_], B.E[:, 0:nc_], B.col_f[:, drc:drc + 1],
                    ALU.mult), waits=[td3] + etoks + wP, inc="dve")
                for te in etoks:
                    K.wr_done("slot1E", tp, append=True)
            K.wr_done("slot1P", tp)
            prev_tile = (i, jlo, nj, tp)
        emit_ptpv(prev_tile)
        prev_tile = None

        if getattr(K, "dbg", False) and h == 0:
            K.tap("qh", B.qh[:, :], K.rd_waits("qh"))
            K.tap("kh", B.kh[:, :], K.rd_waits("kh"))
            K.tap("vsb", B.vsb[:, :], K.rd_waits("vsb"))
            K.tap("rowf", B.row_f[:, :], [self_last_dve(K)])
            K.tap("colf", B.col_f[:, :], [self_last_dve(K), self_last_act(K)])
            K.tap("E", B.E[:, :], [self_last_act(K)])
            K.tap("P", B.P[:, :], [self_last_dve(K)])
            K.tap("osb", B.osb[:, :], K.rd_waits("osb"))
            K.tap("xln", B.x_lnT[:, :], K.rd_waits("big0"))
            K.tap("cful", B.cful[:, :], [self_last_act(K)])
            K.tap("sful", B.sful[:, :], [self_last_dve(K)])
        tso = K.dma("sp", "xd", ost[h], B.osb[:, :],
                    waits=K.rd_waits("osb"))
        K.rd_done("osb", tso)
        otok_prev[h] = tso

    if getattr(K, "proj_only", False):
        tst = K.dma("sp", "xd", xdst[3], B.x_blk[:, :],
                    waits=[self_last_dve(K), self_last_act(K)])
        K.rd_done("x_blk", tst)
        return tst

    for ts in range(NB):
        tosl = K.dma("sp", "xd", B.osl[:, :].rearrange("p (h u) -> p h u", h=H),
                     ost[:, :, ts * S:(ts + 1) * S].rearrange("h p u -> p h u"),
                     waits=otok_prev + K.wr_waits("slot0"))
        tox = K.dma("sp", "xd", B.x_blk[:, :], xsrc[ts],
                    waits=K.wr_waits("x_blk"))
        rtoks = []
        for dc in range(DC):
            wslot_o = B.wring[:, (dc % 2) * DC * 128:(dc % 2 + 1) * DC * 128]
            res_o = f"wro{dc % 2}"
            tww = K.dma("pool", "wd", wslot_o, Wap["wo"][dc],
                        waits=K.wr_waits(res_o))
            Pa = PS.P0 if dc % 2 == 0 else PS.P1
            bank = "P0" if dc % 2 == 0 else "P1"
            wb = K.wr_waits(bank)
            mt = None
            for c in range(DC):
                mt = K.mm(Pa[:, :], wslot_o[:, c * 128:(c + 1) * 128],
                          B.osl[:, c * S:(c + 1) * S],
                          start=(c == 0), stop=(c == DC - 1),
                          waits=[tww, tosl] + (wb if c == 0 else []),
                          inc="pe" if c == DC - 1 else None)
            K.rd_done(res_o, mt)
            K.rd_done("slot0", mt)
            tr_ = K.emit("dve", lambda v, dc=dc, Pa=Pa: v.tensor_tensor(
                B.x_blk[:, dc * S:(dc + 1) * S],
                B.x_blk[:, dc * S:(dc + 1) * S], Pa[:, :], ALU.add),
                waits=[mt, tox], inc="dve")
            K.wr_done(bank, tr_)
            rtoks.append(tr_)
        tst = K.dma("sp", "xd", xdst[ts], B.x_blk[:, :], waits=rtoks)
        K.rd_done("x_blk", tst)
        for r in rtoks:
            K.rd_done("x_blk", r)
    return tst


NT_TILES = 16
WTILES = 8


def self_last_dve(K):
    return ("dve", K.cnt.get("dve", 0))


def self_last_act(K):
    return ("act", K.cnt.get("act", 0))


def emit_ffn(K, B, PS, kind, Wap, xsrc, xdst, pfx=""):
    is0 = (kind == "ffn0")
    pc = getattr(K, "pc", {})
    tsg = K.dma("sp", "xd", B.sgc[:, :], Wap["sg"], waits=K.wr_waits("sgc"))
    tsu = None
    if is0:
        tsu = K.dma("sp", "xd", B.suc[:, :], Wap["su"], waits=K.wr_waits("suc"))
    tsd = K.dma("sp", "xd", B.sdc[:, :], Wap["sd"], waits=K.wr_waits("sdc"))

    tst = None
    for b in range(NB):
        tok_x = K.dma("sp", "xd", B.x_blk[:, :], xsrc[b],
                      waits=K.wr_waits("x_blk"))
        dst = [B.x_lnB[:, c * S:(c + 1) * S] for c in range(DC)]
        ln_toks = emit_ln_block(K, B, PS, tok_x, dst, "slot0")
        for tk_ in ln_toks:
            K.wr_done("slot0", tk_, append=True)

        wfuse = K.wr_waits("fuse")
        ftoks = []
        for ic in range(IC):
            par = ic % 2
            gsl = B.wgu[:, (0 + par) * DC * 128:(1 + par) * DC * 128]
            usl = B.wgu[:, (2 + par) * DC * 128:(3 + par) * DC * 128]
            gres, ures = f"wg{par}", f"wu{par}"
            gsrc = Wap["g_blob"][ic]
            gpc = pc.get((pfx, "g", ic))
            if gpc is not None:
                gsrc = Wap["g_bf"][ic]
            tgw = K.dma("pool", "wd", gsl, gsrc,
                        waits=K.wr_waits(gres) + [gpc])
            tuw = None
            if is0:
                tuw = K.dma("pool", "wd", usl, Wap["u_blob"][ic],
                            waits=K.wr_waits(ures))
            Pg = PS.P0 if par == 0 else PS.P2
            gbank = "P0" if par == 0 else "P2"
            Pu = PS.P1 if par == 0 else PS.P3
            ubank = "P1" if par == 0 else "P3"
            wgb = K.wr_waits(gbank)
            mg = None
            for c in range(DC):
                mg = K.mm(Pg[:, :], gsl[:, c * 128:(c + 1) * 128], dst[c],
                          start=(c == 0), stop=(c == DC - 1),
                          waits=[tgw, ln_toks[c]] + (wgb if c == 0 else []),
                          inc="pe" if c == DC - 1 else None)
            K.rd_done(gres, mg)
            K.rd_done("slot0", mg)
            if is0:
                wub = K.wr_waits(ubank)
                mu = None
                for c in range(DC):
                    mu = K.mm(Pu[:, :], usl[:, c * 128:(c + 1) * 128], dst[c],
                              start=(c == 0), stop=(c == DC - 1),
                              waits=[tuw, ln_toks[c]] + (wub if c == 0 else []),
                              inc="pe" if c == DC - 1 else None)
                K.rd_done(ures, mu)
                K.rd_done("slot0", mu)
                stg = B.sqm if par == 0 else B.bc2
                sres = "sqm" if par == 0 else "bc2"
                tsl = K.emit("act", lambda sc, Pg=Pg, stg=stg, ic=ic:
                             sc.activation(
                    stg[:, :], Pg[:, :], AF.Silu, bias=0.0,
                    scale=B.sgc[:, ic:ic + 1]),
                    waits=[mg, tsg] + K.wr_waits(sres), inc="act")
                K.rd_done(gbank, tsl)
                tf = K.emit("dve", lambda v, Pu=Pu, stg=stg, ic=ic:
                            v.scalar_tensor_tensor(
                    B.fuse[:, ic * S:(ic + 1) * S], Pu[:, :],
                    B.suc[:, ic:ic + 1], stg[:, :], ALU.mult, ALU.mult),
                    waits=[mu, tsu, tsl] + (wfuse if ic == 0 else []),
                    inc="dve")
                K.rd_done(ubank, tf)
                K.wr_done(sres, tf)
            else:
                tf = K.emit("act", lambda sc, Pg=Pg, ic=ic: sc.activation(
                    B.fuse[:, ic * S:(ic + 1) * S], Pg[:, :], AF.Gelu,
                    bias=B.sgc[:, ic:ic + 1], scale=1.0),
                    waits=[mg, tsg] + (wfuse if ic == 0 else []), inc="act")
                K.rd_done(gbank, tf)
            ftoks.append(tf)
            K.wr_done("fuse", tf, append=(ic > 0))

        rtoks = []
        for oc in range(DC):
            par = oc % 2
            dsl = B.wdn[:, par * IC * 128:(par + 1) * IC * 128]
            dres = f"wd{par}"
            dsrc = Wap["d_blob"][oc]
            dpc = pc.get((pfx, "d", oc))
            if dpc is not None:
                dsrc = Wap["d_bf"][oc]
            tdw = K.dma("pool", "wd", dsl, dsrc,
                        waits=K.wr_waits(dres) + [dpc])
            Pd = PS.P0 if par == 0 else PS.P1
            dbank = "P0" if par == 0 else "P1"
            wdb = K.wr_waits(dbank)
            if getattr(K, "dbg", False) and oc == DC - 1 and b == 3 and                     not getattr(K, "_dtapped", 0):
                K._dtapped = 1
                K.tap("fuse_end", B.fuse[:, 48 * S:64 * S], [tdw] + wdb)
                K.tap("wd15", dsl, [tdw])
            md = None
            for c in range(IC):
                md = K.mm(Pd[:, :], dsl[:, c * 128:(c + 1) * 128],
                          B.fuse[:, c * S:(c + 1) * S],
                          start=(c == 0), stop=(c == IC - 1),
                          waits=[tdw, ftoks[c]] + (wdb if c == 0 else []),
                          inc="pe" if c == IC - 1 else None)
            K.rd_done(dres, md)
            K.rd_done("fuse", md)
            if is0:
                tr_ = K.emit("dve", lambda v, oc=oc, Pd=Pd:
                             v.scalar_tensor_tensor(
                    B.x_blk[:, oc * S:(oc + 1) * S], Pd[:, :],
                    B.sdc[:, oc:oc + 1], B.x_blk[:, oc * S:(oc + 1) * S],
                    ALU.mult, ALU.add), waits=[md, tsd, tok_x], inc="dve")
            else:
                tr_ = K.emit("dve", lambda v, oc=oc, Pd=Pd:
                             v.scalar_tensor_tensor(
                    B.x_blk[:, oc * S:(oc + 1) * S], Pd[:, :],
                    B.sdc[:, oc:oc + 1], B.x_blk[:, oc * S:(oc + 1) * S],
                    ALU.add, ALU.add), waits=[md, tsd, tok_x], inc="dve")
            K.wr_done(dbank, tr_)
            rtoks.append(tr_)
        if getattr(K, "dbg", False) and b == 3 and not getattr(K, "_ftapped", 0):
            K._ftapped = 1
            K.tap("fxblk3", B.x_blk[:, :], rtoks)
        tst = K.dma("sp", "xd", xdst[b], B.x_blk[:, :], waits=rtoks)
        K.rd_done("x_blk", tst)
        for r in rtoks:
            K.rd_done("x_blk", r)
    return tst


SUBLAYERS = [
    ("a0a", "L0aa"), ("ffn0", "L0f"), ("a0b", "L0ab"), ("ffn0", "L0f"),
    ("a1a", "L1aa"), ("ffn1", "L1f"), ("a1b", "L1ab"), ("ffn1", "L1f"),
]


def build_program(sub_idx_list):
    K = Prog()
    nc = K.nc
    B = alloc_sbuf(nc)
    PS = alloc_psum(nc)
    need_rope = any(SUBLAYERS[i][0] in ("a0a", "a0b") for i in sub_idx_list)
    need_attn = any(SUBLAYERS[i][0].startswith("a") for i in sub_idx_list)

    xin = nc.dram_tensor("xin", [NB, 128, DC * S], F32, kind="ExternalInput").ap()
    xout = nc.dram_tensor("xout", [NB, 128, DC * S], F32,
                          kind="ExternalOutput").ap()
    nsub = len(sub_idx_list)
    xst = xst2 = None
    if nsub > 1:
        xst = nc.dram_tensor("xst", [NB, 128, DC * S], F32, kind="Internal").ap()
    if nsub > 2:
        xst2 = nc.dram_tensor("xst2", [NB, 128, DC * S], F32,
                              kind="Internal").ap()
    ost = None
    if need_attn:
        ost = nc.dram_tensor("ost", [H, 128, T], BF16, kind="Internal").ap()

    Waps = {}
    for i in sub_idx_list:
        kind, pfx = SUBLAYERS[i]
        if pfx in Waps:
            continue
        W = {}
        if kind.startswith("a"):
            for nm in ("wq", "wk", "wv", "wo"):
                W[nm] = nc.dram_tensor(f"{pfx}_{nm}", [H, 128, DC * 128], F32,
                                       kind="ExternalInput").ap()
            if kind in ("a0b", "a1a"):
                W["nrm"] = nc.dram_tensor(f"{pfx}_nrm", [128, 2], F32,
                                          kind="ExternalInput").ap()
            W["ost"] = ost
        elif kind == "ffn0":
            W["g_blob"] = nc.dram_tensor(f"{pfx}_g", [IC, 128, DC * 128], I8,
                                         kind="ExternalInput").ap()
            W["u_blob"] = nc.dram_tensor(f"{pfx}_u", [IC, 128, DC * 128], I8,
                                         kind="ExternalInput").ap()
            W["d_blob"] = nc.dram_tensor(f"{pfx}_d", [DC, 128, IC * 128], I8,
                                         kind="ExternalInput").ap()
            W["sg"] = nc.dram_tensor(f"{pfx}_sg", [128, IC], F32,
                                     kind="ExternalInput").ap()
            W["su"] = nc.dram_tensor(f"{pfx}_su", [128, IC], F32,
                                     kind="ExternalInput").ap()
            W["sd"] = nc.dram_tensor(f"{pfx}_sd", [128, DC], F32,
                                     kind="ExternalInput").ap()
        else:
            W["g_blob"] = nc.dram_tensor(f"{pfx}_g", [IC, 128, DC * 128], F32,
                                         kind="ExternalInput").ap()
            W["d_blob"] = nc.dram_tensor(f"{pfx}_d", [DC, 128, IC * 128], F32,
                                         kind="ExternalInput").ap()
            W["g_bf"] = nc.dram_tensor(f"{pfx}_gbf", [IC, 128, DC * 128],
                                       BF16, kind="Internal").ap()
            W["d_bf"] = nc.dram_tensor(f"{pfx}_dbf", [DC, 128, IC * 128],
                                       BF16, kind="Internal").ap()
            W["sg"] = nc.dram_tensor(f"{pfx}_sg", [128, IC], F32,
                                     kind="ExternalInput").ap()
            W["sd"] = nc.dram_tensor(f"{pfx}_sd", [128, DC], F32,
                                     kind="ExternalInput").ap()
        Waps[pfx] = W

    emit_prelude(K, B, PS, need_rope)
    K.bg = []
    K.pc = {}
    for pfx, W in Waps.items():
        if "g_bf" not in W:
            continue

        def mk(key, dst, srcb, j):
            def run(key=key, dst=dst, srcb=srcb, j=j):
                K.pc[key] = K.dma("pool", "wd", dst[j], srcb[j])
            return run
        for ic in range(IC):
            K.bg.append(mk((pfx, "g", ic), W["g_bf"], W["g_blob"], ic))
        for oc in range(DC):
            K.bg.append(mk((pfx, "d", oc), W["d_bf"], W["d_blob"], oc))
    last_tok = None
    for j, i in enumerate(sub_idx_list):
        kind, pfx = SUBLAYERS[i]
        src = xin if j == 0 else (xst if j % 2 == 1 else xst2)
        dst = xout if j == nsub - 1 else (xst if j % 2 == 0 else xst2)
        if kind.startswith("a"):
            last_tok = emit_attention(K, B, PS, kind, Waps[pfx], src, dst)
        else:
            last_tok = emit_ffn(K, B, PS, kind, Waps[pfx], src, dst, pfx)
    nc = K.finalize([last_tok])
    return nc


def proj_blob(W):
    din, dout = W.shape
    ndc, noc = din // 128, dout // 128
    return np.ascontiguousarray(
        W.reshape(ndc, 128, noc, 128).transpose(2, 1, 0, 3)
        .reshape(noc, 128, ndc * 128))


def x_to_blocks(xc):
    xT = xc.T
    return np.ascontiguousarray(
        xT.reshape(DC, 128, NB, S).transpose(2, 1, 0, 3)
        .reshape(NB, 128, DC * S))


def x_from_blocks(xb):
    xT = xb.reshape(NB, 128, DC, S).transpose(2, 1, 0, 3).reshape(D, T)
    return np.ascontiguousarray(xT.T)


def colmaj(v):
    return np.ascontiguousarray(v.reshape(-1, 128).T)


def prep_weights(inputs):
    P = {}
    P["L0aa"] = {
        "wq": proj_blob(inputs["L0_attn_a_wq"]),
        "wk": proj_blob(inputs["L0_attn_a_wk"]),
        "wv": proj_blob(inputs["L0_attn_a_wv"]),
        "wo": proj_blob(inputs["L0_attn_a_wo"]),
    }
    P["L0ab"] = {
        "wq": proj_blob(np.ascontiguousarray(inputs["L0_attn_b_Wq"].T)),
        "wk": proj_blob(np.ascontiguousarray(inputs["L0_attn_b_Wk"].T)),
        "wv": proj_blob(np.ascontiguousarray(inputs["L0_attn_b_Wv"].T)),
        "wo": proj_blob(np.ascontiguousarray(inputs["L0_attn_b_Wo"].T)),
        "nrm": np.ascontiguousarray(np.stack(
            [inputs["L0_attn_b_q_norm_w"], inputs["L0_attn_b_k_norm_w"]],
            axis=1)),
    }
    P["L0f"] = {
        "g_blob": proj_blob(inputs["L0_ffn_w_gate"]),
        "u_blob": proj_blob(inputs["L0_ffn_w_up"]),
        "d_blob": proj_blob(inputs["L0_ffn_w_down"]),
        "sg": colmaj(inputs["L0_ffn_s_gate"]),
        "su": colmaj(inputs["L0_ffn_s_up"]),
        "sd": colmaj(inputs["L0_ffn_s_down"]),
    }
    P["L1aa"] = {
        "wq": proj_blob(np.ascontiguousarray(inputs["L1_attn_a_Wq"].T)),
        "wk": proj_blob(np.ascontiguousarray(inputs["L1_attn_a_Wk"].T)),
        "wv": proj_blob(np.ascontiguousarray(inputs["L1_attn_a_Wv"].T)),
        "wo": proj_blob(np.ascontiguousarray(inputs["L1_attn_a_Wo"].T)),
        "nrm": np.ascontiguousarray(np.stack(
            [inputs["L1_attn_a_gamma"], inputs["L1_attn_a_beta"]], axis=1)),
    }
    P["L1f"] = {
        "g_blob": proj_blob(np.ascontiguousarray(inputs["L1_ffn_W0"].T)),
        "d_blob": proj_blob(np.ascontiguousarray(inputs["L1_ffn_W1"].T)),
        "sg": colmaj(inputs["L1_ffn_bias0"]),
        "sd": colmaj(inputs["L1_ffn_bias1"]),
    }
    P["L1ab"] = {
        "wq": proj_blob(np.ascontiguousarray(
            inputs["L1_attn_b_W_Q"].transpose(2, 0, 1).reshape(D, D))),
        "wk": proj_blob(np.ascontiguousarray(
            inputs["L1_attn_b_W_K"].transpose(2, 0, 1).reshape(D, D))),
        "wv": proj_blob(np.ascontiguousarray(
            inputs["L1_attn_b_W_V"].transpose(2, 0, 1).reshape(D, D))),
        "wo": proj_blob(np.ascontiguousarray(
            inputs["L1_attn_b_W_O"].transpose(0, 2, 1).reshape(D, D))),
    }
    return P


def in_map_for(sub_idx_list, P, xblk):
    m = {"xin": xblk}
    seen = set()
    for i in sub_idx_list:
        kind, pfx = SUBLAYERS[i]
        if pfx in seen:
            continue
        seen.add(pfx)
        W = P[pfx]
        if kind.startswith("a"):
            m[f"{pfx}_wq"] = W["wq"]
            m[f"{pfx}_wk"] = W["wk"]
            m[f"{pfx}_wv"] = W["wv"]
            m[f"{pfx}_wo"] = W["wo"]
            if "nrm" in W:
                m[f"{pfx}_nrm"] = W["nrm"]
        else:
            m[f"{pfx}_g"] = W["g_blob"]
            m[f"{pfx}_d"] = W["d_blob"]
            m[f"{pfx}_sg"] = W["sg"]
            m[f"{pfx}_sd"] = W["sd"]
            if "u_blob" in W:
                m[f"{pfx}_u"] = W["u_blob"]
                m[f"{pfx}_su"] = W["su"]
    return m


def run_groups(inputs, groups):
    P = prep_weights(inputs)
    x = np.asarray(inputs["L0_attn_a_hidden_states"], np.float32)
    ncores = x.shape[0]
    xblk = [x_to_blocks(x[c]) for c in range(ncores)]
    for grp in groups:
        nc = build_program(grp)
        in_maps = [in_map_for(grp, P, xblk[c]) for c in range(ncores)]
        res = run_bass_kernel_spmd(nc, in_maps, core_ids=list(range(ncores)))
        xblk = [np.asarray(res.results[c]["xout"], np.float32)
                for c in range(ncores)]
    return np.stack([x_from_blocks(xblk[c]) for c in range(ncores)])


def kernel(**inputs):
    out = run_groups(inputs, [list(range(8))])
    return out.astype(np.float32)
